# version 16
# baseline (speedup 1.0000x reference)
import numpy as np
import ml_dtypes
from contextlib import ExitStack

import concourse.bass as bass
import concourse.mybir as mybir
from concourse.alu_op_type import AluOpType as OP
from concourse.bass_utils import run_bass_kernel_spmd

F32 = mybir.dt.float32
BF16 = mybir.dt.bfloat16
I32 = mybir.dt.int32
AF = mybir.ActivationFunctionType
AX = mybir.AxisListType.X

S = 2048
D = 2048
NT = 16
NH = 16
HEADS = 16
F0 = 5632
NFT = 44
NFC = 11
SCALE = 1.0 / float(np.sqrt(np.float32(128.0)))
NEGBIG = -1.0e30
BF = ml_dtypes.bfloat16
ENGS = ("tensor", "vector", "scalar", "sync", "gpsimd")
DBG_SKIP = set()


def pack_lhsT(wT):
    H, O = wT.shape
    return np.ascontiguousarray(
        wT.reshape(H // 128, 128, O // 128, 128).transpose(2, 1, 0, 3).reshape(O // 128, 128, H))


def pack_rhs(wT):
    H, F = wT.shape
    return np.ascontiguousarray(wT.reshape(H // 128, 128, F))


def make_masks():
    out = np.zeros((128, 4 * 512), dtype=BF)
    r = np.arange(128)[:, None]
    c = np.arange(512)[None, :]
    for oi in range(4):
        out[:, oi * 512:(oi + 1) * 512] = np.where(c <= oi * 128 + r, 0.0, NEGBIG).astype(BF)
    return out


class _MockIns:
    def then_inc(self, *a, **kw):
        return self


class _MockEng:
    def __getattr__(self, name):
        def f(*a, **kw):
            return _MockIns()
        return f


AUTOSEM = {"vector": "sVa", "scalar": "sSa", "gpsimd": "sGa"}
AUTOMAP = {"sV": "sVa", "sS": "sSa", "sG": "sGa"}
_FENCE_SKIP = {"wait_ge", "dma_start", "dma_start_transpose", "indirect_dma_start",
               "nop", "sem_inc", "sem_clear"}


class _Fence:

    def __init__(self, kb, eng, sname):
        self._kb = kb
        self._eng = eng
        self._sname = sname

    def __getattr__(self, a):
        real = getattr(self._eng, a)
        if a in _FENCE_SKIP or not callable(real):
            return real

        def f(*args, **kw):
            kb = self._kb
            sn = self._sname
            kb.sem(sn)
            self._eng.wait_ge(kb.sem_h[sn], kb.cnt.get(sn, 0))
            ins = real(*args, **kw)
            kb.cnt[sn] = kb.cnt.get(sn, 0) + 1
            if not isinstance(ins, _MockIns):
                ins.then_inc(kb.sem_h[sn], 1)
            return ins
        return f


class KB:
    def __init__(self, dbg_upto=None, dbg=False):
        self.nc = bass.Bass("TRN2", target_bir_lowering=False)
        self.dbg = dbg
        self.dbg_upto = 99 if dbg_upto is None else dbg_upto
        self.ctx = ExitStack()
        self.sem_h = {}
        self.cnt = {}
        self.marks = {}
        self.mock = False

    def sem(self, n):
        if n not in self.sem_h:
            self.sem_h[n] = self.ctx.enter_context(self.nc.semaphore(n))
        self.cnt.setdefault(n, 0)
        return self.sem_h[n]

    def bump(self, ins, n, amt=1, mark=None):
        if n in AUTOMAP:
            sn = AUTOMAP[n]
            self.sem(sn)
            v = self.cnt.get(sn, 0)
            if mark is not None:
                self.marks[mark] = (sn, v)
            return v
        self.sem(n)
        self.cnt[n] += amt
        if mark is not None:
            self.marks[mark] = (n, self.cnt[n])
        if ins is not None and not isinstance(ins, _MockIns):
            ins.then_inc(self.sem_h[n], amt)
        return self.cnt[n]

    def wt(self, eng, mark):
        got = self.marks.get(mark)
        if got is None:
            if self.mock:
                return
            raise KeyError(f"unresolved mark {mark}")
        n, v = got
        eng.wait_ge(self.sem(n), v)

    def wtv(self, eng, n, v):
        eng.wait_ge(self.sem(n), v)

    def phase(self, name, bodies):
        def wrap(e, engobj):
            return _Fence(self, engobj, AUTOSEM[e]) if e in AUTOSEM else engobj

        snap = dict(self.cnt)
        self.mock = True
        for e in ENGS:
            if e in bodies:
                bodies[e](wrap(e, _MockEng()))
        mock_end = dict(self.cnt)
        self.cnt = snap
        self.mock = False
        with self.nc.Block(name, no_gpsimd_drain=False) as blk:
            for e in ENGS:
                if e in bodies:
                    body = bodies[e]
                    getattr(blk, e)(lambda eng, b=body, e2=e: b(wrap(e2, eng)))
        for kk2 in set(mock_end) | set(self.cnt):
            assert mock_end.get(kk2, 0) == self.cnt.get(kk2, 0), (
                f"phase {name}: count drift on {kk2}: mock={mock_end.get(kk2, 0)} "
                f"real={self.cnt.get(kk2, 0)}")

    def build(self):
        nc = self.nc
        dbgkind = "ExternalOutput" if self.dbg else "Internal"
        self.x0 = nc.dram_tensor("x0", [S, D], F32, kind="ExternalInput").ap()
        self.wqk0 = nc.dram_tensor("wqk0", [32, 128, D], F32, kind="ExternalInput").ap()
        self.wv0 = nc.dram_tensor("wv0", [16, 128, D], F32, kind="ExternalInput").ap()
        self.wo0 = nc.dram_tensor("wo0", [NH, 128, D], F32, kind="ExternalInput").ap()
        self.w13 = nc.dram_tensor("w13", [NH, 128, 2 * F0], F32, kind="ExternalInput").ap()
        self.w2p = nc.dram_tensor("w2p", [NH, 128, F0], F32, kind="ExternalInput").ap()
        self.nwbi = nc.dram_tensor("nwbi", [128, F0], F32, kind="ExternalInput").ap()
        self.wq1 = nc.dram_tensor("wq1", [NH, 128, D], F32, kind="ExternalInput").ap()
        self.wk1 = nc.dram_tensor("wk1", [1, 128, D], F32, kind="ExternalInput").ap()
        self.wv1 = nc.dram_tensor("wv1", [1, 128, D], F32, kind="ExternalInput").ap()
        self.wo1 = nc.dram_tensor("wo1", [NH, 128, D], F32, kind="ExternalInput").ap()
        self.wgu1 = nc.dram_tensor("wgu1", [NFT, 128, 2 * D], F32, kind="ExternalInput").ap()
        self.wd1 = nc.dram_tensor("wd1", [NH, 128, F0], F32, kind="ExternalInput").ap()
        self.gsp = nc.dram_tensor("gsp", [128, NFT], F32, kind="ExternalInput").ap()
        self.usp = nc.dram_tensor("usp", [128, NFT], F32, kind="ExternalInput").ap()
        self.dsp = nc.dram_tensor("dsp", [128, NH], F32, kind="ExternalInput").ap()
        self.iden = nc.dram_tensor("iden", [128, 128], BF16, kind="ExternalInput").ap()
        self.mskin = nc.dram_tensor("mskin", [128, 2048], BF16, kind="ExternalInput").ap()
        self.cst = nc.dram_tensor("cst", [128, 8], F32, kind="ExternalInput").ap()

        self.out = nc.dram_tensor("outp", [S, D], F32, kind="ExternalOutput").ap()

        self.codes_h = nc.dram_tensor("codes_h", [NH, 128, S], BF16, kind=dbgkind).ap()
        self.xn_h = nc.dram_tensor("xn_h", [NH, 128, S], BF16, kind=dbgkind).ap()
        self.q_hb = nc.dram_tensor("q_hb", [HEADS, 128, S], BF16, kind=dbgkind).ap()
        self.k_hb = nc.dram_tensor("k_hb", [NH, 128, S], BF16, kind=dbgkind).ap()
        self.o_hb = nc.dram_tensor("o_hb", [HEADS, 128, S], BF16, kind=dbgkind).ap()
        self.v_hb = nc.dram_tensor("v_hb", [NT, 128, D], BF16, kind=dbgkind).ap()
        self.s_hb = nc.dram_tensor("s_hb", [NT, 128, D], BF16, kind=dbgkind).ap()
        self.g_hb = nc.dram_tensor("g_hb", [NT, 128, D], BF16, kind=dbgkind).ap()
        self.x1_h = nc.dram_tensor("x1_h", [NT, 128, D], F32, kind=dbgkind).ap()
        self.gn_hb = nc.dram_tensor("gn_hb", [S, F0], BF16, kind=dbgkind).ap()
        self.hf_h = nc.dram_tensor("hf_h", [NFT, 128, S], BF16, kind=dbgkind).ap()

        ctx = self.ctx
        self.P16 = ctx.enter_context(nc.sbuf_tensor("P16", [128, 86016], BF16))
        self.P32 = ctx.enter_context(nc.sbuf_tensor("P32", [128, 8192], F32))
        self.PI = ctx.enter_context(nc.sbuf_tensor("PI", [128, 1024], I32))
        self.ident = ctx.enter_context(nc.sbuf_tensor("identS", [128, 128], BF16))
        self.sinv = ctx.enter_context(nc.sbuf_tensor("sinv", [128, NT], F32))
        self.sinv3 = ctx.enter_context(nc.sbuf_tensor("sinv3", [128, NT], F32))
        self.ssqB = ctx.enter_context(nc.sbuf_tensor("ssqB", [128, NT], F32))
        self.mxgB = ctx.enter_context(nc.sbuf_tensor("mxgB", [128, NT], F32))
        self.aa8 = ctx.enter_context(nc.sbuf_tensor("aa8", [128, NT], F32))
        self.i2v = ctx.enter_context(nc.sbuf_tensor("i2v", [128, NT], F32))
        self.gs_s = ctx.enter_context(nc.sbuf_tensor("gs_s", [128, NFT], F32))
        self.gsh_s = ctx.enter_context(nc.sbuf_tensor("gsh_s", [128, NFT], F32))
        self.us_s = ctx.enter_context(nc.sbuf_tensor("us_s", [128, NFT], F32))
        self.ds_s = ctx.enter_context(nc.sbuf_tensor("ds_s", [128, NH], F32))
        self.epsb = ctx.enter_context(nc.sbuf_tensor("epsb", [128, 1], F32))
        self.st1 = ctx.enter_context(nc.sbuf_tensor("st1", [128, 8], F32))
        self.st2 = ctx.enter_context(nc.sbuf_tensor("st2", [128, 8], F32))
        self.st3 = ctx.enter_context(nc.sbuf_tensor("st3", [128, 8], F32))

        self.A = self.P16[:, 0:32768]
        self.Bp = self.P16[:, 32768:65536]
        self.Cp = self.P16[:, 65536:86016]

        self.phase_setup()
        steps = [self.phase_ln0_, self.phase_qkv0_, self.phase_af0, self.phase_wo0_,
                 self.phase_ffn0S, self.phase_ffn0B, self.phase_ln1_, self.phase_qkv1_,
                 self.phase_af1, self.phase_wo1_, self.phase_ffn1dn, self.phase_final]
        for i, f in enumerate(steps):
            if self.dbg_upto >= i + 1:
                f()
        if self.dbg_upto < len(steps):
            self.phase_flush()
        self.ctx.close()
        return self.nc

    def phase_ln0_(self):
        self.phase_ln(0)

    def phase_ln1_(self):
        self.phase_ln(1)

    def phase_qkv0_(self):
        self.phase_qkv(0)

    def phase_qkv1_(self):
        self.phase_qkv(1)

    def phase_wo0_(self):
        self.phase_wo(0)

    def phase_wo1_(self):
        self.phase_wo(1)

    def phase_setup(self):
        k = self

        def syncf(sy):
            for dst, src in [(k.ident[:], k.iden),
                             (k.gs_s[:], k.gsp), (k.us_s[:], k.usp),
                             (k.ds_s[:], k.dsp), (k.st3[:], k.cst)]:
                sy.dma_start(out=dst, in_=src).then_inc(k.sem("ld"), 16)
                k.cnt["ld"] = k.cnt.get("ld", 0) + 16
            k.wtv(sy, "ld", k.cnt.get("ld", 0))
            k.bump(sy.nop(), "sY", mark="setup_done")

        def vecf(v):
            k.wt(v, "setup_done")
            k.bump(v.tensor_copy(k.epsb[:], k.st3[:, 0:1]), "sV", mark="eps_ready")
            k.bump(v.tensor_scalar_mul(k.gsh_s[:], k.gs_s[:], 0.5), "sV", mark="gsh_ready")

        self.phase("setup", {"sync": syncf, "vector": vecf})

    def msk(self):
        return self.PI[:].bitcast(BF16)

    def phase_ln(self, L):
        k = self
        XF = [k.P32[:, 0:2048], k.P32[:, 2048:4096]]
        XNF = k.P32[:, 4096:6144]
        RS1 = k.P32[:, 6144:7168]
        RS2 = k.P32[:, 7168:8192]
        XN16 = k.Cp[:, 0:2048]
        SC16 = k.Cp[:, 2048:4096]
        CFM = k.Cp[:, 4096:6144]
        SGS = [k.Cp[:, 6144:8192], k.Cp[:, 8192:10240]]
        GGS = [k.Cp[:, 10240:12288], k.Cp[:, 12288:14336]]
        DUMP = k.Cp[:, 14336:16384]
        sums = k.st1[:, 0:1]; ssqv = k.st1[:, 1:2]; mean = k.st1[:, 2:3]
        msq = k.st1[:, 3:4]; varr = k.st1[:, 4:5]; sd = k.st1[:, 5:6]; rstd = k.st1[:, 6:7]
        ssn = k.st1[:, 7:8]
        rmx = k.st2[:, 0:1]; rm2 = k.st2[:, 1:2]; rcp = k.st2[:, 2:3]; rcp127 = k.st2[:, 3:4]
        tmp1 = k.st2[:, 4:5]
        Afm = k.A.rearrange("p (h t) -> p h t", h=NH)

        with self.nc.psum_tensor(f"TP{L}", [128, 2048], BF16) as TP:
            halves = [TP[:, 0:1024], TP[:, 1024:2048]]

            def syncf(sy):
                for i in range(NT):
                    if i >= 2:
                        k.wt(sy, f"ln{L}xfree_{i-2}")
                        if L == 1:
                            k.wtv(sy, "st", k.marks[f"ln{L}stchk_{i-2}"][1]) if False else None
                            k.wt(sy, f"ln{L}stchk_{i-2}")
                    sy.dma_start(out=XF[i % 2][:], in_=k.x0[i * 128:(i + 1) * 128, :]
                                 ).then_inc(k.sem(f"dx{i%2}"), 16)
                    amt = 16
                    if L == 1:
                        sy.dma_start(out=SGS[i % 2][:], in_=k.s_hb[i]).then_inc(k.sem(f"dx{i%2}"), 16)
                        sy.dma_start(out=GGS[i % 2][:], in_=k.g_hb[i]).then_inc(k.sem(f"dx{i%2}"), 16)
                        amt = 48
                    k.bump(None, f"dx{i%2}", amt, mark=f"ln{L}ld_{i}")
                    if L == 1:
                        k.wt(sy, f"ln{L}x1_{i}")
                        sy.dma_start(out=k.x1_h[i], in_=XF[i % 2][:]).then_inc(k.sem("st"), 16)
                        k.bump(None, "st", 16)
                        k.wt(sy, f"ln{L}xnev_{i}_1")
                        sy.dma_start(out=k.xn_h.rearrange("h p t -> p h t")[:, :, i * 128:(i + 1) * 128],
                                     in_=Afm[:, :, i * 128:(i + 1) * 128]).then_inc(k.sem("st"), 16)
                        k.bump(None, "st", 16)
                        k.wtv(sy, "st", k.cnt.get("st", 0))
                        k.bump(sy.nop(), "sY", mark=f"ln{L}stchk_{i}")
                    elif "codes" not in DBG_SKIP:
                        k.wt(sy, f"ln{L}cdev_{i}_1")
                        sy.dma_start(out=k.codes_h.rearrange("h p t -> p h t")[:, :, i * 128:(i + 1) * 128],
                                     in_=CFM.rearrange("p (h t) -> p h t", h=NH)).then_inc(k.sem("st"), 16)
                        k.bump(None, "st", 16)
                        k.wtv(sy, "st", k.cnt.get("st", 0))
                        k.bump(sy.nop(), "sY", mark=f"ln{L}stchk_{i}")
                k.wtv(sy, "st", k.cnt.get("st", 0))
                k.bump(sy.nop(), "sY", mark=f"ln{L}_done")

            def vecf(v):
                for i in range(NT):
                    x = XF[i % 2]
                    k.wt(v, f"ln{L}ld_{i}")
                    if L == 1:
                        v.tensor_tensor(XNF[:], SGS[i % 2][:], GGS[i % 2][:], OP.mult)
                        ins = v.tensor_tensor(x[:], x[:], XNF[:], OP.add)
                        k.bump(ins, "sV", mark=f"ln{L}x1_{i}")
                    ins = v.tensor_reduce(sums, x[:], AX, OP.add)
                    k.bump(ins, "sV", mark=f"ln{L}sum_{i}")
                    k.wt(v, f"ln{L}msq_{i}")
                    v.tensor_tensor(varr, ssn, msq, OP.subtract)
                    ins = v.tensor_tensor(varr, varr, k.epsb[:], OP.add)
                    k.bump(ins, "sV", mark=f"ln{L}var_{i}")
                    k.wt(v, f"ln{L}sd_{i}")
                    ins = v.reciprocal(rstd, sd)
                    k.bump(ins, "sV", mark=f"ln{L}rstd_{i}")
                    k.wt(v, f"ln{L}rstd_{i}")
                    ins = v.tensor_scalar(XNF[:], x[:], mean, rstd, OP.subtract, OP.mult)
                    k.bump(ins, "sV", mark=f"ln{L}xn_{i}")
                    if L == 1:
                        k.marks[f"ln{L}xfree_{i}"] = k.marks[f"ln{L}xn_{i}"]
                    elif "codes" in DBG_SKIP:
                        k.marks[f"ln{L}xfree_{i}"] = k.marks[f"ln{L}xn_{i}"]
                    else:
                        ins = v.tensor_reduce(rmx, XNF[:], AX, OP.max, apply_absolute_value=True)
                        k.bump(ins, "sV", mark=f"ln{L}rmx_{i}")
                        k.wt(v, f"ln{L}rmx_{i}")
                        v.tensor_single_scalar(rm2, rmx, 1e-5, OP.max)
                        ins = v.reciprocal(rcp, rm2)
                        k.bump(ins, "sV", mark=f"ln{L}rcp_{i}")
                        k.wt(v, f"ln{L}rcp_{i}")
                        v.tensor_single_scalar(rcp127, rcp, 127.0, OP.mult)
                        v.tensor_single_scalar(k.sinv[:, i:i + 1], rm2, 1.0 / 127.0, OP.mult)
                        v.tensor_tensor(tmp1, k.sinv[:, i:i + 1], k.sinv[:, i:i + 1], OP.mult)
                        v.tensor_tensor(k.sinv3[:, i:i + 1], tmp1, k.sinv[:, i:i + 1], OP.mult)
                        k.wt(v, f"ln{L}xn16_{i}")
                        for hh in range(2):
                            xnh = XNF[:, hh * 1024:(hh + 1) * 1024]
                            v.tensor_scalar(RS1[:], xnh, rcp127, 0.5, OP.mult, OP.add)
                            v.tensor_copy(k.PI[:], RS1[:])
                            v.tensor_copy(RS2[:], k.PI[:])
                            v.tensor_tensor(xnh, RS2[:], RS1[:], OP.is_gt)
                            ins = v.tensor_tensor(SC16[:, hh * 1024:(hh + 1) * 1024],
                                                  RS2[:], xnh, OP.subtract)
                        k.bump(ins, "sV", mark=f"ln{L}codes_{i}")
                        k.marks[f"ln{L}xfree_{i}"] = k.marks[f"ln{L}codes_{i}"]

            def scaf(sc):
                for i in range(NT):
                    k.wt(sc, f"ln{L}ld_{i}")
                    if L == 1:
                        k.wt(sc, f"ln{L}x1_{i}")
                    ins = sc.activation(DUMP[:], XF[i % 2][:], AF.Square, accum_out=ssqv)
                    k.bump(ins, "sS", mark=f"ln{L}ssq_{i}")
                    k.wt(sc, f"ln{L}sum_{i}")
                    ins = sc.mul(mean, sums, 1.0 / D)
                    k.bump(ins, "sS", mark=f"ln{L}mean_{i}")
                    k.wt(sc, f"ln{L}ssq_{i}")
                    sc.mul(ssn, ssqv, 1.0 / D)
                    k.wt(sc, f"ln{L}mean_{i}")
                    ins = sc.activation(msq, mean, AF.Square)
                    k.bump(ins, "sS", mark=f"ln{L}msq_{i}")
                    k.wt(sc, f"ln{L}var_{i}")
                    ins = sc.activation(sd, varr, AF.Sqrt)
                    k.bump(ins, "sS", mark=f"ln{L}sd_{i}")
                    k.wt(sc, f"ln{L}xn_{i}")
                    ins = sc.copy(XN16[:], XNF[:])
                    k.bump(ins, "sS", mark=f"ln{L}xn16_{i}")
                    if "xpose" not in DBG_SKIP:
                        for b in range(2):
                            k.wt(sc, f"ln{L}txn_{i}_{b}")
                            if "ev3d" in DBG_SKIP:
                                for j in range(8):
                                    ins = sc.copy(k.A[:, (b * 8 + j) * 2048 + i * 128:
                                                     (b * 8 + j) * 2048 + i * 128 + 128],
                                                  halves[b][:, j * 128:(j + 1) * 128])
                            else:
                                dst = Afm[:, b * 8:(b + 1) * 8, i * 128:(i + 1) * 128]
                                ins = sc.copy(dst, halves[b].rearrange("p (h t) -> p h t", h=8))
                            k.bump(ins, "sS", mark=f"ln{L}xnev_{i}_{b}")
                    if L == 0 and "codes" not in DBG_SKIP and "xpose" not in DBG_SKIP:
                        for b in range(2):
                            k.wt(sc, f"ln{L}tcd_{i}_{b}")
                            dst = CFM.rearrange("p (h t) -> p h t", h=NH)[:, b * 8:(b + 1) * 8, :]
                            ins = sc.copy(dst, halves[b].rearrange("p (h t) -> p h t", h=8))
                            k.bump(ins, "sS", mark=f"ln{L}cdev_{i}_{b}")

            def tenf(t):
                if "xpose" in DBG_SKIP:
                    return
                tphist = {}
                for i in range(NT):
                    k.wt(t, f"ln{L}xn16_{i}")
                    for b in range(2):
                        if b in tphist:
                            k.wt(t, tphist[b])
                        for j in range(8):
                            ins = t.transpose(halves[b][:, j * 128:(j + 1) * 128],
                                              XN16[:, (b * 8 + j) * 128:(b * 8 + j) * 128 + 128],
                                              k.ident[:])
                        k.bump(ins, "sT", mark=f"ln{L}txn_{i}_{b}")
                        tphist[b] = f"ln{L}xnev_{i}_{b}"
                    if L == 0 and "codes" not in DBG_SKIP:
                        k.wt(t, f"ln{L}codes_{i}")
                        for b in range(2):
                            k.wt(t, tphist[b])
                            for j in range(8):
                                ins = t.transpose(halves[b][:, j * 128:(j + 1) * 128],
                                                  SC16[:, (b * 8 + j) * 128:(b * 8 + j) * 128 + 128],
                                                  k.ident[:])
                            k.bump(ins, "sT", mark=f"ln{L}tcd_{i}_{b}")
                            tphist[b] = f"ln{L}cdev_{i}_{b}"

            self.phase(f"ln{L}", {"tensor": tenf, "vector": vecf, "scalar": scaf, "sync": syncf})

    def phase_qkv(self, L):
        k = self
        WS = [k.Cp[:, 0:2048], k.Cp[:, 2048:4096]]
        QS = k.Cp[:, 4096:6144]
        VTS = k.Cp[:, 6144:8192]
        if L == 0:
            obs = [("q", i, k.wqk0[i]) for i in range(16)] + \
                  [("k", i, k.wqk0[16 + i]) for i in range(16)] + \
                  [("v", i, k.wv0[i]) for i in range(16)]
        else:
            obs = [("q", i, k.wq1[i]) for i in range(16)] + \
                  [("k", 0, k.wk1[0]), ("v", 0, k.wv1[0])]

        with self.nc.psum_tensor(f"QP{L}", [128, 2048], F32) as PB, \
             self.nc.psum_tensor(f"QT{L}", [128, 2048], BF16) as TP:
            tphalves = [TP[:, 0:1024], TP[:, 1024:2048]]

            def gpsf(g):
                for n, (kind, ob, src) in enumerate(obs):
                    if n >= 2:
                        k.wt(g, f"qkv{L}free_{n-2}")
                    g.dma_start(out=WS[n % 2][:], in_=src).then_inc(k.sem(f"dw{n%2}"), 16)
                    k.bump(None, f"dw{n%2}", 16, mark=f"qkv{L}w_{n}")

            def tenf(t):
                tphist = {}
                for n, (kind, ob, src) in enumerate(obs):
                    k.wt(t, f"qkv{L}w_{n}")
                    for tc in range(4):
                        if n >= 1:
                            k.wt(t, f"qkv{L}ev_{n-1}_{tc}")
                        pq = PB[:, tc * 512:(tc + 1) * 512]
                        for ht in range(NH):
                            ins = t.matmul(pq, WS[n % 2][:, ht * 128:(ht + 1) * 128],
                                           k.A[:, ht * 2048 + tc * 512: ht * 2048 + tc * 512 + 512],
                                           start=(ht == 0), stop=(ht == NH - 1))
                        k.bump(ins, "sT", mark=f"qkv{L}mm_{n}_{tc}")
                    if kind == "v":
                        k.wt(t, f"qkv{L}ev_{n}_3")
                        for half in range(2):
                            if half in tphist:
                                k.wt(t, tphist[half])
                            for j in range(8):
                                tt = half * 8 + j
                                ins = t.transpose(tphalves[half][:, j * 128:(j + 1) * 128],
                                                  QS[:, tt * 128:(tt + 1) * 128], k.ident[:])
                            k.bump(ins, "sT", mark=f"qkv{L}vt_{n}_{half}")
                            tphist[half] = f"qkv{L}vte_{n}_{half}"

            def vecf(v):
                for n, (kind, ob, src) in enumerate(obs):
                    for tc in range(4):
                        k.wt(v, f"qkv{L}mm_{n}_{tc}")
                        pq = PB[:, tc * 512:(tc + 1) * 512]
                        if kind == "q":
                            if n >= 1 and tc == 0 and obs[n - 1][0] in ("q", "k"):
                                k.wt(v, f"qkv{L}qst_{n-1}")
                            ins = v.tensor_single_scalar(QS[:, tc * 512:(tc + 1) * 512], pq, SCALE, OP.mult)
                        elif kind == "k":
                            if L == 0:
                                if tc == 0:
                                    pv = [m for m, (kk2, _, _) in enumerate(obs[:n]) if kk2 in ("q", "k")]
                                    if pv:
                                        k.wt(v, f"qkv{L}qst_{pv[-1]}")
                                dst = QS[:, tc * 512:(tc + 1) * 512]
                            else:
                                dst = k.Cp[:, 8192 + tc * 512: 8192 + tc * 512 + 512]
                            ins = v.tensor_copy(dst, pq)
                        else:
                            if tc == 0:
                                users = ("q", "k", "v") if L == 0 else ("q", "v")
                                pv = [m for m, (kk2, _, _) in enumerate(obs[:n]) if kk2 in users]
                                if pv:
                                    pn = pv[-1]
                                    if obs[pn][0] == "v":
                                        k.wt(v, f"qkv{L}vt_{pn}_1")
                                    else:
                                        k.wt(v, f"qkv{L}qst_{pn}")
                            ins = v.tensor_copy(QS[:, tc * 512:(tc + 1) * 512], pq)
                        k.bump(ins, "sV", mark=f"qkv{L}ev_{n}_{tc}")
                    k.marks[f"qkv{L}free_{n}"] = k.marks[f"qkv{L}ev_{n}_3"]

            def scaf(sc):
                for n, (kind, ob, src) in enumerate(obs):
                    if kind != "v":
                        continue
                    for half in range(2):
                        k.wt(sc, f"qkv{L}vt_{n}_{half}")
                        prevv = [m for m, (kk, _, _) in enumerate(obs[:n]) if kk == "v"]
                        if prevv:
                            k.wt(sc, f"qkv{L}vstf_{prevv[-1]}_{half}")
                        ins = sc.copy(VTS[:, half * 1024:(half + 1) * 1024], tphalves[half][:])
                        k.bump(ins, "sS", mark=f"qkv{L}vte_{n}_{half}")

            def syncf(sy):
                for n, (kind, ob, src) in enumerate(obs):
                    if kind == "q" or (kind == "k" and L == 0):
                        k.wt(sy, f"qkv{L}ev_{n}_3")
                        dst = k.q_hb[ob] if kind == "q" else k.k_hb[ob]
                        sy.dma_start(out=dst, in_=QS[:]).then_inc(k.sem("st"), 16)
                        k.bump(None, "st", 16)
                        k.wtv(sy, "st", k.cnt.get("st", 0))
                        k.bump(sy.nop(), "sY", mark=f"qkv{L}qst_{n}")
                    elif kind == "v":
                        for half in range(2):
                            k.wt(sy, f"qkv{L}vte_{n}_{half}")
                            dst = k.v_hb.rearrange("n p t -> p n t")[:, half * 8:(half + 1) * 8,
                                                                     ob * 128:(ob + 1) * 128]
                            srcap = VTS[:, half * 1024:(half + 1) * 1024].rearrange("p (n t) -> p n t", n=8)
                            sy.dma_start(out=dst, in_=srcap).then_inc(k.sem("st"), 16)
                            k.bump(None, "st", 16)
                            k.wtv(sy, "st", k.cnt.get("st", 0))
                            k.bump(sy.nop(), "sY", mark=f"qkv{L}vstf_{n}_{half}")
                k.wtv(sy, "st", k.cnt.get("st", 0))
                k.bump(sy.nop(), "sY", mark=f"qkv{L}_done")

            self.phase(f"qkv{L}", {"tensor": tenf, "vector": vecf, "scalar": scaf,
                                   "sync": syncf, "gpsimd": gpsf})

    def phase_att(self, L):
        k = self
        causal = (L == 0)
        QH = [k.Cp[:, 0:2048], k.Cp[:, 2048:4096]]
        VH = [k.Cp[:, 4096:6144], k.Cp[:, 6144:8192]]
        PSBs = ([k.Cp[:, 8192:10240], k.Cp[:, 18432:20480]] if L == 1
                else [k.Cp[:, 8192:10240], k.Cp[:, 8192:10240]])
        PT = k.Cp[:, 10240:18432]
        ngp = [k.st2[:, j:j + 1] for j in range(4)]
        negm_b = [k.st1[:, 0:1], k.st1[:, 6:7]]
        za = k.st1[:, 1:2]; zb = k.st1[:, 2:3]
        zs = k.st1[:, 3:4]; rz = k.st1[:, 4:5]; tmp = k.st1[:, 5:6]

        def nchf(qt):
            return qt // 4 + 1 if causal else 4

        def nregf(qt):
            return 1 if nchf(qt) <= 2 else 2

        regs_of = {}
        rr = 0
        for h in range(HEADS):
            for qt in range(NT):
                n = nregf(qt)
                regs_of[(h, qt)] = [rr, (rr + 1) % 3][:n]
                rr = (rr + n) % 3

        with self.nc.psum_tensor(f"SR0{L}", [128, 1024], F32) as SR0, \
             self.nc.psum_tensor(f"SR1{L}", [128, 1024], F32) as SR1, \
             self.nc.psum_tensor(f"SR2{L}", [128, 1024], F32) as SR2, \
             self.nc.psum_tensor(f"PP{L}", [128, 1024], BF16) as PP, \
             self.nc.psum_tensor(f"OT{L}", [128, 512], F32) as OT:
            SCR = [SR0, SR1, SR2]

            def scslice(h, qt, c):
                rg = regs_of[(h, qt)][c // 2]
                return SCR[rg][:, (c % 2) * 512:(c % 2 + 1) * 512]

            def transp(t, h, qt, ppq):
                nkb = nchf(qt) * 4
                PSB = PSBs[qt % 2]
                k.wt(t, f"a{L}pn_{h}_{qt}")
                for b in range((nkb + 7) // 8):
                    if ppq:
                        for m in ppq[0]:
                            k.wt(t, m)
                    j0, j1 = b * 8, min(nkb, b * 8 + 8)
                    for j in range(j0, j1):
                        ins = t.transpose(PP[:, (j - j0) * 128:(j - j0 + 1) * 128],
                                          PSB[:, j * 128:(j + 1) * 128], k.ident[:])
                    k.bump(ins, "sT", mark=f"a{L}tpb_{h}_{qt}_{b}")
                    nm = "S" if b % 2 == 0 else "V"
                    ppq[:] = [[f"a{L}pte{nm}_{h}_{qt}_{b}"]]
                k.marks[f"a{L}tp_{h}_{qt}"] = k.marks[f"a{L}tpb_{h}_{qt}_{(nkb-1)//8}"]

            def syncf(sy):
                for h in range(HEADS):
                    if h >= 2:
                        k.wt(sy, f"a{L}hdone_{h-2}")
                    sy.dma_start(out=QH[h % 2][:], in_=k.q_hb[h]).then_inc(k.sem(f"dq{h%2}"), 16)
                    c0 = h * 128 if L == 0 else 0
                    sy.dma_start(out=VH[h % 2][:].rearrange("p (n t) -> p n t", n=16),
                                 in_=k.v_hb.rearrange("n p t -> p n t")[:, :, c0:c0 + 128]
                                 ).then_inc(k.sem(f"dq{h%2}"), 16)
                    k.bump(None, f"dq{h%2}", 32, mark=f"a{L}qv_{h}")

            def tenf(t):
                ppq = []
                regfree = {}
                for h in range(HEADS):
                    k.wt(t, f"a{L}qv_{h}")
                    for qt in range(NT):
                        nch = nchf(qt)
                        kbase = h * 2048 if L == 0 else 0
                        regs = regs_of[(h, qt)]
                        for i, rg in enumerate(regs):
                            if rg in regfree:
                                k.wt(t, regfree[rg])
                            for c in range(2 * i, min(nch, 2 * i + 2)):
                                ins = t.matmul(scslice(h, qt, c),
                                               QH[h % 2][:, qt * 128:(qt + 1) * 128],
                                               k.Bp[:, kbase + c * 512: kbase + c * 512 + 512],
                                               start=True, stop=True)
                                k.bump(ins, "sT", mark=f"a{L}scc_{h}_{qt}_{c}")
                            regfree[rg] = f"a{L}exp_{h}_{qt}_{i}"
                        if qt > 0:
                            transp(t, h, qt - 1, ppq)
                            if (qt - 1) % 4 == 3:
                                self._att_pv(t, L, h, (qt - 1) // 4, causal, VH, PT, OT)
                    transp(t, h, NT - 1, ppq)
                    self._att_pv(t, L, h, 3, causal, VH, PT, OT)

            def do_rzpn(v, h, qt):
                nch = nchf(qt)
                nreg = nregf(qt)
                if nreg == 2:
                    k.wt(v, f"a{L}exp_{h}_{qt}_1")
                    ins = v.tensor_tensor(zs, za, zb, OP.add)
                    k.bump(ins, "sV")
                    k.wtv(v, "sVa", k.cnt["sVa"])
                    ins = v.reciprocal(rz, zs)
                else:
                    k.wt(v, f"a{L}exp_{h}_{qt}_0")
                    ins = v.reciprocal(rz, za)
                k.bump(ins, "sV", mark=f"a{L}rz_{h}_{qt}")
                k.wt(v, f"a{L}rz_{h}_{qt}")
                PSB = PSBs[qt % 2]
                ins = v.tensor_scalar_mul(PSB[:, 0:nch * 512], PSB[:, 0:nch * 512], rz)
                k.bump(ins, "sV", mark=f"a{L}pn_{h}_{qt}")

            def vecf(v):
                for h in range(HEADS):
                    for qt in range(NT):
                        nch = nchf(qt)
                        nreg = nregf(qt)
                        regs = regs_of[(h, qt)]
                        off = qt % 4
                        if qt > 0:
                            do_rzpn(v, h, qt - 1)
                        if qt > 1:
                            self._ptevac_v(v, L, h, qt - 2, nchf, PP, PT)
                        for i, rg in enumerate(regs):
                            sp = min(nch, 2 * i + 2) - 2 * i
                            for c in range(2 * i, 2 * i + sp):
                                k.wt(v, f"a{L}scc_{h}_{qt}_{c}")
                            if causal and 2 * i + sp == nch:
                                sl = scslice(h, qt, nch - 1)
                                v.tensor_tensor(sl, sl, k.msk()[:, off * 512:(off + 1) * 512], OP.add)
                            ins = v.tensor_reduce(ngp[i], SCR[rg][:, 0:sp * 512], AX, OP.max, negate=True)
                        k.bump(ins, "sV")
                        k.wtv(v, "sVa", k.cnt["sVa"])
                        negm = negm_b[qt % 2]
                        if nreg == 1:
                            ins = v.tensor_copy(negm, ngp[0])
                        else:
                            ins = v.tensor_tensor(negm, ngp[0], ngp[1], OP.min)
                        k.bump(ins, "sV", mark=f"a{L}neg_{h}_{qt}")
                        if qt % 4 == 1 and qt >= 5:
                            g = qt // 4 - 1
                            k.wt(v, f"a{L}pv_{h}_{g}")
                            ins = v.tensor_copy(k.A[:, h * 2048 + g * 512: h * 2048 + g * 512 + 512], OT[:])
                            k.bump(ins, "sV", mark=f"a{L}oev_{h}_{g}")
                    do_rzpn(v, h, NT - 1)
                    self._ptevac_v(v, L, h, NT - 2, nchf, PP, PT)
                    self._ptevac_v(v, L, h, NT - 1, nchf, PP, PT)
                    k.wt(v, f"a{L}pv_{h}_3")
                    ins = v.tensor_copy(k.A[:, h * 2048 + 3 * 512: h * 2048 + 3 * 512 + 512], OT[:])
                    k.bump(ins, "sV", mark=f"a{L}oev_{h}_3")
                    k.marks[f"a{L}hdone_{h}"] = k.marks[f"a{L}oev_{h}_3"]

            def scaf(sc):
                for h in range(HEADS):
                    for qt in range(NT):
                        nch = nchf(qt)
                        regs = regs_of[(h, qt)]
                        PSB = PSBs[qt % 2]
                        if L == 0 and qt > 0:
                            self._ptevac_s(sc, L, h, qt - 1, nchf, PP, PT)
                        k.wt(sc, f"a{L}neg_{h}_{qt}")
                        if L == 0:
                            if qt > 0:
                                k.wt(sc, f"a{L}tp_{h}_{qt-1}")
                            elif h > 0:
                                k.wt(sc, f"a{L}tp_{h-1}_{NT-1}")
                        else:
                            if qt > 1:
                                k.wt(sc, f"a{L}tp_{h}_{qt-2}")
                            elif h > 0:
                                k.wt(sc, f"a{L}tp_{h-1}_{NT-2+qt}")
                        for i, rg in enumerate(regs):
                            sp = min(nch, 2 * i + 2) - 2 * i
                            ins = sc.activation(PSB[:, 2 * i * 512:(2 * i + sp) * 512],
                                                SCR[rg][:, 0:sp * 512], AF.Exp,
                                                bias=negm_b[qt % 2], scale=1.0,
                                                accum_out=(za if i == 0 else zb))
                            k.bump(ins, "sS", mark=f"a{L}exp_{h}_{qt}_{i}")
                            if L == 1 and i == 0 and qt > 0:
                                self._ptevac_s(sc, L, h, qt - 1, nchf, PP, PT)
                    self._ptevac_s(sc, L, h, NT - 1, nchf, PP, PT)

            self.phase(f"att{L}", {"tensor": tenf, "vector": vecf, "scalar": scaf, "sync": syncf})

    def _ptevac_common(self, eng, semn, L, h, qt, nchf, PP, PT, parity):
        k = self
        nkb = nchf(qt) * 4
        nb = (nkb + 7) // 8
        for b in range(nb):
            if b % 2 != parity:
                continue
            j0, j1 = b * 8, min(nkb, b * 8 + 8)
            if qt % 4 == 0:
                if qt > 0:
                    k.wt(eng, f"a{L}pv_{h}_{qt//4 - 1}")
                elif h > 0:
                    k.wt(eng, f"a{L}pv_{h-1}_3")
            k.wt(eng, f"a{L}tpb_{h}_{qt}_{b}")
            dst = PT.rearrange("p (n t) -> p n t", n=16)[:, j0:j1,
                                                         (qt % 4) * 128:(qt % 4) * 128 + 128]
            src = PP[:, 0:(j1 - j0) * 128].rearrange("p (n t) -> p n t", n=j1 - j0)
            if semn == "sS":
                ins = eng.copy(dst, src)
            else:
                ins = eng.tensor_copy(dst, src)
            nm = "S" if parity == 0 else "V"
            k.bump(ins, semn, mark=f"a{L}pte{nm}_{h}_{qt}_{b}")

    def _ptevac_s(self, sc, L, h, qt, nchf, PP, PT):
        self._ptevac_common(sc, "sS", L, h, qt, nchf, PP, PT, parity=0)

    def _ptevac_v(self, v, L, h, qt, nchf, PP, PT):
        self._ptevac_common(v, "sV", L, h, qt, nchf, PP, PT, parity=1)

    def _att_pv(self, t, L, h, g, causal, VH, PT, OT):
        k = self
        ktn = (4 * g + 4) if causal else 16
        for qq in range(4 * g, 4 * g + 4):
            nkb = ((qq // 4 + 1) if causal else 4) * 4
            for b in range((nkb + 7) // 8):
                nm = "S" if b % 2 == 0 else "V"
                k.wt(t, f"a{L}pte{nm}_{h}_{qq}_{b}")
        if not (h == 0 and g == 0):
            k.wt(t, f"a{L}oev_{h}_{g-1}" if g > 0 else f"a{L}oev_{h-1}_3")
        for kt in range(ktn):
            ins = t.matmul(OT[:], VH[h % 2][:, kt * 128:(kt + 1) * 128],
                           PT[:, kt * 512:(kt + 1) * 512],
                           start=(kt == 0), stop=(kt == ktn - 1))
        k.bump(ins, "sT", mark=f"a{L}pv_{h}_{g}")


    def phase_af0(self):
        k = self
        L = 0
        QH = k.Cp[:, 0:2048]
        KH = k.Cp[:, 2048:4096]
        VH = k.Cp[:, 4096:6144]
        PSBs = [k.Cp[:, 6144:8192], k.Cp[:, 18432:20480]]
        GNS = [k.Cp[:, 8192:8704], k.Cp[:, 8704:9216]]
        DUMP = k.Cp[:, 9216:9728]
        OS = k.Cp[:, 9728:10240]
        PT = k.Cp[:, 10240:18432]
        RL = k.P32[:, 0:512]
        M1 = k.P32[:, 512:1024]
        GT = k.P32[:, 1024:1536]
        NWB = k.P32[:, 1536:7168]
        negm_b = [k.st1[:, 0:1], k.st1[:, 6:7]]
        za = k.st1[:, 1:2]; rz = k.st1[:, 4:5]
        ssp = k.st2[:, 6:7]; mxp = k.st2[:, 7:8]
        NU = NFC * NT

        def nchf(qt):
            return qt // 4 + 1

        with self.nc.psum_tensor("SCm0", [128, 2048], F32) as SC, \
             self.nc.psum_tensor("PPm0", [128, 1024], BF16) as PP, \
             self.nc.psum_tensor("OTm0", [128, 512], F32) as OT, \
             self.nc.psum_tensor("FAm0", [128, 1024], F32) as FA:
            X1 = FA[:, 0:512]
            X3 = FA[:, 512:1024]

            def transp(t, h, qt, ppq):
                nkb = nchf(qt) * 4
                PSB = PSBs[qt % 2]
                k.wt(t, f"m0pn_{h}_{qt}")
                for b in range((nkb + 7) // 8):
                    if ppq:
                        k.wt(t, ppq[0])
                    j0, j1 = b * 8, min(nkb, b * 8 + 8)
                    for j in range(j0, j1):
                        ins = t.transpose(PP[:, (j - j0) * 128:(j - j0 + 1) * 128],
                                          PSB[:, j * 128:(j + 1) * 128], k.ident[:])
                    k.bump(ins, "sT", mark=f"m0tpb_{h}_{qt}_{b}")
                    ppq[:] = [f"m0pteV_{h}_{qt}_{b}"]
                k.marks[f"m0tp_{h}_{qt}"] = k.marks[f"m0tpb_{h}_{qt}_{(nkb-1)//8}"]

            def ptevac(eng, semn, h, qt, parity):
                nkb = nchf(qt) * 4
                for b in range((nkb + 7) // 8):
                    if qt % 4 == 0 and b == 0:
                        if qt > 0:
                            k.wt(eng, f"m0pv_{h}_{qt//4 - 1}")
                        elif h > 0:
                            k.wt(eng, f"m0pv_{h-1}_3")
                    k.wt(eng, f"m0tpb_{h}_{qt}_{b}")
                    j0, j1 = b * 8, min(nkb, b * 8 + 8)
                    dst = PT.rearrange("p (n t) -> p n t", n=16)[:, j0:j1,
                                                                 (qt % 4) * 128:(qt % 4) * 128 + 128]
                    src = PP[:, 0:(j1 - j0) * 128].rearrange("p (n t) -> p n t", n=j1 - j0)
                    ins = eng.copy(dst, src) if semn == "sS" else eng.tensor_copy(dst, src)
                    k.bump(ins, semn, mark=f"m0pteV_{h}_{qt}_{b}")

            def pv(t, h, g):
                ktn = 4 * g + 4
                for qq in range(4 * g, 4 * g + 4):
                    nkb = (qq // 4 + 1) * 4
                    for b in range((nkb + 7) // 8):
                        k.wt(t, f"m0pteV_{h}_{qq}_{b}")
                if not (h == 0 and g == 0):
                    k.wt(t, f"m0oev_{h}_{g-1}" if g > 0 else f"m0oev_{h-1}_3")
                for kt in range(ktn):
                    ins = t.matmul(OT[:], VH[:, kt * 128:(kt + 1) * 128],
                                   PT[:, kt * 512:(kt + 1) * 512],
                                   start=(kt == 0), stop=(kt == ktn - 1))
                k.bump(ins, "sT", mark=f"m0pv_{h}_{g}")

            def syncf(sy):
                sy.dma_start(out=k.Bp[:].rearrange("p (h t) -> p h t", h=NH),
                             in_=k.codes_h.rearrange("h p t -> p h t")).then_inc(k.sem("ld"), 16)
                k.cnt["ld"] = k.cnt.get("ld", 0) + 16
                sy.dma_start(out=NWB[:], in_=k.nwbi).then_inc(k.sem("ld"), 16)
                k.cnt["ld"] = k.cnt.get("ld", 0) + 16
                sy.dma_start(out=k.msk(), in_=k.mskin).then_inc(k.sem("ld"), 16)
                k.cnt["ld"] = k.cnt.get("ld", 0) + 16
                k.wtv(sy, "ld", k.cnt.get("ld", 0))
                k.bump(sy.nop(), "sY", mark="m0_pre")
                un = [0]
                acc = [0.0]

                def ffn_units(nmax):
                    while un[0] < nmax:
                        n = un[0]
                        fc, tt = divmod(n, NT)
                        k.wt(sy, f"f0gn_{fc}_{tt}")
                        sy.dma_start(out=k.gn_hb[tt * 128:(tt + 1) * 128, fc * 512:(fc + 1) * 512],
                                     in_=GNS[n % 2][:]).then_inc(k.sem("st"), 16)
                        k.bump(None, "st", 16)
                        k.wtv(sy, "st", k.cnt.get("st", 0))
                        k.bump(sy.nop(), "sY", mark=f"f0gnfree_{fc}_{tt}")
                        un[0] += 1

                for h in range(HEADS):
                    if h >= 1:
                        k.wt(sy, f"m0hdone_{h-1}")
                    sy.dma_start(out=QH[:], in_=k.q_hb[h]).then_inc(k.sem("dq0"), 16)
                    sy.dma_start(out=KH[:], in_=k.k_hb[h]).then_inc(k.sem("dq0"), 16)
                    sy.dma_start(out=VH[:].rearrange("p (n t) -> p n t", n=16),
                                 in_=k.v_hb.rearrange("n p t -> p n t")[:, :, h * 128:(h + 1) * 128]
                                 ).then_inc(k.sem("dq0"), 16)
                    k.bump(None, "dq0", 48, mark=f"m0qv_{h}")
                    for g in range(4):
                        acc[0] += NU / (HEADS * 4)
                        ffn_units(min(NU, int(acc[0])))
                        k.wt(sy, f"m0oev_{h}_{g}")
                        sy.dma_start(out=k.o_hb[h][:, g * 512:(g + 1) * 512], in_=OS[:]
                                     ).then_inc(k.sem("st"), 16)
                        k.bump(None, "st", 16)
                        k.wtv(sy, "st", k.cnt.get("st", 0))
                        k.bump(sy.nop(), "sY", mark=f"m0ofree_{h}_{g}")
                ffn_units(NU)
                k.bump(sy.nop(), "sY", mark="af0_done")

            def tenf(t):
                k.wt(t, "m0_pre")
                ppq = []
                un = [0]
                acc = [0.0]

                def ffn_units(nmax):
                    while un[0] < nmax:
                        n = un[0]
                        fc, tt = divmod(n, NT)
                        k.wt(t, f"f0w_{fc}")
                        if n >= 1:
                            pf, pt2 = divmod(n - 1, NT)
                            k.wt(t, f"f0gt_{pf}_{pt2}")
                        for ht in range(NH):
                            t.matmul(X1, k.Bp[:, ht * 2048 + tt * 128: ht * 2048 + tt * 128 + 128],
                                     k.A[:, (fc % 2) * 8192 + ht * 512:(fc % 2) * 8192 + ht * 512 + 512],
                                     start=(ht == 0), stop=(ht == NH - 1))
                        for ht in range(NH):
                            ins = t.matmul(X3, k.Bp[:, ht * 2048 + tt * 128: ht * 2048 + tt * 128 + 128],
                                           k.A[:, 16384 + (fc % 2) * 8192 + ht * 512:
                                               16384 + (fc % 2) * 8192 + ht * 512 + 512],
                                           start=(ht == 0), stop=(ht == NH - 1))
                        k.bump(ins, "sT", mark=f"f0mm_{fc}_{tt}")
                        if tt == NT - 1:
                            k.marks[f"f0adone_{fc}"] = k.marks[f"f0mm_{fc}_{tt}"]
                        un[0] += 1

                prev = [0]
                for h in range(HEADS):
                    for qt in range(NT):
                        nch = nchf(qt)
                        ffn_units(min(NU, prev[0]))
                        if qt == 0:
                            k.wt(t, f"m0qv_{h}")
                        if qt > 0:
                            k.wt(t, f"m0exp_{h}_{qt-1}")
                        elif h > 0:
                            k.wt(t, f"m0exp_{h-1}_{NT-1}")
                        for c in range(nch):
                            ins = t.matmul(SC[:, c * 512:(c + 1) * 512],
                                           QH[:, qt * 128:(qt + 1) * 128],
                                           KH[:, c * 512:(c + 1) * 512],
                                           start=True, stop=True)
                        k.bump(ins, "sT", mark=f"m0sc_{h}_{qt}")
                        if qt > 0:
                            transp(t, h, qt - 1, ppq)
                            if (qt - 1) % 4 == 3:
                                pv(t, h, (qt - 1) // 4)
                        acc[0] += NU / (HEADS * NT)
                        prev[0] = int(acc[0])
                    ffn_units(min(NU, prev[0]))
                    transp(t, h, NT - 1, ppq)
                    pv(t, h, 3)
                ffn_units(NU)

            def vecf(v):
                un = [0]
                acc = [0.0]

                def ffn_units(nmax):
                    while un[0] < nmax:
                        n = un[0]
                        fc, tt = divmod(n, NT)
                        k.wt(v, f"f0rl_{fc}_{tt}")
                        v.tensor_tensor(M1[:], X1, RL[:], OP.mult)
                        ins = v.scalar_tensor_tensor(GT[:], M1[:], k.sinv3[:, tt:tt + 1], X3,
                                                     OP.mult, OP.mult)
                        k.bump(ins, "sV", mark=f"f0gt_{fc}_{tt}")
                        if n >= 2:
                            pf, pt2 = divmod(n - 2, NT)
                            k.wt(v, f"f0gnfree_{pf}_{pt2}")
                        ins = v.tensor_tensor(GNS[n % 2][:], GT[:], NWB[:, fc * 512:(fc + 1) * 512], OP.mult)
                        k.bump(ins, "sV", mark=f"f0gn_{fc}_{tt}")
                        ins = v.tensor_reduce(mxp, GNS[n % 2][:], AX, OP.max, apply_absolute_value=True)
                        k.bump(ins, "sV", mark=f"f0mxr_{fc}_{tt}")
                        k.wt(v, f"f0ss_{fc}_{tt}")
                        k.wt(v, f"f0mxr_{fc}_{tt}")
                        if fc == 0:
                            v.tensor_copy(k.ssqB[:, tt:tt + 1], ssp)
                            ins = v.tensor_copy(k.mxgB[:, tt:tt + 1], mxp)
                        else:
                            v.tensor_tensor(k.ssqB[:, tt:tt + 1], k.ssqB[:, tt:tt + 1], ssp, OP.add)
                            ins = v.tensor_tensor(k.mxgB[:, tt:tt + 1], k.mxgB[:, tt:tt + 1], mxp, OP.max)
                        k.bump(ins, "sV", mark=f"f0ssacc_{fc}_{tt}")
                        un[0] += 1

                for h in range(HEADS):
                    for qt in range(NT):
                        nch = nchf(qt)
                        off = qt % 4
                        k.wt(v, f"m0sc_{h}_{qt}")
                        if h == 0 and qt == 0:
                            k.wt(v, "m0_pre")
                        v.tensor_tensor(SC[:, (nch - 1) * 512:nch * 512],
                                        SC[:, (nch - 1) * 512:nch * 512],
                                        k.msk()[:, off * 512:(off + 1) * 512], OP.add)
                        ins = v.tensor_reduce(negm_b[qt % 2], SC[:, 0:nch * 512], AX, OP.max, negate=True)
                        k.bump(ins, "sV", mark=f"m0neg_{h}_{qt}")
                        k.wt(v, f"m0exp_{h}_{qt}")
                        ins = v.reciprocal(rz, za)
                        k.bump(ins, "sV", mark=f"m0rz_{h}_{qt}")
                        k.wt(v, f"m0rz_{h}_{qt}")
                        PSB = PSBs[qt % 2]
                        ins = v.tensor_scalar_mul(PSB[:, 0:nch * 512], PSB[:, 0:nch * 512], rz)
                        k.bump(ins, "sV", mark=f"m0pn_{h}_{qt}")
                        if qt > 0:
                            ptevac(v, "sV", h, qt - 1, None)
                        if qt % 4 == 1 and qt >= 5:
                            g = qt // 4 - 1
                            k.wt(v, f"m0pv_{h}_{g}")
                            if g > 0 or h > 0:
                                k.wt(v, f"m0ofree_{h}_{g-1}" if g > 0 else f"m0ofree_{h-1}_3")
                            ins = v.tensor_copy(OS[:], OT[:])
                            k.bump(ins, "sV", mark=f"m0oev_{h}_{g}")
                        acc[0] += NU / (HEADS * NT)
                        ffn_units(min(NU, int(acc[0])))
                    ptevac(v, "sV", h, NT - 1, None)
                    k.wt(v, f"m0pv_{h}_3")
                    k.wt(v, f"m0ofree_{h}_2")
                    ins = v.tensor_copy(OS[:], OT[:])
                    k.bump(ins, "sV", mark=f"m0oev_{h}_3")
                    k.marks[f"m0hdone_{h}"] = k.marks[f"m0oev_{h}_3"]
                ffn_units(NU)

            def scaf(sc):
                un = [0]
                acc = [0.0]

                def ffn_units(nmax):
                    while un[0] < nmax:
                        n = un[0]
                        fc, tt = divmod(n, NT)
                        k.wt(sc, f"f0mm_{fc}_{tt}")
                        ins = sc.activation(RL[:], X1, AF.Relu)
                        k.bump(ins, "sS", mark=f"f0rl_{fc}_{tt}")
                        k.wt(sc, f"f0gt_{fc}_{tt}")
                        if n >= 1:
                            pf, pt2 = divmod(n - 1, NT)
                            k.wt(sc, f"f0ssacc_{pf}_{pt2}")
                        ins = sc.activation(DUMP[:], GT[:], AF.Square, accum_out=ssp)
                        k.bump(ins, "sS", mark=f"f0ss_{fc}_{tt}")
                        un[0] += 1

                for h in range(HEADS):
                    for qt in range(NT):
                        nch = nchf(qt)
                        k.wt(sc, f"m0neg_{h}_{qt}")
                        if qt > 1:
                            k.wt(sc, f"m0tp_{h}_{qt-2}")
                        elif h > 0:
                            k.wt(sc, f"m0tp_{h-1}_{NT-2+qt}")
                        ins = sc.activation(PSBs[qt % 2][:, 0:nch * 512], SC[:, 0:nch * 512], AF.Exp,
                                            bias=negm_b[qt % 2], scale=1.0, accum_out=za)
                        k.bump(ins, "sS", mark=f"m0exp_{h}_{qt}")
                        acc[0] += NU / (HEADS * NT)
                        ffn_units(min(NU, int(acc[0])))
                ffn_units(NU)

            def gpsf(g):
                for fc in range(NFC):
                    if fc >= 2:
                        k.wt(g, f"f0adone_{fc-2}")
                    g.dma_start(out=k.A[:, (fc % 2) * 8192:(fc % 2) * 8192 + 8192]
                                .rearrange("p (h t) -> p h t", h=NH),
                                in_=k.w13[:, :, fc * 512:(fc + 1) * 512].rearrange("h p t -> p h t")
                                ).then_inc(k.sem(f"dw{fc%2}"), 16)
                    g.dma_start(out=k.A[:, 16384 + (fc % 2) * 8192:16384 + (fc % 2) * 8192 + 8192]
                                .rearrange("p (h t) -> p h t", h=NH),
                                in_=k.w13[:, :, F0 + fc * 512:F0 + fc * 512 + 512].rearrange("h p t -> p h t")
                                ).then_inc(k.sem(f"dw{fc%2}"), 16)
                    k.bump(None, f"dw{fc%2}", 32, mark=f"f0w_{fc}")

            self.phase("af0", {"tensor": tenf, "vector": vecf, "scalar": scaf,
                               "sync": syncf, "gpsimd": gpsf})

    def phase_af1(self):
        k = self
        QH = k.Cp[:, 0:2048]
        V1 = k.Cp[:, 2048:4096]
        PSBs = [k.Cp[:, 4096:6144], k.Cp[:, 6144:8192]]
        OS = k.Cp[:, 18432:18944]
        K1 = k.Cp[:, 8192:10240]
        PT = k.Cp[:, 10240:18432]
        HF = k.A[:, 16384:18432]
        GG = k.P32[:, 0:512]
        UU = k.P32[:, 512:1024]
        TT = k.P32[:, 1024:1536]
        ZZ = k.P32[:, 1536:2048]
        negm_b = [k.st1[:, 0:1], k.st1[:, 6:7]]
        za = k.st1[:, 1:2]; rz = k.st1[:, 4:5]
        WGU = [k.A[:, i * 4096:(i + 1) * 4096] for i in range(4)]
        NU = NFT * 4

        with self.nc.psum_tensor("SCm1", [128, 2048], F32) as SC, \
             self.nc.psum_tensor("PPm1", [128, 1024], BF16) as PP, \
             self.nc.psum_tensor("OTm1", [128, 512], F32) as OT, \
             self.nc.psum_tensor("GPm1", [128, 512], F32) as GP, \
             self.nc.psum_tensor("UPm1", [128, 512], F32) as UP:

            def transp(t, h, qt, ppq):
                PSB = PSBs[qt % 2]
                k.wt(t, f"m1pn_{h}_{qt}")
                for b in range(2):
                    if ppq:
                        k.wt(t, ppq[0])
                    for j in range(8):
                        ins = t.transpose(PP[:, j * 128:(j + 1) * 128],
                                          PSB[:, (b * 8 + j) * 128:(b * 8 + j) * 128 + 128], k.ident[:])
                    k.bump(ins, "sT", mark=f"m1tpb_{h}_{qt}_{b}")
                    ppq[:] = [f"m1pteV_{h}_{qt}_{b}"]
                k.marks[f"m1tp_{h}_{qt}"] = k.marks[f"m1tpb_{h}_{qt}_1"]

            def ptevac(eng, semn, h, qt, parity):
                for b in range(2):
                    if qt % 4 == 0 and b == 0:
                        if qt > 0:
                            k.wt(eng, f"m1pv_{h}_{qt//4 - 1}")
                        elif h > 0:
                            k.wt(eng, f"m1pv_{h-1}_3")
                    k.wt(eng, f"m1tpb_{h}_{qt}_{b}")
                    dst = PT.rearrange("p (n t) -> p n t", n=16)[:, b * 8:b * 8 + 8,
                                                                 (qt % 4) * 128:(qt % 4) * 128 + 128]
                    src = PP[:].rearrange("p (n t) -> p n t", n=8)
                    ins = eng.copy(dst, src) if semn == "sS" else eng.tensor_copy(dst, src)
                    k.bump(ins, semn, mark=f"m1pteV_{h}_{qt}_{b}")

            def pv(t, h, g):
                for qq in range(4 * g, 4 * g + 4):
                    for b in range(2):
                        k.wt(t, f"m1pteV_{h}_{qq}_{b}")
                if not (h == 0 and g == 0):
                    k.wt(t, f"m1oev_{h}_{g-1}" if g > 0 else f"m1oev_{h-1}_3")
                for kt in range(16):
                    ins = t.matmul(OT[:], V1[:, kt * 128:(kt + 1) * 128],
                                   PT[:, kt * 512:(kt + 1) * 512],
                                   start=(kt == 0), stop=(kt == 15))
                k.bump(ins, "sT", mark=f"m1pv_{h}_{g}")

            def syncf(sy):
                sy.dma_start(out=k.Bp[:].rearrange("p (h t) -> p h t", h=NH),
                             in_=k.xn_h.rearrange("h p t -> p h t")).then_inc(k.sem("ld"), 16)
                k.cnt["ld"] = k.cnt.get("ld", 0) + 16
                sy.dma_start(out=V1[:].rearrange("p (n t) -> p n t", n=16),
                             in_=k.v_hb.rearrange("n p t -> p n t")[:, :, 0:128]
                             ).then_inc(k.sem("ld"), 16)
                k.cnt["ld"] = k.cnt.get("ld", 0) + 16
                k.wtv(sy, "ld", k.cnt.get("ld", 0))
                k.bump(sy.nop(), "sY", mark="m1_pre")
                un = [0]
                acc = [0.0]

                def ffn_units(nmax):
                    while un[0] < nmax:
                        n = un[0]
                        ft, tc = divmod(n, 4)
                        if tc == 3:
                            k.wt(sy, f"f1h_{ft}_{tc}")
                            sy.dma_start(out=k.hf_h[ft], in_=HF[:]).then_inc(k.sem("st"), 16)
                            k.bump(None, "st", 16)
                            k.wtv(sy, "st", k.cnt.get("st", 0))
                            k.bump(sy.nop(), "sY", mark=f"f1hfree_{ft}")
                        un[0] += 1

                for h in range(HEADS):
                    if h >= 1:
                        k.wt(sy, f"m1hdone_{h-1}")
                    sy.dma_start(out=QH[:], in_=k.q_hb[h]).then_inc(k.sem("dq0"), 16)
                    k.bump(None, "dq0", 16, mark=f"m1qv_{h}")
                    for g in range(4):
                        acc[0] += NU / (HEADS * 4)
                        ffn_units(min(NU, int(acc[0])))
                        k.wt(sy, f"m1oev_{h}_{g}")
                        sy.dma_start(out=k.o_hb[h][:, g * 512:(g + 1) * 512], in_=OS[:]
                                     ).then_inc(k.sem("st"), 16)
                        k.bump(None, "st", 16)
                        k.wtv(sy, "st", k.cnt.get("st", 0))
                        k.bump(sy.nop(), "sY", mark=f"m1ofree_{h}_{g}")
                ffn_units(NU)
                k.bump(sy.nop(), "sY", mark="af1_done")

            def tenf(t):
                k.wt(t, "m1_pre")
                ppq = []
                un = [0]
                acc = [0.0]

                def ffn_units(nmax):
                    while un[0] < nmax:
                        n = un[0]
                        ft, tc = divmod(n, 4)
                        k.wt(t, f"f1w_{ft}")
                        if n >= 1:
                            pf, pc = divmod(n - 1, 4)
                            k.wt(t, f"f1gt_{pf}_{pc}")
                            k.wt(t, f"f1ue_{pf}_{pc}")
                        for ht in range(NH):
                            t.matmul(GP[:], WGU[ft % 4][:, ht * 128:(ht + 1) * 128],
                                     k.Bp[:, ht * 2048 + tc * 512: ht * 2048 + tc * 512 + 512],
                                     start=(ht == 0), stop=(ht == NH - 1))
                        for ht in range(NH):
                            ins = t.matmul(UP[:], WGU[ft % 4][:, 2048 + ht * 128: 2048 + ht * 128 + 128],
                                           k.Bp[:, ht * 2048 + tc * 512: ht * 2048 + tc * 512 + 512],
                                           start=(ht == 0), stop=(ht == NH - 1))
                        k.bump(ins, "sT", mark=f"f1mm_{ft}_{tc}")
                        if tc == 3:
                            k.marks[f"f1ftdone_{ft}"] = k.marks[f"f1mm_{ft}_{tc}"]
                        un[0] += 1

                prev = [0]
                for h in range(HEADS):
                    for qt in range(NT):
                        ffn_units(min(NU, prev[0]))
                        if qt == 0:
                            k.wt(t, f"m1qv_{h}")
                        if qt > 0:
                            k.wt(t, f"m1exp_{h}_{qt-1}")
                        elif h > 0:
                            k.wt(t, f"m1exp_{h-1}_{NT-1}")
                        for c in range(4):
                            ins = t.matmul(SC[:, c * 512:(c + 1) * 512],
                                           QH[:, qt * 128:(qt + 1) * 128],
                                           K1[:, c * 512:(c + 1) * 512],
                                           start=True, stop=True)
                        k.bump(ins, "sT", mark=f"m1sc_{h}_{qt}")
                        if qt > 0:
                            transp(t, h, qt - 1, ppq)
                            if (qt - 1) % 4 == 3:
                                pv(t, h, (qt - 1) // 4)
                        acc[0] += NU / (HEADS * NT)
                        prev[0] = int(acc[0])
                    ffn_units(min(NU, prev[0]))
                    transp(t, h, NT - 1, ppq)
                    pv(t, h, 3)
                ffn_units(NU)

            def vecf(v):
                un = [0]
                acc = [0.0]

                def ffn_units(nmax):
                    while un[0] < nmax:
                        n = un[0]
                        ft, tc = divmod(n, 4)
                        k.wt(v, f"f1mm_{ft}_{tc}")
                        ins = v.tensor_scalar_mul(UU[:], UP[:], k.us_s[:, ft:ft + 1])
                        k.bump(ins, "sV", mark=f"f1ue_{ft}_{tc}")
                        k.wt(v, f"f1gt_{ft}_{tc}")
                        ins = v.scalar_tensor_tensor(GG[:], TT[:], 1.0, ZZ[:], OP.add, OP.mult)
                        k.bump(ins, "sV", mark=f"f1gg_{ft}_{tc}")
                        if tc == 0 and ft >= 1:
                            k.wt(v, f"f1hfree_{ft-1}")
                        ins = v.tensor_tensor(HF[:, tc * 512:(tc + 1) * 512], GG[:], UU[:], OP.mult)
                        k.bump(ins, "sV", mark=f"f1h_{ft}_{tc}")
                        un[0] += 1

                for h in range(HEADS):
                    for qt in range(NT):
                        k.wt(v, f"m1sc_{h}_{qt}")
                        ins = v.tensor_reduce(negm_b[qt % 2], SC[:], AX, OP.max, negate=True)
                        k.bump(ins, "sV", mark=f"m1neg_{h}_{qt}")
                        k.wt(v, f"m1exp_{h}_{qt}")
                        ins = v.reciprocal(rz, za)
                        k.bump(ins, "sV", mark=f"m1rz_{h}_{qt}")
                        k.wt(v, f"m1rz_{h}_{qt}")
                        PSB = PSBs[qt % 2]
                        ins = v.tensor_scalar_mul(PSB[:], PSB[:], rz)
                        k.bump(ins, "sV", mark=f"m1pn_{h}_{qt}")
                        if qt > 0:
                            ptevac(v, "sV", h, qt - 1, None)
                        if qt % 4 == 1 and qt >= 5:
                            g = qt // 4 - 1
                            k.wt(v, f"m1pv_{h}_{g}")
                            if g > 0 or h > 0:
                                k.wt(v, f"m1ofree_{h}_{g-1}" if g > 0 else f"m1ofree_{h-1}_3")
                            ins = v.tensor_copy(OS[:], OT[:])
                            k.bump(ins, "sV", mark=f"m1oev_{h}_{g}")
                        acc[0] += NU / (HEADS * NT)
                        ffn_units(min(NU, int(acc[0])))
                    ptevac(v, "sV", h, NT - 1, None)
                    k.wt(v, f"m1pv_{h}_3")
                    k.wt(v, f"m1ofree_{h}_2")
                    ins = v.tensor_copy(OS[:], OT[:])
                    k.bump(ins, "sV", mark=f"m1oev_{h}_3")
                    k.marks[f"m1hdone_{h}"] = k.marks[f"m1oev_{h}_3"]
                ffn_units(NU)

            def scaf(sc):
                un = [0]
                acc = [0.0]
                k.wt(sc, "gsh_ready")

                def ffn_units(nmax):
                    while un[0] < nmax:
                        n = un[0]
                        ft, tc = divmod(n, 4)
                        k.wt(sc, f"f1mm_{ft}_{tc}")
                        if n >= 1:
                            pf, pc = divmod(n - 1, 4)
                            k.wt(sc, f"f1gg_{pf}_{pc}")
                        ins = sc.activation(TT[:], GP[:], AF.Tanh, scale=k.gsh_s[:, ft:ft + 1])
                        k.bump(ins, "sS")
                        ins = sc.activation(ZZ[:], GP[:], AF.Copy, scale=k.gsh_s[:, ft:ft + 1])
                        k.bump(ins, "sS", mark=f"f1gt_{ft}_{tc}")
                        un[0] += 1

                for h in range(HEADS):
                    for qt in range(NT):
                        k.wt(sc, f"m1neg_{h}_{qt}")
                        if qt > 1:
                            k.wt(sc, f"m1tp_{h}_{qt-2}")
                        elif h > 0:
                            k.wt(sc, f"m1tp_{h-1}_{NT-2+qt}")
                        ins = sc.activation(PSBs[qt % 2][:], SC[:], AF.Exp,
                                            bias=negm_b[qt % 2], scale=1.0, accum_out=za)
                        k.bump(ins, "sS", mark=f"m1exp_{h}_{qt}")
                        acc[0] += NU / (HEADS * NT)
                        ffn_units(min(NU, int(acc[0])))
                ffn_units(NU)

            def gpsf(g):
                for ft in range(NFT):
                    if ft >= 4:
                        k.wt(g, f"f1ftdone_{ft-4}")
                    g.dma_start(out=WGU[ft % 4][:], in_=k.wgu1[ft]).then_inc(k.sem(f"dw{ft%4}"), 16)
                    k.bump(None, f"dw{ft%4}", 16, mark=f"f1w_{ft}")

            self.phase("af1", {"tensor": tenf, "vecf": None} if False else
                       {"tensor": tenf, "vector": vecf, "scalar": scaf,
                        "sync": syncf, "gpsimd": gpsf})

    def phase_wo(self, L):
        k = self
        wo = k.wo0 if L == 0 else k.wo1
        SST = k.Cp[:, 0:2048]
        with self.nc.psum_tensor(f"WOP{L}", [128, 2048], F32) as PB:
            def gpsf(g):
                for hd in range(NH):
                    g.dma_start(out=k.A[:, hd * 2048:(hd + 1) * 2048], in_=wo[hd]
                                ).then_inc(k.sem("cw"), 16)
                    k.cnt["cw"] = k.cnt.get("cw", 0) + 16
                k.wtv(g, "cw", k.cnt.get("cw", 0))
                k.bump(g.nop(), "sGc", mark=f"wo{L}_w")

            def syncf(sy):
                for hd in range(HEADS):
                    sy.dma_start(out=k.Bp[:, hd * 2048:(hd + 1) * 2048], in_=k.o_hb[hd]
                                 ).then_inc(k.sem("ld"), 16)
                    k.cnt["ld"] = k.cnt.get("ld", 0) + 16
                k.wtv(sy, "ld", k.cnt.get("ld", 0))
                k.bump(sy.nop(), "sY", mark=f"wo{L}_o")
                for ts in range(NT):
                    k.wt(sy, f"wo{L}ev_{ts*4+3}")
                    sy.dma_start(out=k.s_hb[ts], in_=SST[:]).then_inc(k.sem("st"), 16)
                    k.bump(None, "st", 16)
                    k.wtv(sy, "st", k.cnt.get("st", 0))
                    k.bump(sy.nop(), "sY", mark=f"wo{L}sfree_{ts}")
                k.bump(sy.nop(), "sY", mark=f"wo{L}_done")

            def tenf(t):
                k.wt(t, f"wo{L}_w")
                k.wt(t, f"wo{L}_o")
                for ts in range(NT):
                    for ec in range(4):
                        idx = ts * 4 + ec
                        if idx >= 4:
                            k.wt(t, f"wo{L}ev_{idx-4}")
                        pq = PB[:, ec * 512:(ec + 1) * 512]
                        for hd in range(NH):
                            ins = t.matmul(pq, k.Bp[:, hd * 2048 + ts * 128: hd * 2048 + ts * 128 + 128],
                                           k.A[:, hd * 2048 + ec * 512: hd * 2048 + ec * 512 + 512],
                                           start=(hd == 0), stop=(hd == NH - 1))
                        k.bump(ins, "sT", mark=f"wo{L}mm_{idx}")

            def vecf(v):
                for ts in range(NT):
                    for ec in range(4):
                        idx = ts * 4 + ec
                        k.wt(v, f"wo{L}mm_{idx}")
                        if ts > 0 and ec == 0:
                            k.wt(v, f"wo{L}sfree_{ts-1}")
                        ins = v.tensor_copy(SST[:, ec * 512:(ec + 1) * 512], PB[:, ec * 512:(ec + 1) * 512])
                        k.bump(ins, "sV", mark=f"wo{L}ev_{idx}")

            self.phase(f"wo{L}", {"tensor": tenf, "vector": vecf, "sync": syncf, "gpsimd": gpsf})

    def phase_ffn0A(self):
        k = self
        W1S = [k.A[:, 0:8192], k.A[:, 8192:16384]]
        W3S = [k.A[:, 16384:24576], k.A[:, 24576:32768]]
        NWB = k.Cp[:, 0:5632]
        GNS = [k.Cp[:, 5632:6144], k.Cp[:, 6144:6656]]
        DUMP = k.Cp[:, 6656:7168]
        RL = k.P32[:, 0:512]
        M1 = k.P32[:, 512:1024]
        GT = k.P32[:, 1024:1536]
        ssp = k.st1[:, 0:1]; mxp = k.st1[:, 1:2]

        with self.nc.psum_tensor("F0P", [128, 2048], F32) as PB:
            X1 = [PB[:, 0:512], PB[:, 512:1024]]
            X3 = [PB[:, 1024:1536], PB[:, 1536:2048]]

            def gpsf(g):
                g.dma_start(out=NWB[:], in_=k.nwbi).then_inc(k.sem("cw"), 16)
                k.cnt["cw"] = k.cnt.get("cw", 0) + 16
                k.wtv(g, "cw", k.cnt.get("cw", 0))
                k.bump(g.nop(), "sGc", mark="f0_nwb")
                for fc in range(NFC):
                    if fc >= 2:
                        k.wt(g, f"f0adone_{fc-2}")
                    g.dma_start(out=W1S[fc % 2][:].rearrange("p (h t) -> p h t", h=NH),
                                in_=k.w13[:, :, fc * 512:(fc + 1) * 512].rearrange("h p t -> p h t")
                                ).then_inc(k.sem(f"dw{fc%2}"), 16)
                    g.dma_start(out=W3S[fc % 2][:].rearrange("p (h t) -> p h t", h=NH),
                                in_=k.w13[:, :, F0 + fc * 512:F0 + fc * 512 + 512].rearrange("h p t -> p h t")
                                ).then_inc(k.sem(f"dw{fc%2}"), 16)
                    k.bump(None, f"dw{fc%2}", 32, mark=f"f0w_{fc}")

            def syncf(sy):
                sy.dma_start(out=k.Bp[:].rearrange("p (h t) -> p h t", h=NH),
                             in_=k.codes_h.rearrange("h p t -> p h t")).then_inc(k.sem("ld"), 16)
                k.cnt["ld"] = k.cnt.get("ld", 0) + 16
                k.wtv(sy, "ld", k.cnt.get("ld", 0))
                k.bump(sy.nop(), "sY", mark="f0_codes")
                for fc in range(NFC):
                    for tt in range(NT):
                        n = fc * NT + tt
                        k.wt(sy, f"f0gn_{fc}_{tt}")
                        sy.dma_start(out=k.gn_hb[tt * 128:(tt + 1) * 128, fc * 512:(fc + 1) * 512],
                                     in_=GNS[n % 2][:]).then_inc(k.sem("st"), 16)
                        k.bump(None, "st", 16)
                        k.wtv(sy, "st", k.cnt.get("st", 0))
                        k.bump(sy.nop(), "sY", mark=f"f0gnfree_{fc}_{tt}")
                k.bump(sy.nop(), "sY", mark="f0A_done")

            def tenf(t):
                k.wt(t, "f0_codes")
                for fc in range(NFC):
                    k.wt(t, f"f0w_{fc}")
                    for tt in range(NT):
                        pp = tt % 2
                        n = fc * NT + tt
                        if n >= 2:
                            pf, pt = divmod(n - 2, NT)
                            k.wt(t, f"f0gt_{pf}_{pt}")
                        for ht in range(NH):
                            t.matmul(X1[pp], k.Bp[:, ht * 2048 + tt * 128: ht * 2048 + tt * 128 + 128],
                                     W1S[fc % 2][:, ht * 512:(ht + 1) * 512],
                                     start=(ht == 0), stop=(ht == NH - 1))
                        for ht in range(NH):
                            ins = t.matmul(X3[pp], k.Bp[:, ht * 2048 + tt * 128: ht * 2048 + tt * 128 + 128],
                                           W3S[fc % 2][:, ht * 512:(ht + 1) * 512],
                                           start=(ht == 0), stop=(ht == NH - 1))
                        k.bump(ins, "sT", mark=f"f0mm_{fc}_{tt}")
                    k.marks[f"f0adone_{fc}"] = k.marks[f"f0mm_{fc}_{NT-1}"]

            def scaf(sc):
                for fc in range(NFC):
                    for tt in range(NT):
                        pp = tt % 2
                        n = fc * NT + tt
                        k.wt(sc, f"f0mm_{fc}_{tt}")
                        ins = sc.activation(RL[:], X1[pp], AF.Relu)
                        k.bump(ins, "sS", mark=f"f0rl_{fc}_{tt}")
                        k.wt(sc, f"f0gt_{fc}_{tt}")
                        if n >= 1:
                            pf, pt = divmod(n - 1, NT)
                            k.wt(sc, f"f0ssacc_{pf}_{pt}")
                        ins = sc.activation(DUMP[:], GT[:], AF.Square, accum_out=ssp)
                        k.bump(ins, "sS", mark=f"f0ss_{fc}_{tt}")

            def vecf(v):
                k.wt(v, "f0_nwb")
                for fc in range(NFC):
                    for tt in range(NT):
                        pp = tt % 2
                        n = fc * NT + tt
                        k.wt(v, f"f0rl_{fc}_{tt}")
                        v.tensor_tensor(M1[:], X1[pp], RL[:], OP.mult)
                        ins = v.scalar_tensor_tensor(GT[:], M1[:], k.sinv3[:, tt:tt + 1], X3[pp],
                                                     OP.mult, OP.mult)
                        k.bump(ins, "sV", mark=f"f0gt_{fc}_{tt}")
                        if n >= 2:
                            pf, pt = divmod(n - 2, NT)
                            k.wt(v, f"f0gnfree_{pf}_{pt}")
                        ins = v.tensor_tensor(GNS[n % 2][:], GT[:], NWB[:, fc * 512:(fc + 1) * 512], OP.mult)
                        k.bump(ins, "sV", mark=f"f0gn_{fc}_{tt}")
                        ins = v.tensor_reduce(mxp, GNS[n % 2][:], AX, OP.max, apply_absolute_value=True)
                        k.bump(ins, "sV", mark=f"f0mxr_{fc}_{tt}")
                        k.wt(v, f"f0ss_{fc}_{tt}")
                        k.wt(v, f"f0mxr_{fc}_{tt}")
                        if fc == 0:
                            v.tensor_copy(k.ssqB[:, tt:tt + 1], ssp)
                            ins = v.tensor_copy(k.mxgB[:, tt:tt + 1], mxp)
                        else:
                            v.tensor_tensor(k.ssqB[:, tt:tt + 1], k.ssqB[:, tt:tt + 1], ssp, OP.add)
                            ins = v.tensor_tensor(k.mxgB[:, tt:tt + 1], k.mxgB[:, tt:tt + 1], mxp, OP.max)
                        k.bump(ins, "sV", mark=f"f0ssacc_{fc}_{tt}")

            self.phase("ffn0A", {"tensor": tenf, "vector": vecf, "scalar": scaf,
                                 "sync": syncf, "gpsimd": gpsf})

    def phase_ffn0S(self):
        k = self
        SD16 = k.P32[:, 0:16]
        RSQ = k.P32[:, 16:32]
        CLP = k.P32[:, 32:48]
        REC = k.P32[:, 48:64]

        def scaf(sc):
            k.wt(sc, "f0A_vecdone")
            ins = sc.activation(SD16[:], k.ssqB[:], AF.Sqrt, bias=k.epsb[:], scale=1.0 / F0)
            k.bump(ins, "sS", mark="f0_sd")

        def vecf(v):
            k.bump(v.nop(), "sV", mark="f0A_vecdone")
            k.wt(v, "f0_sd")
            ins = v.reciprocal(RSQ[:], SD16[:])
            k.bump(ins, "sV", mark="f0_rsq")
            k.wt(v, "f0_rsq")
            v.tensor_tensor(CLP[:], RSQ[:], k.mxgB[:], OP.mult)
            v.tensor_single_scalar(CLP[:], CLP[:], 1e-5, OP.max)
            ins = v.reciprocal(REC[:], CLP[:])
            k.bump(ins, "sV", mark="f0_rec")
            k.wt(v, "f0_rec")
            v.tensor_single_scalar(k.i2v[:], CLP[:], 1.0 / 127.0, OP.mult)
            v.tensor_tensor(k.aa8[:], RSQ[:], REC[:], OP.mult)
            ins = v.tensor_single_scalar(k.aa8[:], k.aa8[:], 127.0, OP.mult)
            k.bump(ins, "sV", mark="f0_stats")

        self.phase("ffn0S", {"scalar": scaf, "vector": vecf})

    def phase_ffn0B(self):
        k = self
        iq2 = [k.Bp[:, ft * 1024:(ft + 1) * 1024] for ft in range(32)] + \
              [k.A[:, 20480 + (ft - 32) * 1024: 20480 + (ft - 31) * 1024] for ft in range(32, NFT)]
        GNL = [k.Cp[:, 0:512], k.Cp[:, 512:1024]]
        IQS = k.Cp[:, 1024:1536]
        GF = k.Cp[:, 1536:2560]
        GS = k.Cp[:, 2560:3584]
        W2L = [k.A[:, 0:5632], k.A[:, 5632:11264]]
        YY = k.P32[:, 0:512]
        FF = k.P32[:, 512:1024]
        CMP = k.P32[:, 1024:1536]

        with self.nc.psum_tensor("F0D", [128, 2048], F32) as DN, \
             self.nc.psum_tensor("F0T", [128, 2048], BF16) as TP, \
             self.nc.psum_tensor("F0T2", [128, 1024], BF16) as TP2:
            tph = [TP[:, 0:1024], TP[:, 1024:2048]]

            def syncf(sy):
                for tb in range(2):
                    for tt8 in range(8):
                        for fc in range(NFC):
                            n = (tb * 8 + tt8) * NFC + fc
                            if n >= 2:
                                k.wt(sy, f"f0y_{n-2}")
                            tt = tb * 8 + tt8
                            sy.dma_start(out=GNL[n % 2][:],
                                         in_=k.gn_hb[tt * 128:(tt + 1) * 128, fc * 512:(fc + 1) * 512]
                                         ).then_inc(k.sem(f"dx{n%2}"), 16)
                            k.bump(None, f"dx{n%2}", 16, mark=f"f0gnl_{n}")
                    for dblk in range(NH):
                        k.wt(sy, f"f0gse_{tb}_{dblk}")
                        for b in range(8):
                            tsub = tb * 8 + b
                            sy.dma_start(out=k.g_hb[tsub][:, dblk * 128:(dblk + 1) * 128],
                                         in_=GS[:, b * 128:(b + 1) * 128]).then_inc(k.sem("st"), 16)
                            k.bump(None, "st", 16)
                        k.wtv(sy, "st", k.cnt.get("st", 0))
                        k.bump(sy.nop(), "sY", mark=f"f0gsfree_{tb}_{dblk}")
                k.wtv(sy, "st", k.cnt.get("st", 0))
                k.bump(sy.nop(), "sY", mark="f0B_done")

            def gpsf(g):
                for tb in range(2):
                    for dblk in range(NH):
                        n = tb * NH + dblk
                        if n >= 2:
                            pn = n - 2
                            k.wt(g, f"f0dn_{pn//NH}_{pn%NH}")
                        g.dma_start(out=W2L[n % 2][:], in_=k.w2p[dblk]).then_inc(k.sem(f"dw{n%2}"), 16)
                        k.bump(None, f"dw{n%2}", 16, mark=f"f0w2_{tb}_{dblk}")

            def vecf(v):
                for tb in range(2):
                    for tt8 in range(8):
                        tt = tb * 8 + tt8
                        for fc in range(NFC):
                            n = (tb * 8 + tt8) * NFC + fc
                            k.wt(v, f"f0gnl_{n}")
                            v.tensor_scalar(YY[:], GNL[n % 2][:], k.aa8[:, tt:tt + 1], 0.5,
                                            OP.mult, OP.add)
                            ins = v.tensor_copy(k.PI[:, 0:512], YY[:])
                            k.bump(ins, "sV", mark=f"f0y_{n}")
                            v.tensor_copy(FF[:], k.PI[:, 0:512])
                            v.tensor_tensor(CMP[:], FF[:], YY[:], OP.is_gt)
                            v.tensor_tensor(FF[:], FF[:], CMP[:], OP.subtract)
                            if n >= 1:
                                k.wt(v, f"f0tiq_{n-1}")
                            ins = v.tensor_scalar_mul(IQS[:], FF[:], k.i2v[:, tt:tt + 1])
                            k.bump(ins, "sV", mark=f"f0iq_{n}")

            def tenf(t):
                for tb in range(2):
                    for tt8 in range(8):
                        for fc in range(NFC):
                            n = (tb * 8 + tt8) * NFC + fc
                            k.wt(t, f"f0iq_{n}")
                            if n >= 2:
                                k.wt(t, f"f0iqe_{n-2}")
                            for b in range(4):
                                ins = t.transpose(tph[n % 2][:, b * 128:(b + 1) * 128],
                                                  IQS[:, b * 128:(b + 1) * 128], k.ident[:])
                            k.bump(ins, "sT", mark=f"f0tiq_{n}")
                    for dblk in range(NH):
                        nn = tb * NH + dblk
                        k.wt(t, f"f0w2_{tb}_{dblk}")
                        k.wt(t, f"f0iqe_{(tb*8+7)*NFC + NFC - 1}")
                        if nn >= 2:
                            pn = nn - 2
                            k.wt(t, f"f0sil_{pn//NH}_{pn%NH}")
                        half = nn % 2
                        for ft in range(NFT):
                            for tc in range(2):
                                ins = t.matmul(DN[:, half * 1024 + tc * 512: half * 1024 + tc * 512 + 512],
                                               W2L[nn % 2][:, ft * 128:(ft + 1) * 128],
                                               iq2[ft][:, tc * 512:(tc + 1) * 512],
                                               start=(ft == 0), stop=(ft == NFT - 1))
                        k.bump(ins, "sT", mark=f"f0dn_{tb}_{dblk}")
                        k.wt(t, f"f0sil_{tb}_{dblk}")
                        if nn >= 1:
                            pn = nn - 1
                            k.wt(t, f"f0gte_{pn//NH}_{pn%NH}")
                        for b in range(8):
                            ins = t.transpose(TP2[:, (b % 8) * 128:(b % 8) * 128 + 128],
                                              GF[:, b * 128:(b + 1) * 128], k.ident[:])
                        k.bump(ins, "sT", mark=f"f0tg_{tb}_{dblk}")

            def scaf(sc):
                for tb in range(2):
                    for tt8 in range(8):
                        tt = tb * 8 + tt8
                        for fc in range(NFC):
                            n = (tb * 8 + tt8) * NFC + fc
                            k.wt(sc, f"f0tiq_{n}")
                            if tb == 1:
                                k.wt(sc, "f0dn_0_15")
                            ins = None
                            for b in range(4):
                                ft = fc * 4 + b
                                ins = sc.copy(iq2[ft][:, tt8 * 128:(tt8 + 1) * 128],
                                              tph[n % 2][:, b * 128:(b + 1) * 128])
                            k.bump(ins, "sS", mark=f"f0iqe_{n}")
                    for dblk in range(NH):
                        nn = tb * NH + dblk
                        k.wt(sc, f"f0dn_{tb}_{dblk}")
                        half = nn % 2
                        ins = sc.activation(GF[:], DN[:, half * 1024:half * 1024 + 1024], AF.Silu)
                        k.bump(ins, "sS", mark=f"f0sil_{tb}_{dblk}")
                        k.wt(sc, f"f0tg_{tb}_{dblk}")
                        if nn >= 1:
                            pn = nn - 1
                            k.wt(sc, f"f0gsfree_{pn//NH}_{pn%NH}")
                        ins = sc.copy(GS[:], TP2[:])
                        k.bump(ins, "sS", mark=f"f0gte_{tb}_{dblk}")
                        k.marks[f"f0gse_{tb}_{dblk}"] = k.marks[f"f0gte_{tb}_{dblk}"]

            self.phase("ffn0B", {"tensor": tenf, "vector": vecf, "scalar": scaf,
                                 "sync": syncf, "gpsimd": gpsf})

    def phase_ffn1gu(self):
        k = self
        WGU = [k.Cp[:, i * 4096:(i + 1) * 4096] for i in range(4)]
        WG = [WGU[i][:, 0:2048] for i in range(4)]
        WU = [WGU[i][:, 2048:4096] for i in range(4)]
        GG = k.P32[:, 0:2048]
        UU = k.P32[:, 2048:4096]
        HF = k.Cp[:, 16384:18432]

        with self.nc.psum_tensor("F1G", [128, 2048], F32) as GP, \
             self.nc.psum_tensor("F1U", [128, 2048], F32) as UP:
            def syncf(sy):
                sy.dma_start(out=k.Bp[:].rearrange("p (h t) -> p h t", h=NH),
                             in_=k.xn_h.rearrange("h p t -> p h t")).then_inc(k.sem("ld"), 16)
                k.cnt["ld"] = k.cnt.get("ld", 0) + 16
                k.wtv(sy, "ld", k.cnt.get("ld", 0))
                k.bump(sy.nop(), "sY", mark="f1_xn")
                for ft in range(NFT):
                    k.wt(sy, f"f1h_{ft}")
                    sy.dma_start(out=k.hf_h[ft], in_=HF[:]).then_inc(k.sem("st"), 16)
                    k.bump(None, "st", 16)
                    k.wtv(sy, "st", k.cnt.get("st", 0))
                    k.bump(sy.nop(), "sY", mark=f"f1hfree_{ft}")
                k.bump(sy.nop(), "sY", mark="f1gu_done")

            def gpsf(g):
                for ft in range(NFT):
                    if ft >= 4:
                        k.wt(g, f"f1mm_{ft-4}")
                    g.dma_start(out=WGU[ft % 4][:], in_=k.wgu1[ft]).then_inc(k.sem(f"dw{ft%4}"), 16)
                    k.bump(None, f"dw{ft%4}", 16, mark=f"f1w_{ft}")

            def tenf(t):
                k.wt(t, "f1_xn")
                for ft in range(NFT):
                    k.wt(t, f"f1w_{ft}")
                    if ft >= 1:
                        k.wt(t, f"f1ue_{ft-1}")
                        k.wt(t, f"f1gg_{ft-1}")
                    for ht in range(NH):
                        for tc in range(4):
                            t.matmul(GP[:, tc * 512:(tc + 1) * 512],
                                     WG[ft % 4][:, ht * 128:(ht + 1) * 128],
                                     k.Bp[:, ht * 2048 + tc * 512: ht * 2048 + tc * 512 + 512],
                                     start=(ht == 0), stop=(ht == NH - 1))
                    for ht in range(NH):
                        for tc in range(4):
                            ins = t.matmul(UP[:, tc * 512:(tc + 1) * 512],
                                           WU[ft % 4][:, ht * 128:(ht + 1) * 128],
                                           k.Bp[:, ht * 2048 + tc * 512: ht * 2048 + tc * 512 + 512],
                                           start=(ht == 0), stop=(ht == NH - 1))
                    k.bump(ins, "sT", mark=f"f1mm_{ft}")

            def scaf(sc):
                for ft in range(NFT):
                    k.wt(sc, f"f1mm_{ft}")
                    if ft >= 1:
                        k.wt(sc, f"f1h_{ft-1}")
                    ins = sc.activation(GG[:], GP[:], AF.Silu, scale=k.gs_s[:, ft:ft + 1])
                    k.bump(ins, "sS", mark=f"f1gg_{ft}")

            def vecf(v):
                for ft in range(NFT):
                    k.wt(v, f"f1mm_{ft}")
                    ins = v.tensor_scalar_mul(UU[:], UP[:], k.us_s[:, ft:ft + 1])
                    k.bump(ins, "sV", mark=f"f1ue_{ft}")
                    k.wt(v, f"f1gg_{ft}")
                    if ft >= 1:
                        k.wt(v, f"f1hfree_{ft-1}")
                    ins = v.tensor_tensor(HF[:], GG[:], UU[:], OP.mult)
                    k.bump(ins, "sV", mark=f"f1h_{ft}")

            self.phase("ffn1gu", {"tensor": tenf, "vector": vecf, "scalar": scaf,
                                  "sync": syncf, "gpsimd": gpsf})

    def phase_ffn1dn(self):
        k = self
        hfm = [k.Bp[:, ft * 1024:(ft + 1) * 1024] for ft in range(32)] + \
              [k.A[:, 20480 + (ft - 32) * 1024: 20480 + (ft - 31) * 1024] for ft in range(32, NFT)]
        WD = [k.A[:, 0:5632], k.A[:, 5632:11264]]
        GF = k.Cp[:, 0:1024]
        GS = k.Cp[:, 1024:2048]

        with self.nc.psum_tensor("F1D", [128, 2048], F32) as DN, \
             self.nc.psum_tensor("F1T", [128, 1024], BF16) as TP:
            def syncf(sy):
                for tb in range(2):
                    if tb == 1:
                        k.wt(sy, "f1dn_0_15")
                    for ft in range(NFT):
                        n = tb * NFT + ft
                        sy.dma_start(out=hfm[ft][:], in_=k.hf_h[ft][:, tb * 1024:(tb + 1) * 1024]
                                     ).then_inc(k.sem(f"dx{n%2}"), 16)
                        k.bump(None, f"dx{n%2}", 16)
                    k.wtv(sy, "dx0", k.cnt.get("dx0", 0))
                    k.wtv(sy, "dx1", k.cnt.get("dx1", 0))
                    k.bump(sy.nop(), "sY", mark=f"f1hl_all_{tb}")
                    for dblk in range(NH):
                        k.wt(sy, f"f1gte_{tb}_{dblk}")
                        for b in range(8):
                            tsub = tb * 8 + b
                            sy.dma_start(out=k.g_hb[tsub][:, dblk * 128:(dblk + 1) * 128],
                                         in_=GS[:, b * 128:(b + 1) * 128]).then_inc(k.sem("st"), 16)
                            k.bump(None, "st", 16)
                        k.wtv(sy, "st", k.cnt.get("st", 0))
                        k.bump(sy.nop(), "sY", mark=f"f1gsfree_{tb}_{dblk}")
                k.wtv(sy, "st", k.cnt.get("st", 0))
                k.bump(sy.nop(), "sY", mark="f1dn_done")

            def gpsf(g):
                for tb in range(2):
                    for dblk in range(NH):
                        n = tb * NH + dblk
                        if n >= 2:
                            pn = n - 2
                            k.wt(g, f"f1dn_{pn//NH}_{pn%NH}")
                        g.dma_start(out=WD[n % 2][:], in_=k.wd1[dblk]).then_inc(k.sem(f"dw{n%2}"), 16)
                        k.bump(None, f"dw{n%2}", 16, mark=f"f1wd_{tb}_{dblk}")

            def tenf(t):
                for tb in range(2):
                    k.wt(t, f"f1hl_all_{tb}")
                    for dblk in range(NH):
                        nn = tb * NH + dblk
                        k.wt(t, f"f1wd_{tb}_{dblk}")
                        if nn >= 2:
                            pn = nn - 2
                            k.wt(t, f"f1sil_{pn//NH}_{pn%NH}")
                        half = nn % 2
                        for ft in range(NFT):
                            for tc in range(2):
                                ins = t.matmul(DN[:, half * 1024 + tc * 512: half * 1024 + tc * 512 + 512],
                                               WD[nn % 2][:, ft * 128:(ft + 1) * 128],
                                               hfm[ft][:, tc * 512:(tc + 1) * 512],
                                               start=(ft == 0), stop=(ft == NFT - 1))
                        k.bump(ins, "sT", mark=f"f1dn_{tb}_{dblk}")
                        k.wt(t, f"f1sil_{tb}_{dblk}")
                        if nn >= 1:
                            pn = nn - 1
                            k.wt(t, f"f1gte_{pn//NH}_{pn%NH}")
                        for b in range(8):
                            ins = t.transpose(TP[:, b * 128:(b + 1) * 128],
                                              GF[:, b * 128:(b + 1) * 128], k.ident[:])
                        k.bump(ins, "sT", mark=f"f1tg_{tb}_{dblk}")

            def scaf(sc):
                for tb in range(2):
                    for dblk in range(NH):
                        nn = tb * NH + dblk
                        k.wt(sc, f"f1dn_{tb}_{dblk}")
                        half = nn % 2
                        ins = sc.activation(GF[:], DN[:, half * 1024:half * 1024 + 1024],
                                            AF.Silu, scale=k.ds_s[:, dblk:dblk + 1])
                        k.bump(ins, "sS", mark=f"f1sil_{tb}_{dblk}")
                        k.wt(sc, f"f1tg_{tb}_{dblk}")
                        if nn >= 1:
                            pn = nn - 1
                            k.wt(sc, f"f1gsfree_{pn//NH}_{pn%NH}")
                        ins = sc.copy(GS[:], TP[:])
                        k.bump(ins, "sS", mark=f"f1gte_{tb}_{dblk}")

            self.phase("ffn1dn", {"tensor": tenf, "scalar": scaf, "sync": syncf, "gpsimd": gpsf})

    def phase_final(self):
        k = self
        XF = [k.P32[:, 0:2048], k.P32[:, 2048:4096]]
        SGF = k.P32[:, 4096:6144]
        SGS = [k.Cp[:, 0:2048], k.Cp[:, 2048:4096]]
        GGS = [k.Cp[:, 4096:6144], k.Cp[:, 6144:8192]]

        def syncf(sy):
            for i in range(NT):
                sy.dma_start(out=XF[i % 2][:], in_=k.x1_h[i]).then_inc(k.sem(f"dx{i%2}"), 16)
                sy.dma_start(out=SGS[i % 2][:], in_=k.s_hb[i]).then_inc(k.sem(f"dx{i%2}"), 16)
                sy.dma_start(out=GGS[i % 2][:], in_=k.g_hb[i]).then_inc(k.sem(f"dx{i%2}"), 16)
                k.bump(None, f"dx{i%2}", 48, mark=f"fin_ld_{i}")
                k.wt(sy, f"fin_x2_{i}")
                sy.dma_start(out=k.out[i * 128:(i + 1) * 128, :], in_=XF[i % 2][:]
                             ).then_inc(k.sem("out"), 16)
                k.cnt["out"] = k.cnt.get("out", 0) + 16
                k.wtv(sy, "out", k.cnt.get("out", 0))
            k.wtv(sy, "out", k.cnt.get("out", 0))

        def vecf(v):
            for i in range(NT):
                k.wt(v, f"fin_ld_{i}")
                v.tensor_tensor(SGF[:], SGS[i % 2][:], GGS[i % 2][:], OP.mult)
                ins = v.tensor_tensor(XF[i % 2][:], XF[i % 2][:], SGF[:], OP.add)
                k.bump(ins, "sV", mark=f"fin_x2_{i}")

        self.phase("final", {"vector": vecf, "sync": syncf})

    def phase_flush(self):
        k = self

        def syncf(sy):
            if k.cnt.get("st", 0):
                k.wtv(sy, "st", k.cnt.get("st", 0))

        self.phase("flush", {"sync": syncf})


def _prep_inputs(ins):
    f32 = np.float32
    wqkvT = np.ascontiguousarray(ins["L0_attn_w_qkv"].astype(f32).T)
    woT0 = np.ascontiguousarray(ins["L0_attn_w_o"].astype(f32).T)
    w13T = np.ascontiguousarray(ins["L0_ffn_w13"].astype(f32).T)
    w2T = np.ascontiguousarray(ins["L0_ffn_w2"].astype(f32).T)
    qwT = np.ascontiguousarray(ins["L1_attn_q_w"].astype(f32).T)
    kwT = np.ascontiguousarray(ins["L1_attn_k_w"].astype(f32).T)
    vwT = np.ascontiguousarray(ins["L1_attn_v_w"].astype(f32).T)
    owT = np.ascontiguousarray(ins["L1_attn_o_w"].astype(f32).T)
    gwT = np.ascontiguousarray(ins["L1_ffn_gate_w"].astype(f32).T)
    uwT = np.ascontiguousarray(ins["L1_ffn_up_w"].astype(f32).T)
    dwT = np.ascontiguousarray(ins["L1_ffn_down_w"].astype(f32).T)

    d = {}
    d["wqk0"] = pack_lhsT(wqkvT[:, 0:4096])
    d["wv0"] = pack_lhsT(wqkvT[:, 4096:6144])
    d["wo0"] = pack_rhs(woT0)
    d["w13"] = pack_rhs(w13T)
    d["w2p"] = pack_lhsT(w2T)
    d["nwbi"] = np.ascontiguousarray(
        np.broadcast_to(ins["L0_ffn_norm_w"].astype(f32)[None, :], (128, F0)))
    d["wq1"] = pack_lhsT(qwT)
    d["wk1"] = pack_lhsT(kwT)
    d["wv1"] = pack_lhsT(vwT)
    d["wo1"] = pack_rhs(owT)
    d["wgu1"] = np.ascontiguousarray(
        np.concatenate([pack_lhsT(gwT), pack_lhsT(uwT)], axis=2))
    d["wd1"] = pack_lhsT(dwT)
    d["gsp"] = np.ascontiguousarray(ins["L1_ffn_gate_s"].astype(f32).reshape(NFT, 128).T)
    d["usp"] = np.ascontiguousarray(ins["L1_ffn_up_s"].astype(f32).reshape(NFT, 128).T)
    d["dsp"] = np.ascontiguousarray(ins["L1_ffn_down_s"].astype(f32).reshape(NH, 128).T)
    d["iden"] = np.eye(128, dtype=BF)
    d["mskin"] = make_masks()
    cst = np.zeros((128, 8), np.float32)
    cst[:, 0] = 1e-5
    d["cst"] = cst
    return d


def kernel(**inputs):
    shared = _prep_inputs(inputs)
    x_all = np.ascontiguousarray(inputs["L0_attn_hidden_states"].astype(np.float32))
    kb = KB()
    nc = kb.build()
    in_maps = []
    for b in range(8):
        m = dict(shared)
        m["x0"] = x_all[b]
        in_maps.append(m)
    res = run_bass_kernel_spmd(nc, in_maps, core_ids=list(range(8)))
    return np.stack([r["outp"] for r in res.results], axis=0).astype(np.float32)


# version 17
# speedup vs baseline: 1.1691x; 1.1691x over previous
import numpy as np
import ml_dtypes
from contextlib import ExitStack

import concourse.bass as bass
import concourse.mybir as mybir
from concourse.alu_op_type import AluOpType as OP
from concourse.bass_utils import run_bass_kernel_spmd

F32 = mybir.dt.float32
BF16 = mybir.dt.bfloat16
I32 = mybir.dt.int32
AF = mybir.ActivationFunctionType
AX = mybir.AxisListType.X

S = 2048
D = 2048
NT = 16
NH = 16
HEADS = 16
F0 = 5632
NFT = 44
NFC = 11
SCALE = 1.0 / float(np.sqrt(np.float32(128.0)))
NEGBIG = -1.0e30
BF = ml_dtypes.bfloat16
ENGS = ("tensor", "vector", "scalar", "sync", "gpsimd")
DBG_SKIP = set()


def pack_lhsT(wT):
    H, O = wT.shape
    return np.ascontiguousarray(
        wT.reshape(H // 128, 128, O // 128, 128).transpose(2, 1, 0, 3).reshape(O // 128, 128, H))


def pack_rhs(wT):
    H, F = wT.shape
    return np.ascontiguousarray(wT.reshape(H // 128, 128, F))


def make_masks():
    out = np.zeros((128, 4 * 512), dtype=BF)
    r = np.arange(128)[:, None]
    c = np.arange(512)[None, :]
    for oi in range(4):
        out[:, oi * 512:(oi + 1) * 512] = np.where(c <= oi * 128 + r, 0.0, NEGBIG).astype(BF)
    return out


class _MockIns:
    def then_inc(self, *a, **kw):
        return self


class _MockEng:
    def __getattr__(self, name):
        def f(*a, **kw):
            return _MockIns()
        return f


AUTOSEM = {"vector": "sVa", "scalar": "sSa", "gpsimd": "sGa"}
AUTOMAP = {"sV": "sVa", "sS": "sSa", "sG": "sGa"}
_FENCE_SKIP = {"wait_ge", "dma_start", "dma_start_transpose", "indirect_dma_start",
               "nop", "sem_inc", "sem_clear"}


class _Fence:

    def __init__(self, kb, eng, sname):
        self._kb = kb
        self._eng = eng
        self._sname = sname

    def __getattr__(self, a):
        real = getattr(self._eng, a)
        if a in _FENCE_SKIP or not callable(real):
            return real

        def f(*args, **kw):
            kb = self._kb
            sn = self._sname
            kb.sem(sn)
            self._eng.wait_ge(kb.sem_h[sn], kb.cnt.get(sn, 0))
            ins = real(*args, **kw)
            kb.cnt[sn] = kb.cnt.get(sn, 0) + 1
            if not isinstance(ins, _MockIns):
                ins.then_inc(kb.sem_h[sn], 1)
            return ins
        return f


class KB:
    def __init__(self, dbg_upto=None, dbg=False):
        self.nc = bass.Bass("TRN2", target_bir_lowering=False)
        self.dbg = dbg
        self.dbg_upto = 99 if dbg_upto is None else dbg_upto
        self.ctx = ExitStack()
        self.sem_h = {}
        self.cnt = {}
        self.marks = {}
        self.mock = False

    def sem(self, n):
        if n not in self.sem_h:
            self.sem_h[n] = self.ctx.enter_context(self.nc.semaphore(n))
        self.cnt.setdefault(n, 0)
        return self.sem_h[n]

    def bump(self, ins, n, amt=1, mark=None):
        if n in AUTOMAP:
            sn = AUTOMAP[n]
            self.sem(sn)
            v = self.cnt.get(sn, 0)
            if mark is not None:
                self.marks[mark] = (sn, v)
            return v
        self.sem(n)
        self.cnt[n] += amt
        if mark is not None:
            self.marks[mark] = (n, self.cnt[n])
        if ins is not None and not isinstance(ins, _MockIns):
            ins.then_inc(self.sem_h[n], amt)
        return self.cnt[n]

    def wt(self, eng, mark):
        got = self.marks.get(mark)
        if got is None:
            if self.mock:
                return
            raise KeyError(f"unresolved mark {mark}")
        n, v = got
        eng.wait_ge(self.sem(n), v)

    def wtv(self, eng, n, v):
        eng.wait_ge(self.sem(n), v)

    def phase(self, name, bodies):
        def wrap(e, engobj):
            return _Fence(self, engobj, AUTOSEM[e]) if e in AUTOSEM else engobj

        snap = dict(self.cnt)
        self.mock = True
        for e in ENGS:
            if e in bodies:
                bodies[e](wrap(e, _MockEng()))
        mock_end = dict(self.cnt)
        self.cnt = snap
        self.mock = False
        with self.nc.Block(name, no_gpsimd_drain=False) as blk:
            for e in ENGS:
                if e in bodies:
                    body = bodies[e]
                    getattr(blk, e)(lambda eng, b=body, e2=e: b(wrap(e2, eng)))
        for kk2 in set(mock_end) | set(self.cnt):
            assert mock_end.get(kk2, 0) == self.cnt.get(kk2, 0), (
                f"phase {name}: count drift on {kk2}: mock={mock_end.get(kk2, 0)} "
                f"real={self.cnt.get(kk2, 0)}")

    def build(self):
        nc = self.nc
        dbgkind = "ExternalOutput" if self.dbg else "Internal"
        self.x0 = nc.dram_tensor("x0", [S, D], F32, kind="ExternalInput").ap()
        self.wqk0 = nc.dram_tensor("wqk0", [32, 128, D], F32, kind="ExternalInput").ap()
        self.wv0 = nc.dram_tensor("wv0", [16, 128, D], F32, kind="ExternalInput").ap()
        self.wo0 = nc.dram_tensor("wo0", [NH, 128, D], F32, kind="ExternalInput").ap()
        self.w13 = nc.dram_tensor("w13", [NH, 128, 2 * F0], F32, kind="ExternalInput").ap()
        self.w2p = nc.dram_tensor("w2p", [NH, 128, F0], F32, kind="ExternalInput").ap()
        self.nwbi = nc.dram_tensor("nwbi", [128, F0], F32, kind="ExternalInput").ap()
        self.wq1 = nc.dram_tensor("wq1", [NH, 128, D], F32, kind="ExternalInput").ap()
        self.wk1 = nc.dram_tensor("wk1", [1, 128, D], F32, kind="ExternalInput").ap()
        self.wv1 = nc.dram_tensor("wv1", [1, 128, D], F32, kind="ExternalInput").ap()
        self.wo1 = nc.dram_tensor("wo1", [NH, 128, D], F32, kind="ExternalInput").ap()
        self.wgu1 = nc.dram_tensor("wgu1", [NFT, 128, 2 * D], F32, kind="ExternalInput").ap()
        self.wd1 = nc.dram_tensor("wd1", [NH, 128, F0], F32, kind="ExternalInput").ap()
        self.gsp = nc.dram_tensor("gsp", [128, NFT], F32, kind="ExternalInput").ap()
        self.usp = nc.dram_tensor("usp", [128, NFT], F32, kind="ExternalInput").ap()
        self.dsp = nc.dram_tensor("dsp", [128, NH], F32, kind="ExternalInput").ap()
        self.iden = nc.dram_tensor("iden", [128, 128], BF16, kind="ExternalInput").ap()
        self.mskin = nc.dram_tensor("mskin", [128, 2048], BF16, kind="ExternalInput").ap()
        self.cst = nc.dram_tensor("cst", [128, 8], F32, kind="ExternalInput").ap()

        self.out = nc.dram_tensor("outp", [S, D], F32, kind="ExternalOutput").ap()

        self.codes_h = nc.dram_tensor("codes_h", [NH, 128, S], BF16, kind=dbgkind).ap()
        self.xn_h = nc.dram_tensor("xn_h", [NH, 128, S], BF16, kind=dbgkind).ap()
        self.q_hb = nc.dram_tensor("q_hb", [HEADS, 128, S], BF16, kind=dbgkind).ap()
        self.k_hb = nc.dram_tensor("k_hb", [NH, 128, S], BF16, kind=dbgkind).ap()
        self.o_hb = nc.dram_tensor("o_hb", [HEADS, 128, S], BF16, kind=dbgkind).ap()
        self.v_hb = nc.dram_tensor("v_hb", [NT, 128, D], BF16, kind=dbgkind).ap()
        self.s_hb = nc.dram_tensor("s_hb", [NT, 128, D], BF16, kind=dbgkind).ap()
        self.g_hb = nc.dram_tensor("g_hb", [NT, 128, D], BF16, kind=dbgkind).ap()
        self.x1_h = nc.dram_tensor("x1_h", [NT, 128, D], F32, kind=dbgkind).ap()
        self.gn_hb = nc.dram_tensor("gn_hb", [S, F0], BF16, kind=dbgkind).ap()
        self.hf_h = nc.dram_tensor("hf_h", [NFT, 128, S], BF16, kind=dbgkind).ap()

        ctx = self.ctx
        self.P16 = ctx.enter_context(nc.sbuf_tensor("P16", [128, 86016], BF16))
        self.P32 = ctx.enter_context(nc.sbuf_tensor("P32", [128, 8192], F32))
        self.PI = ctx.enter_context(nc.sbuf_tensor("PI", [128, 1024], I32))
        self.ident = ctx.enter_context(nc.sbuf_tensor("identS", [128, 128], BF16))
        self.sinv = ctx.enter_context(nc.sbuf_tensor("sinv", [128, NT], F32))
        self.sinv3 = ctx.enter_context(nc.sbuf_tensor("sinv3", [128, NT], F32))
        self.ssqB = ctx.enter_context(nc.sbuf_tensor("ssqB", [128, NT], F32))
        self.mxgB = ctx.enter_context(nc.sbuf_tensor("mxgB", [128, NT], F32))
        self.aa8 = ctx.enter_context(nc.sbuf_tensor("aa8", [128, NT], F32))
        self.i2v = ctx.enter_context(nc.sbuf_tensor("i2v", [128, NT], F32))
        self.gs_s = ctx.enter_context(nc.sbuf_tensor("gs_s", [128, NFT], F32))
        self.gsh_s = ctx.enter_context(nc.sbuf_tensor("gsh_s", [128, NFT], F32))
        self.us_s = ctx.enter_context(nc.sbuf_tensor("us_s", [128, NFT], F32))
        self.ds_s = ctx.enter_context(nc.sbuf_tensor("ds_s", [128, NH], F32))
        self.epsb = ctx.enter_context(nc.sbuf_tensor("epsb", [128, 1], F32))
        self.st1 = ctx.enter_context(nc.sbuf_tensor("st1", [128, 8], F32))
        self.st2 = ctx.enter_context(nc.sbuf_tensor("st2", [128, 8], F32))
        self.st3 = ctx.enter_context(nc.sbuf_tensor("st3", [128, 8], F32))

        self.A = self.P16[:, 0:32768]
        self.Bp = self.P16[:, 32768:65536]
        self.Cp = self.P16[:, 65536:86016]

        self.phase_setup()
        steps = [self.phase_ln0_, self.phase_qkv0_, self.phase_af0, self.phase_wo0_,
                 self.phase_ffn0S, self.phase_ffn0B, self.phase_ln1_, self.phase_qkv1_,
                 self.phase_af1, self.phase_wo1_, self.phase_ffn1dn, self.phase_final]
        for i, f in enumerate(steps):
            if self.dbg_upto >= i + 1:
                f()
        if self.dbg_upto < len(steps):
            self.phase_flush()
        self.ctx.close()
        return self.nc

    def phase_ln0_(self):
        self.phase_ln(0)

    def phase_ln1_(self):
        self.phase_ln(1)

    def phase_qkv0_(self):
        self.phase_qkv(0)

    def phase_qkv1_(self):
        self.phase_qkv(1)

    def phase_wo0_(self):
        self.phase_wo(0)

    def phase_wo1_(self):
        self.phase_wo(1)

    def phase_setup(self):
        k = self

        def syncf(sy):
            for dst, src in [(k.ident[:], k.iden), (k.msk(), k.mskin),
                             (k.gs_s[:], k.gsp), (k.us_s[:], k.usp),
                             (k.ds_s[:], k.dsp), (k.st3[:], k.cst)]:
                sy.dma_start(out=dst, in_=src).then_inc(k.sem("ld"), 16)
                k.cnt["ld"] = k.cnt.get("ld", 0) + 16
            k.wtv(sy, "ld", k.cnt.get("ld", 0))
            k.bump(sy.nop(), "sY", mark="setup_done")

        def vecf(v):
            k.wt(v, "setup_done")
            k.bump(v.tensor_copy(k.epsb[:], k.st3[:, 0:1]), "sV", mark="eps_ready")
            k.bump(v.tensor_scalar_mul(k.gsh_s[:], k.gs_s[:], 0.5), "sV", mark="gsh_ready")

        self.phase("setup", {"sync": syncf, "vector": vecf})

    def msk(self):
        return self.Cp[:, 18432:20480]

    def phase_ln(self, L):
        k = self
        XF = [k.P32[:, 0:2048], k.P32[:, 2048:4096]]
        XNF = k.P32[:, 4096:6144]
        RS1 = k.P32[:, 6144:7168]
        RS2 = k.P32[:, 7168:8192]
        XN16 = k.Cp[:, 0:2048]
        SC16 = k.Cp[:, 2048:4096]
        CFM = k.Cp[:, 4096:6144]
        SGS = [k.Cp[:, 6144:8192], k.Cp[:, 8192:10240]]
        GGS = [k.Cp[:, 10240:12288], k.Cp[:, 12288:14336]]
        DUMP = k.Cp[:, 14336:16384]
        sums = k.st1[:, 0:1]; ssqv = k.st1[:, 1:2]; mean = k.st1[:, 2:3]
        msq = k.st1[:, 3:4]; varr = k.st1[:, 4:5]; sd = k.st1[:, 5:6]; rstd = k.st1[:, 6:7]
        ssn = k.st1[:, 7:8]
        rmx = k.st2[:, 0:1]; rm2 = k.st2[:, 1:2]; rcp = k.st2[:, 2:3]; rcp127 = k.st2[:, 3:4]
        tmp1 = k.st2[:, 4:5]
        Afm = k.A.rearrange("p (h t) -> p h t", h=NH)

        with self.nc.psum_tensor(f"TP{L}", [128, 2048], BF16) as TP:
            halves = [TP[:, 0:1024], TP[:, 1024:2048]]

            def syncf(sy):
                for i in range(NT):
                    if i >= 2:
                        k.wt(sy, f"ln{L}xfree_{i-2}")
                        if L == 1:
                            k.wtv(sy, "st", k.marks[f"ln{L}stchk_{i-2}"][1]) if False else None
                            k.wt(sy, f"ln{L}stchk_{i-2}")
                    sy.dma_start(out=XF[i % 2][:], in_=k.x0[i * 128:(i + 1) * 128, :]
                                 ).then_inc(k.sem(f"dx{i%2}"), 16)
                    amt = 16
                    if L == 1:
                        sy.dma_start(out=SGS[i % 2][:], in_=k.s_hb[i]).then_inc(k.sem(f"dx{i%2}"), 16)
                        sy.dma_start(out=GGS[i % 2][:], in_=k.g_hb[i]).then_inc(k.sem(f"dx{i%2}"), 16)
                        amt = 48
                    k.bump(None, f"dx{i%2}", amt, mark=f"ln{L}ld_{i}")
                    if L == 1:
                        k.wt(sy, f"ln{L}x1_{i}")
                        sy.dma_start(out=k.x1_h[i], in_=XF[i % 2][:]).then_inc(k.sem("st"), 16)
                        k.bump(None, "st", 16)
                        k.wt(sy, f"ln{L}xnev_{i}_1")
                        sy.dma_start(out=k.xn_h.rearrange("h p t -> p h t")[:, :, i * 128:(i + 1) * 128],
                                     in_=Afm[:, :, i * 128:(i + 1) * 128]).then_inc(k.sem("st"), 16)
                        k.bump(None, "st", 16)
                        k.wtv(sy, "st", k.cnt.get("st", 0))
                        k.bump(sy.nop(), "sY", mark=f"ln{L}stchk_{i}")
                    elif "codes" not in DBG_SKIP:
                        k.wt(sy, f"ln{L}cdev_{i}_1")
                        sy.dma_start(out=k.codes_h.rearrange("h p t -> p h t")[:, :, i * 128:(i + 1) * 128],
                                     in_=CFM.rearrange("p (h t) -> p h t", h=NH)).then_inc(k.sem("st"), 16)
                        k.bump(None, "st", 16)
                        k.wtv(sy, "st", k.cnt.get("st", 0))
                        k.bump(sy.nop(), "sY", mark=f"ln{L}stchk_{i}")
                k.wtv(sy, "st", k.cnt.get("st", 0))
                k.bump(sy.nop(), "sY", mark=f"ln{L}_done")

            def vecf(v):
                for i in range(NT):
                    x = XF[i % 2]
                    k.wt(v, f"ln{L}ld_{i}")
                    if L == 1:
                        v.tensor_tensor(XNF[:], SGS[i % 2][:], GGS[i % 2][:], OP.mult)
                        ins = v.tensor_tensor(x[:], x[:], XNF[:], OP.add)
                        k.bump(ins, "sV", mark=f"ln{L}x1_{i}")
                    ins = v.tensor_reduce(sums, x[:], AX, OP.add)
                    k.bump(ins, "sV", mark=f"ln{L}sum_{i}")
                    k.wt(v, f"ln{L}msq_{i}")
                    v.tensor_tensor(varr, ssn, msq, OP.subtract)
                    ins = v.tensor_tensor(varr, varr, k.epsb[:], OP.add)
                    k.bump(ins, "sV", mark=f"ln{L}var_{i}")
                    k.wt(v, f"ln{L}sd_{i}")
                    ins = v.reciprocal(rstd, sd)
                    k.bump(ins, "sV", mark=f"ln{L}rstd_{i}")
                    k.wt(v, f"ln{L}rstd_{i}")
                    ins = v.tensor_scalar(XNF[:], x[:], mean, rstd, OP.subtract, OP.mult)
                    k.bump(ins, "sV", mark=f"ln{L}xn_{i}")
                    if L == 1:
                        k.marks[f"ln{L}xfree_{i}"] = k.marks[f"ln{L}xn_{i}"]
                    elif "codes" in DBG_SKIP:
                        k.marks[f"ln{L}xfree_{i}"] = k.marks[f"ln{L}xn_{i}"]
                    else:
                        ins = v.tensor_reduce(rmx, XNF[:], AX, OP.max, apply_absolute_value=True)
                        k.bump(ins, "sV", mark=f"ln{L}rmx_{i}")
                        k.wt(v, f"ln{L}rmx_{i}")
                        v.tensor_single_scalar(rm2, rmx, 1e-5, OP.max)
                        ins = v.reciprocal(rcp, rm2)
                        k.bump(ins, "sV", mark=f"ln{L}rcp_{i}")
                        k.wt(v, f"ln{L}rcp_{i}")
                        v.tensor_single_scalar(rcp127, rcp, 127.0, OP.mult)
                        v.tensor_single_scalar(k.sinv[:, i:i + 1], rm2, 1.0 / 127.0, OP.mult)
                        v.tensor_tensor(tmp1, k.sinv[:, i:i + 1], k.sinv[:, i:i + 1], OP.mult)
                        v.tensor_tensor(k.sinv3[:, i:i + 1], tmp1, k.sinv[:, i:i + 1], OP.mult)
                        k.wt(v, f"ln{L}xn16_{i}")
                        for hh in range(2):
                            xnh = XNF[:, hh * 1024:(hh + 1) * 1024]
                            v.tensor_scalar(RS1[:], xnh, rcp127, 0.5, OP.mult, OP.add)
                            v.tensor_copy(k.PI[:], RS1[:])
                            v.tensor_copy(RS2[:], k.PI[:])
                            v.tensor_tensor(xnh, RS2[:], RS1[:], OP.is_gt)
                            ins = v.tensor_tensor(SC16[:, hh * 1024:(hh + 1) * 1024],
                                                  RS2[:], xnh, OP.subtract)
                        k.bump(ins, "sV", mark=f"ln{L}codes_{i}")
                        k.marks[f"ln{L}xfree_{i}"] = k.marks[f"ln{L}codes_{i}"]

            def scaf(sc):
                for i in range(NT):
                    k.wt(sc, f"ln{L}ld_{i}")
                    if L == 1:
                        k.wt(sc, f"ln{L}x1_{i}")
                    ins = sc.activation(DUMP[:], XF[i % 2][:], AF.Square, accum_out=ssqv)
                    k.bump(ins, "sS", mark=f"ln{L}ssq_{i}")
                    k.wt(sc, f"ln{L}sum_{i}")
                    ins = sc.mul(mean, sums, 1.0 / D)
                    k.bump(ins, "sS", mark=f"ln{L}mean_{i}")
                    k.wt(sc, f"ln{L}ssq_{i}")
                    sc.mul(ssn, ssqv, 1.0 / D)
                    k.wt(sc, f"ln{L}mean_{i}")
                    ins = sc.activation(msq, mean, AF.Square)
                    k.bump(ins, "sS", mark=f"ln{L}msq_{i}")
                    k.wt(sc, f"ln{L}var_{i}")
                    ins = sc.activation(sd, varr, AF.Sqrt)
                    k.bump(ins, "sS", mark=f"ln{L}sd_{i}")
                    k.wt(sc, f"ln{L}xn_{i}")
                    ins = sc.copy(XN16[:], XNF[:])
                    k.bump(ins, "sS", mark=f"ln{L}xn16_{i}")
                    if "xpose" not in DBG_SKIP:
                        for b in range(2):
                            k.wt(sc, f"ln{L}txn_{i}_{b}")
                            if "ev3d" in DBG_SKIP:
                                for j in range(8):
                                    ins = sc.copy(k.A[:, (b * 8 + j) * 2048 + i * 128:
                                                     (b * 8 + j) * 2048 + i * 128 + 128],
                                                  halves[b][:, j * 128:(j + 1) * 128])
                            else:
                                dst = Afm[:, b * 8:(b + 1) * 8, i * 128:(i + 1) * 128]
                                ins = sc.copy(dst, halves[b].rearrange("p (h t) -> p h t", h=8))
                            k.bump(ins, "sS", mark=f"ln{L}xnev_{i}_{b}")
                    if L == 0 and "codes" not in DBG_SKIP and "xpose" not in DBG_SKIP:
                        for b in range(2):
                            k.wt(sc, f"ln{L}tcd_{i}_{b}")
                            dst = CFM.rearrange("p (h t) -> p h t", h=NH)[:, b * 8:(b + 1) * 8, :]
                            ins = sc.copy(dst, halves[b].rearrange("p (h t) -> p h t", h=8))
                            k.bump(ins, "sS", mark=f"ln{L}cdev_{i}_{b}")

            def tenf(t):
                if "xpose" in DBG_SKIP:
                    return
                tphist = {}
                for i in range(NT):
                    k.wt(t, f"ln{L}xn16_{i}")
                    for b in range(2):
                        if b in tphist:
                            k.wt(t, tphist[b])
                        for j in range(8):
                            ins = t.transpose(halves[b][:, j * 128:(j + 1) * 128],
                                              XN16[:, (b * 8 + j) * 128:(b * 8 + j) * 128 + 128],
                                              k.ident[:])
                        k.bump(ins, "sT", mark=f"ln{L}txn_{i}_{b}")
                        tphist[b] = f"ln{L}xnev_{i}_{b}"
                    if L == 0 and "codes" not in DBG_SKIP:
                        k.wt(t, f"ln{L}codes_{i}")
                        for b in range(2):
                            k.wt(t, tphist[b])
                            for j in range(8):
                                ins = t.transpose(halves[b][:, j * 128:(j + 1) * 128],
                                                  SC16[:, (b * 8 + j) * 128:(b * 8 + j) * 128 + 128],
                                                  k.ident[:])
                            k.bump(ins, "sT", mark=f"ln{L}tcd_{i}_{b}")
                            tphist[b] = f"ln{L}cdev_{i}_{b}"

            self.phase(f"ln{L}", {"tensor": tenf, "vector": vecf, "scalar": scaf, "sync": syncf})

    def phase_qkv(self, L):
        k = self
        WS = [k.Cp[:, 0:2048], k.Cp[:, 2048:4096]]
        QS = k.Cp[:, 4096:6144]
        VTS = k.Cp[:, 6144:8192]
        if L == 0:
            obs = [("q", i, k.wqk0[i]) for i in range(16)] + \
                  [("k", i, k.wqk0[16 + i]) for i in range(16)] + \
                  [("v", i, k.wv0[i]) for i in range(16)]
        else:
            obs = [("q", i, k.wq1[i]) for i in range(16)] + \
                  [("k", 0, k.wk1[0]), ("v", 0, k.wv1[0])]

        with self.nc.psum_tensor(f"QP{L}", [128, 2048], F32) as PB, \
             self.nc.psum_tensor(f"QT{L}", [128, 2048], BF16) as TP:
            tphalves = [TP[:, 0:1024], TP[:, 1024:2048]]

            def gpsf(g):
                for n, (kind, ob, src) in enumerate(obs):
                    if n >= 2:
                        k.wt(g, f"qkv{L}free_{n-2}")
                    g.dma_start(out=WS[n % 2][:], in_=src).then_inc(k.sem(f"dw{n%2}"), 16)
                    k.bump(None, f"dw{n%2}", 16, mark=f"qkv{L}w_{n}")

            def tenf(t):
                tphist = {}
                for n, (kind, ob, src) in enumerate(obs):
                    k.wt(t, f"qkv{L}w_{n}")
                    for tc in range(4):
                        if n >= 1:
                            k.wt(t, f"qkv{L}ev_{n-1}_{tc}")
                        pq = PB[:, tc * 512:(tc + 1) * 512]
                        for ht in range(NH):
                            ins = t.matmul(pq, WS[n % 2][:, ht * 128:(ht + 1) * 128],
                                           k.A[:, ht * 2048 + tc * 512: ht * 2048 + tc * 512 + 512],
                                           start=(ht == 0), stop=(ht == NH - 1))
                        k.bump(ins, "sT", mark=f"qkv{L}mm_{n}_{tc}")
                    if kind == "v":
                        k.wt(t, f"qkv{L}ev_{n}_3")
                        for half in range(2):
                            if half in tphist:
                                k.wt(t, tphist[half])
                            for j in range(8):
                                tt = half * 8 + j
                                ins = t.transpose(tphalves[half][:, j * 128:(j + 1) * 128],
                                                  QS[:, tt * 128:(tt + 1) * 128], k.ident[:])
                            k.bump(ins, "sT", mark=f"qkv{L}vt_{n}_{half}")
                            tphist[half] = f"qkv{L}vte_{n}_{half}"

            def vecf(v):
                for n, (kind, ob, src) in enumerate(obs):
                    for tc in range(4):
                        k.wt(v, f"qkv{L}mm_{n}_{tc}")
                        pq = PB[:, tc * 512:(tc + 1) * 512]
                        if kind == "q":
                            if n >= 1 and tc == 0 and obs[n - 1][0] in ("q", "k"):
                                k.wt(v, f"qkv{L}qst_{n-1}")
                            ins = v.tensor_single_scalar(QS[:, tc * 512:(tc + 1) * 512], pq, SCALE, OP.mult)
                        elif kind == "k":
                            if L == 0:
                                if tc == 0:
                                    pv = [m for m, (kk2, _, _) in enumerate(obs[:n]) if kk2 in ("q", "k")]
                                    if pv:
                                        k.wt(v, f"qkv{L}qst_{pv[-1]}")
                                dst = QS[:, tc * 512:(tc + 1) * 512]
                            else:
                                dst = k.Cp[:, 8192 + tc * 512: 8192 + tc * 512 + 512]
                            ins = v.tensor_copy(dst, pq)
                        else:
                            if tc == 0:
                                users = ("q", "k", "v") if L == 0 else ("q", "v")
                                pv = [m for m, (kk2, _, _) in enumerate(obs[:n]) if kk2 in users]
                                if pv:
                                    pn = pv[-1]
                                    if obs[pn][0] == "v":
                                        k.wt(v, f"qkv{L}vt_{pn}_1")
                                    else:
                                        k.wt(v, f"qkv{L}qst_{pn}")
                            ins = v.tensor_copy(QS[:, tc * 512:(tc + 1) * 512], pq)
                        k.bump(ins, "sV", mark=f"qkv{L}ev_{n}_{tc}")
                    k.marks[f"qkv{L}free_{n}"] = k.marks[f"qkv{L}ev_{n}_3"]

            def scaf(sc):
                for n, (kind, ob, src) in enumerate(obs):
                    if kind != "v":
                        continue
                    for half in range(2):
                        k.wt(sc, f"qkv{L}vt_{n}_{half}")
                        prevv = [m for m, (kk, _, _) in enumerate(obs[:n]) if kk == "v"]
                        if prevv:
                            k.wt(sc, f"qkv{L}vstf_{prevv[-1]}_{half}")
                        ins = sc.copy(VTS[:, half * 1024:(half + 1) * 1024], tphalves[half][:])
                        k.bump(ins, "sS", mark=f"qkv{L}vte_{n}_{half}")

            def syncf(sy):
                for n, (kind, ob, src) in enumerate(obs):
                    if kind == "q" or (kind == "k" and L == 0):
                        k.wt(sy, f"qkv{L}ev_{n}_3")
                        dst = k.q_hb[ob] if kind == "q" else k.k_hb[ob]
                        sy.dma_start(out=dst, in_=QS[:]).then_inc(k.sem("st"), 16)
                        k.bump(None, "st", 16)
                        k.wtv(sy, "st", k.cnt.get("st", 0))
                        k.bump(sy.nop(), "sY", mark=f"qkv{L}qst_{n}")
                    elif kind == "v":
                        for half in range(2):
                            k.wt(sy, f"qkv{L}vte_{n}_{half}")
                            dst = k.v_hb.rearrange("n p t -> p n t")[:, half * 8:(half + 1) * 8,
                                                                     ob * 128:(ob + 1) * 128]
                            srcap = VTS[:, half * 1024:(half + 1) * 1024].rearrange("p (n t) -> p n t", n=8)
                            sy.dma_start(out=dst, in_=srcap).then_inc(k.sem("st"), 16)
                            k.bump(None, "st", 16)
                            k.wtv(sy, "st", k.cnt.get("st", 0))
                            k.bump(sy.nop(), "sY", mark=f"qkv{L}vstf_{n}_{half}")
                k.wtv(sy, "st", k.cnt.get("st", 0))
                k.bump(sy.nop(), "sY", mark=f"qkv{L}_done")

            self.phase(f"qkv{L}", {"tensor": tenf, "vector": vecf, "scalar": scaf,
                                   "sync": syncf, "gpsimd": gpsf})

    def phase_att(self, L):
        k = self
        causal = (L == 0)
        QH = [k.Cp[:, 0:2048], k.Cp[:, 2048:4096]]
        VH = [k.Cp[:, 4096:6144], k.Cp[:, 6144:8192]]
        PSBs = ([k.Cp[:, 8192:10240], k.Cp[:, 18432:20480]] if L == 1
                else [k.Cp[:, 8192:10240], k.Cp[:, 8192:10240]])
        PT = k.Cp[:, 10240:18432]
        ngp = [k.st2[:, j:j + 1] for j in range(4)]
        negm_b = [k.st1[:, 0:1], k.st1[:, 6:7]]
        za = k.st1[:, 1:2]; zb = k.st1[:, 2:3]
        zs = k.st1[:, 3:4]; rz = k.st1[:, 4:5]; tmp = k.st1[:, 5:6]

        def nchf(qt):
            return qt // 4 + 1 if causal else 4

        def nregf(qt):
            return 1 if nchf(qt) <= 2 else 2

        regs_of = {}
        rr = 0
        for h in range(HEADS):
            for qt in range(NT):
                n = nregf(qt)
                regs_of[(h, qt)] = [rr, (rr + 1) % 3][:n]
                rr = (rr + n) % 3

        with self.nc.psum_tensor(f"SR0{L}", [128, 1024], F32) as SR0, \
             self.nc.psum_tensor(f"SR1{L}", [128, 1024], F32) as SR1, \
             self.nc.psum_tensor(f"SR2{L}", [128, 1024], F32) as SR2, \
             self.nc.psum_tensor(f"PP{L}", [128, 1024], BF16) as PP, \
             self.nc.psum_tensor(f"OT{L}", [128, 512], F32) as OT:
            SCR = [SR0, SR1, SR2]

            def scslice(h, qt, c):
                rg = regs_of[(h, qt)][c // 2]
                return SCR[rg][:, (c % 2) * 512:(c % 2 + 1) * 512]

            def transp(t, h, qt, ppq):
                nkb = nchf(qt) * 4
                PSB = PSBs[qt % 2]
                k.wt(t, f"a{L}pn_{h}_{qt}")
                for b in range((nkb + 7) // 8):
                    if ppq:
                        for m in ppq[0]:
                            k.wt(t, m)
                    j0, j1 = b * 8, min(nkb, b * 8 + 8)
                    for j in range(j0, j1):
                        ins = t.transpose(PP[:, (j - j0) * 128:(j - j0 + 1) * 128],
                                          PSB[:, j * 128:(j + 1) * 128], k.ident[:])
                    k.bump(ins, "sT", mark=f"a{L}tpb_{h}_{qt}_{b}")
                    nm = "S" if b % 2 == 0 else "V"
                    ppq[:] = [[f"a{L}pte{nm}_{h}_{qt}_{b}"]]
                k.marks[f"a{L}tp_{h}_{qt}"] = k.marks[f"a{L}tpb_{h}_{qt}_{(nkb-1)//8}"]

            def syncf(sy):
                for h in range(HEADS):
                    if h >= 2:
                        k.wt(sy, f"a{L}hdone_{h-2}")
                    sy.dma_start(out=QH[h % 2][:], in_=k.q_hb[h]).then_inc(k.sem(f"dq{h%2}"), 16)
                    c0 = h * 128 if L == 0 else 0
                    sy.dma_start(out=VH[h % 2][:].rearrange("p (n t) -> p n t", n=16),
                                 in_=k.v_hb.rearrange("n p t -> p n t")[:, :, c0:c0 + 128]
                                 ).then_inc(k.sem(f"dq{h%2}"), 16)
                    k.bump(None, f"dq{h%2}", 32, mark=f"a{L}qv_{h}")

            def tenf(t):
                ppq = []
                regfree = {}
                for h in range(HEADS):
                    k.wt(t, f"a{L}qv_{h}")
                    for qt in range(NT):
                        nch = nchf(qt)
                        kbase = h * 2048 if L == 0 else 0
                        regs = regs_of[(h, qt)]
                        for i, rg in enumerate(regs):
                            if rg in regfree:
                                k.wt(t, regfree[rg])
                            for c in range(2 * i, min(nch, 2 * i + 2)):
                                ins = t.matmul(scslice(h, qt, c),
                                               QH[h % 2][:, qt * 128:(qt + 1) * 128],
                                               k.Bp[:, kbase + c * 512: kbase + c * 512 + 512],
                                               start=True, stop=True)
                                k.bump(ins, "sT", mark=f"a{L}scc_{h}_{qt}_{c}")
                            regfree[rg] = f"a{L}exp_{h}_{qt}_{i}"
                        if qt > 0:
                            transp(t, h, qt - 1, ppq)
                            if (qt - 1) % 4 == 3:
                                self._att_pv(t, L, h, (qt - 1) // 4, causal, VH, PT, OT)
                    transp(t, h, NT - 1, ppq)
                    self._att_pv(t, L, h, 3, causal, VH, PT, OT)

            def do_rzpn(v, h, qt):
                nch = nchf(qt)
                nreg = nregf(qt)
                if nreg == 2:
                    k.wt(v, f"a{L}exp_{h}_{qt}_1")
                    ins = v.tensor_tensor(zs, za, zb, OP.add)
                    k.bump(ins, "sV")
                    k.wtv(v, "sVa", k.cnt["sVa"])
                    ins = v.reciprocal(rz, zs)
                else:
                    k.wt(v, f"a{L}exp_{h}_{qt}_0")
                    ins = v.reciprocal(rz, za)
                k.bump(ins, "sV", mark=f"a{L}rz_{h}_{qt}")
                k.wt(v, f"a{L}rz_{h}_{qt}")
                PSB = PSBs[qt % 2]
                ins = v.tensor_scalar_mul(PSB[:, 0:nch * 512], PSB[:, 0:nch * 512], rz)
                k.bump(ins, "sV", mark=f"a{L}pn_{h}_{qt}")

            def vecf(v):
                for h in range(HEADS):
                    for qt in range(NT):
                        nch = nchf(qt)
                        nreg = nregf(qt)
                        regs = regs_of[(h, qt)]
                        off = qt % 4
                        if qt > 0:
                            do_rzpn(v, h, qt - 1)
                        if qt > 1:
                            self._ptevac_v(v, L, h, qt - 2, nchf, PP, PT)
                        for i, rg in enumerate(regs):
                            sp = min(nch, 2 * i + 2) - 2 * i
                            for c in range(2 * i, 2 * i + sp):
                                k.wt(v, f"a{L}scc_{h}_{qt}_{c}")
                            if causal and 2 * i + sp == nch:
                                sl = scslice(h, qt, nch - 1)
                                v.tensor_tensor(sl, sl, k.msk()[:, off * 512:(off + 1) * 512], OP.add)
                            ins = v.tensor_reduce(ngp[i], SCR[rg][:, 0:sp * 512], AX, OP.max, negate=True)
                        k.bump(ins, "sV")
                        k.wtv(v, "sVa", k.cnt["sVa"])
                        negm = negm_b[qt % 2]
                        if nreg == 1:
                            ins = v.tensor_copy(negm, ngp[0])
                        else:
                            ins = v.tensor_tensor(negm, ngp[0], ngp[1], OP.min)
                        k.bump(ins, "sV", mark=f"a{L}neg_{h}_{qt}")
                        if qt % 4 == 1 and qt >= 5:
                            g = qt // 4 - 1
                            k.wt(v, f"a{L}pv_{h}_{g}")
                            ins = v.tensor_copy(k.A[:, h * 2048 + g * 512: h * 2048 + g * 512 + 512], OT[:])
                            k.bump(ins, "sV", mark=f"a{L}oev_{h}_{g}")
                    do_rzpn(v, h, NT - 1)
                    self._ptevac_v(v, L, h, NT - 2, nchf, PP, PT)
                    self._ptevac_v(v, L, h, NT - 1, nchf, PP, PT)
                    k.wt(v, f"a{L}pv_{h}_3")
                    ins = v.tensor_copy(k.A[:, h * 2048 + 3 * 512: h * 2048 + 3 * 512 + 512], OT[:])
                    k.bump(ins, "sV", mark=f"a{L}oev_{h}_3")
                    k.marks[f"a{L}hdone_{h}"] = k.marks[f"a{L}oev_{h}_3"]

            def scaf(sc):
                for h in range(HEADS):
                    for qt in range(NT):
                        nch = nchf(qt)
                        regs = regs_of[(h, qt)]
                        PSB = PSBs[qt % 2]
                        if L == 0 and qt > 0:
                            self._ptevac_s(sc, L, h, qt - 1, nchf, PP, PT)
                        k.wt(sc, f"a{L}neg_{h}_{qt}")
                        if L == 0:
                            if qt > 0:
                                k.wt(sc, f"a{L}tp_{h}_{qt-1}")
                            elif h > 0:
                                k.wt(sc, f"a{L}tp_{h-1}_{NT-1}")
                        else:
                            if qt > 1:
                                k.wt(sc, f"a{L}tp_{h}_{qt-2}")
                            elif h > 0:
                                k.wt(sc, f"a{L}tp_{h-1}_{NT-2+qt}")
                        for i, rg in enumerate(regs):
                            sp = min(nch, 2 * i + 2) - 2 * i
                            ins = sc.activation(PSB[:, 2 * i * 512:(2 * i + sp) * 512],
                                                SCR[rg][:, 0:sp * 512], AF.Exp,
                                                bias=negm_b[qt % 2], scale=1.0,
                                                accum_out=(za if i == 0 else zb))
                            k.bump(ins, "sS", mark=f"a{L}exp_{h}_{qt}_{i}")
                            if L == 1 and i == 0 and qt > 0:
                                self._ptevac_s(sc, L, h, qt - 1, nchf, PP, PT)
                    self._ptevac_s(sc, L, h, NT - 1, nchf, PP, PT)

            self.phase(f"att{L}", {"tensor": tenf, "vector": vecf, "scalar": scaf, "sync": syncf})

    def _ptevac_common(self, eng, semn, L, h, qt, nchf, PP, PT, parity):
        k = self
        nkb = nchf(qt) * 4
        nb = (nkb + 7) // 8
        for b in range(nb):
            if b % 2 != parity:
                continue
            j0, j1 = b * 8, min(nkb, b * 8 + 8)
            if qt % 4 == 0:
                if qt > 0:
                    k.wt(eng, f"a{L}pv_{h}_{qt//4 - 1}")
                elif h > 0:
                    k.wt(eng, f"a{L}pv_{h-1}_3")
            k.wt(eng, f"a{L}tpb_{h}_{qt}_{b}")
            dst = PT.rearrange("p (n t) -> p n t", n=16)[:, j0:j1,
                                                         (qt % 4) * 128:(qt % 4) * 128 + 128]
            src = PP[:, 0:(j1 - j0) * 128].rearrange("p (n t) -> p n t", n=j1 - j0)
            if semn == "sS":
                ins = eng.copy(dst, src)
            else:
                ins = eng.tensor_copy(dst, src)
            nm = "S" if parity == 0 else "V"
            k.bump(ins, semn, mark=f"a{L}pte{nm}_{h}_{qt}_{b}")

    def _ptevac_s(self, sc, L, h, qt, nchf, PP, PT):
        self._ptevac_common(sc, "sS", L, h, qt, nchf, PP, PT, parity=0)

    def _ptevac_v(self, v, L, h, qt, nchf, PP, PT):
        self._ptevac_common(v, "sV", L, h, qt, nchf, PP, PT, parity=1)

    def _att_pv(self, t, L, h, g, causal, VH, PT, OT):
        k = self
        ktn = (4 * g + 4) if causal else 16
        for qq in range(4 * g, 4 * g + 4):
            nkb = ((qq // 4 + 1) if causal else 4) * 4
            for b in range((nkb + 7) // 8):
                nm = "S" if b % 2 == 0 else "V"
                k.wt(t, f"a{L}pte{nm}_{h}_{qq}_{b}")
        if not (h == 0 and g == 0):
            k.wt(t, f"a{L}oev_{h}_{g-1}" if g > 0 else f"a{L}oev_{h-1}_3")
        for kt in range(ktn):
            ins = t.matmul(OT[:], VH[h % 2][:, kt * 128:(kt + 1) * 128],
                           PT[:, kt * 512:(kt + 1) * 512],
                           start=(kt == 0), stop=(kt == ktn - 1))
        k.bump(ins, "sT", mark=f"a{L}pv_{h}_{g}")


    def phase_af0(self):
        k = self
        L = 0
        QH = k.Cp[:, 0:2048]
        KH = k.Cp[:, 2048:4096]
        VH = k.Cp[:, 4096:6144]
        PSB = k.Cp[:, 6144:8192]
        GNS = [k.Cp[:, 8192:8704], k.Cp[:, 8704:9216]]
        DUMP = k.Cp[:, 9216:9728]
        OS = k.Cp[:, 9728:10240]
        PT = k.Cp[:, 10240:18432]
        RL = k.P32[:, 0:512]
        M1 = k.P32[:, 512:1024]
        GT = k.P32[:, 1024:1536]
        NWB = k.P32[:, 1536:7168]
        negm_b = [k.st1[:, 0:1], k.st1[:, 6:7]]
        za = k.st1[:, 1:2]; rz = k.st1[:, 4:5]
        ssp = k.st2[:, 6:7]; mxp = k.st2[:, 7:8]
        NU = NFC * NT

        def nchf(qt):
            return qt // 4 + 1

        with self.nc.psum_tensor("SCm0", [128, 2048], F32) as SC, \
             self.nc.psum_tensor("PPm0", [128, 1024], BF16) as PP, \
             self.nc.psum_tensor("OTm0", [128, 512], F32) as OT, \
             self.nc.psum_tensor("FAm0", [128, 1024], F32) as FA:
            X1 = FA[:, 0:512]
            X3 = FA[:, 512:1024]

            def transp(t, h, qt, ppq):
                nkb = nchf(qt) * 4
                k.wt(t, f"m0pn_{h}_{qt}")
                for b in range((nkb + 7) // 8):
                    if ppq:
                        k.wt(t, ppq[0])
                    j0, j1 = b * 8, min(nkb, b * 8 + 8)
                    for j in range(j0, j1):
                        ins = t.transpose(PP[:, (j - j0) * 128:(j - j0 + 1) * 128],
                                          PSB[:, j * 128:(j + 1) * 128], k.ident[:])
                    k.bump(ins, "sT", mark=f"m0tpb_{h}_{qt}_{b}")
                    nm = "S" if b % 2 == 0 else "V"
                    ppq[:] = [f"m0pte{nm}_{h}_{qt}_{b}"]
                k.marks[f"m0tp_{h}_{qt}"] = k.marks[f"m0tpb_{h}_{qt}_{(nkb-1)//8}"]

            def ptevac(eng, semn, h, qt, parity):
                nkb = nchf(qt) * 4
                for b in range((nkb + 7) // 8):
                    if b % 2 != parity:
                        continue
                    if qt % 4 == 0:
                        if qt > 0:
                            k.wt(eng, f"m0pv_{h}_{qt//4 - 1}")
                        elif h > 0:
                            k.wt(eng, f"m0pv_{h-1}_3")
                    k.wt(eng, f"m0tpb_{h}_{qt}_{b}")
                    j0, j1 = b * 8, min(nkb, b * 8 + 8)
                    dst = PT.rearrange("p (n t) -> p n t", n=16)[:, j0:j1,
                                                                 (qt % 4) * 128:(qt % 4) * 128 + 128]
                    src = PP[:, 0:(j1 - j0) * 128].rearrange("p (n t) -> p n t", n=j1 - j0)
                    ins = eng.copy(dst, src) if semn == "sS" else eng.tensor_copy(dst, src)
                    nm = "S" if parity == 0 else "V"
                    k.bump(ins, semn, mark=f"m0pte{nm}_{h}_{qt}_{b}")

            def pv(t, h, g):
                ktn = 4 * g + 4
                for qq in range(4 * g, 4 * g + 4):
                    nkb = (qq // 4 + 1) * 4
                    for b in range((nkb + 7) // 8):
                        nm = "S" if b % 2 == 0 else "V"
                        k.wt(t, f"m0pte{nm}_{h}_{qq}_{b}")
                if not (h == 0 and g == 0):
                    k.wt(t, f"m0oev_{h}_{g-1}" if g > 0 else f"m0oev_{h-1}_3")
                for kt in range(ktn):
                    ins = t.matmul(OT[:], VH[:, kt * 128:(kt + 1) * 128],
                                   PT[:, kt * 512:(kt + 1) * 512],
                                   start=(kt == 0), stop=(kt == ktn - 1))
                k.bump(ins, "sT", mark=f"m0pv_{h}_{g}")

            def syncf(sy):
                sy.dma_start(out=k.Bp[:].rearrange("p (h t) -> p h t", h=NH),
                             in_=k.codes_h.rearrange("h p t -> p h t")).then_inc(k.sem("ld"), 16)
                k.cnt["ld"] = k.cnt.get("ld", 0) + 16
                sy.dma_start(out=NWB[:], in_=k.nwbi).then_inc(k.sem("ld"), 16)
                k.cnt["ld"] = k.cnt.get("ld", 0) + 16
                k.wtv(sy, "ld", k.cnt.get("ld", 0))
                k.bump(sy.nop(), "sY", mark="m0_pre")
                un = [0]
                acc = [0.0]

                def ffn_units(nmax):
                    while un[0] < nmax:
                        n = un[0]
                        fc, tt = divmod(n, NT)
                        k.wt(sy, f"f0gn_{fc}_{tt}")
                        sy.dma_start(out=k.gn_hb[tt * 128:(tt + 1) * 128, fc * 512:(fc + 1) * 512],
                                     in_=GNS[n % 2][:]).then_inc(k.sem("st"), 16)
                        k.bump(None, "st", 16)
                        k.wtv(sy, "st", k.cnt.get("st", 0))
                        k.bump(sy.nop(), "sY", mark=f"f0gnfree_{fc}_{tt}")
                        un[0] += 1

                for h in range(HEADS):
                    if h >= 1:
                        k.wt(sy, f"m0hdone_{h-1}")
                    sy.dma_start(out=QH[:], in_=k.q_hb[h]).then_inc(k.sem("dq0"), 16)
                    sy.dma_start(out=KH[:], in_=k.k_hb[h]).then_inc(k.sem("dq0"), 16)
                    sy.dma_start(out=VH[:].rearrange("p (n t) -> p n t", n=16),
                                 in_=k.v_hb.rearrange("n p t -> p n t")[:, :, h * 128:(h + 1) * 128]
                                 ).then_inc(k.sem("dq0"), 16)
                    k.bump(None, "dq0", 48, mark=f"m0qv_{h}")
                    for g in range(4):
                        acc[0] += NU / (HEADS * 4)
                        ffn_units(min(NU, int(acc[0])))
                        k.wt(sy, f"m0oev_{h}_{g}")
                        sy.dma_start(out=k.o_hb[h][:, g * 512:(g + 1) * 512], in_=OS[:]
                                     ).then_inc(k.sem("st"), 16)
                        k.bump(None, "st", 16)
                        k.wtv(sy, "st", k.cnt.get("st", 0))
                        k.bump(sy.nop(), "sY", mark=f"m0ofree_{h}_{g}")
                ffn_units(NU)
                k.bump(sy.nop(), "sY", mark="af0_done")

            def tenf(t):
                k.wt(t, "m0_pre")
                ppq = []
                un = [0]
                acc = [0.0]

                def ffn_units(nmax):
                    while un[0] < nmax:
                        n = un[0]
                        fc, tt = divmod(n, NT)
                        k.wt(t, f"f0w_{fc}")
                        if n >= 1:
                            pf, pt2 = divmod(n - 1, NT)
                            k.wt(t, f"f0gt_{pf}_{pt2}")
                        for ht in range(NH):
                            t.matmul(X1, k.Bp[:, ht * 2048 + tt * 128: ht * 2048 + tt * 128 + 128],
                                     k.A[:, (fc % 2) * 8192 + ht * 512:(fc % 2) * 8192 + ht * 512 + 512],
                                     start=(ht == 0), stop=(ht == NH - 1))
                        for ht in range(NH):
                            ins = t.matmul(X3, k.Bp[:, ht * 2048 + tt * 128: ht * 2048 + tt * 128 + 128],
                                           k.A[:, 16384 + (fc % 2) * 8192 + ht * 512:
                                               16384 + (fc % 2) * 8192 + ht * 512 + 512],
                                           start=(ht == 0), stop=(ht == NH - 1))
                        k.bump(ins, "sT", mark=f"f0mm_{fc}_{tt}")
                        if tt == NT - 1:
                            k.marks[f"f0adone_{fc}"] = k.marks[f"f0mm_{fc}_{tt}"]
                        un[0] += 1

                for h in range(HEADS):
                    k.wt(t, f"m0qv_{h}")
                    for qt in range(NT):
                        nch = nchf(qt)
                        if qt > 0:
                            k.wt(t, f"m0exp_{h}_{qt-1}")
                        elif h > 0:
                            k.wt(t, f"m0exp_{h-1}_{NT-1}")
                        for c in range(nch):
                            ins = t.matmul(SC[:, c * 512:(c + 1) * 512],
                                           QH[:, qt * 128:(qt + 1) * 128],
                                           KH[:, c * 512:(c + 1) * 512],
                                           start=True, stop=True)
                        k.bump(ins, "sT", mark=f"m0sc_{h}_{qt}")
                        if qt > 0:
                            transp(t, h, qt - 1, ppq)
                            if (qt - 1) % 4 == 3:
                                pv(t, h, (qt - 1) // 4)
                        acc[0] += NU / (HEADS * NT)
                        ffn_units(min(NU, int(acc[0])))
                    transp(t, h, NT - 1, ppq)
                    pv(t, h, 3)
                ffn_units(NU)

            def vecf(v):
                un = [0]
                acc = [0.0]

                def ffn_units(nmax):
                    while un[0] < nmax:
                        n = un[0]
                        fc, tt = divmod(n, NT)
                        k.wt(v, f"f0rl_{fc}_{tt}")
                        v.tensor_tensor(M1[:], X1, RL[:], OP.mult)
                        ins = v.scalar_tensor_tensor(GT[:], M1[:], k.sinv3[:, tt:tt + 1], X3,
                                                     OP.mult, OP.mult)
                        k.bump(ins, "sV", mark=f"f0gt_{fc}_{tt}")
                        if n >= 2:
                            pf, pt2 = divmod(n - 2, NT)
                            k.wt(v, f"f0gnfree_{pf}_{pt2}")
                        ins = v.tensor_tensor(GNS[n % 2][:], GT[:], NWB[:, fc * 512:(fc + 1) * 512], OP.mult)
                        k.bump(ins, "sV", mark=f"f0gn_{fc}_{tt}")
                        ins = v.tensor_reduce(mxp, GNS[n % 2][:], AX, OP.max, apply_absolute_value=True)
                        k.bump(ins, "sV", mark=f"f0mxr_{fc}_{tt}")
                        k.wt(v, f"f0ss_{fc}_{tt}")
                        k.wt(v, f"f0mxr_{fc}_{tt}")
                        if fc == 0:
                            v.tensor_copy(k.ssqB[:, tt:tt + 1], ssp)
                            ins = v.tensor_copy(k.mxgB[:, tt:tt + 1], mxp)
                        else:
                            v.tensor_tensor(k.ssqB[:, tt:tt + 1], k.ssqB[:, tt:tt + 1], ssp, OP.add)
                            ins = v.tensor_tensor(k.mxgB[:, tt:tt + 1], k.mxgB[:, tt:tt + 1], mxp, OP.max)
                        k.bump(ins, "sV", mark=f"f0ssacc_{fc}_{tt}")
                        un[0] += 1

                for h in range(HEADS):
                    for qt in range(NT):
                        nch = nchf(qt)
                        off = qt % 4
                        k.wt(v, f"m0sc_{h}_{qt}")
                        v.tensor_tensor(SC[:, (nch - 1) * 512:nch * 512],
                                        SC[:, (nch - 1) * 512:nch * 512],
                                        k.msk()[:, off * 512:(off + 1) * 512], OP.add)
                        ins = v.tensor_reduce(negm_b[qt % 2], SC[:, 0:nch * 512], AX, OP.max, negate=True)
                        k.bump(ins, "sV", mark=f"m0neg_{h}_{qt}")
                        k.wt(v, f"m0exp_{h}_{qt}")
                        ins = v.reciprocal(rz, za)
                        k.bump(ins, "sV", mark=f"m0rz_{h}_{qt}")
                        k.wt(v, f"m0rz_{h}_{qt}")
                        ins = v.tensor_scalar_mul(PSB[:, 0:nch * 512], PSB[:, 0:nch * 512], rz)
                        k.bump(ins, "sV", mark=f"m0pn_{h}_{qt}")
                        if qt > 0:
                            ptevac(v, "sV", h, qt - 1, 1)
                        if qt % 4 == 1 and qt >= 5:
                            g = qt // 4 - 1
                            k.wt(v, f"m0pv_{h}_{g}")
                            if g > 0 or h > 0:
                                k.wt(v, f"m0ofree_{h}_{g-1}" if g > 0 else f"m0ofree_{h-1}_3")
                            ins = v.tensor_copy(OS[:], OT[:])
                            k.bump(ins, "sV", mark=f"m0oev_{h}_{g}")
                        acc[0] += NU / (HEADS * NT)
                        ffn_units(min(NU, int(acc[0])))
                    ptevac(v, "sV", h, NT - 1, 1)
                    k.wt(v, f"m0pv_{h}_3")
                    k.wt(v, f"m0ofree_{h}_2")
                    ins = v.tensor_copy(OS[:], OT[:])
                    k.bump(ins, "sV", mark=f"m0oev_{h}_3")
                    k.marks[f"m0hdone_{h}"] = k.marks[f"m0oev_{h}_3"]
                ffn_units(NU)

            def scaf(sc):
                un = [0]
                acc = [0.0]

                def ffn_units(nmax):
                    while un[0] < nmax:
                        n = un[0]
                        fc, tt = divmod(n, NT)
                        k.wt(sc, f"f0mm_{fc}_{tt}")
                        ins = sc.activation(RL[:], X1, AF.Relu)
                        k.bump(ins, "sS", mark=f"f0rl_{fc}_{tt}")
                        k.wt(sc, f"f0gt_{fc}_{tt}")
                        if n >= 1:
                            pf, pt2 = divmod(n - 1, NT)
                            k.wt(sc, f"f0ssacc_{pf}_{pt2}")
                        ins = sc.activation(DUMP[:], GT[:], AF.Square, accum_out=ssp)
                        k.bump(ins, "sS", mark=f"f0ss_{fc}_{tt}")
                        un[0] += 1

                for h in range(HEADS):
                    for qt in range(NT):
                        nch = nchf(qt)
                        if qt > 0:
                            ptevac(sc, "sS", h, qt - 1, 0)
                        k.wt(sc, f"m0neg_{h}_{qt}")
                        if qt > 0:
                            k.wt(sc, f"m0tp_{h}_{qt-1}")
                        elif h > 0:
                            k.wt(sc, f"m0tp_{h-1}_{NT-1}")
                        ins = sc.activation(PSB[:, 0:nch * 512], SC[:, 0:nch * 512], AF.Exp,
                                            bias=negm_b[qt % 2], scale=1.0, accum_out=za)
                        k.bump(ins, "sS", mark=f"m0exp_{h}_{qt}")
                        acc[0] += NU / (HEADS * NT)
                        ffn_units(min(NU, int(acc[0])))
                    ptevac(sc, "sS", h, NT - 1, 0)
                ffn_units(NU)

            def gpsf(g):
                for fc in range(NFC):
                    if fc >= 2:
                        k.wt(g, f"f0adone_{fc-2}")
                    g.dma_start(out=k.A[:, (fc % 2) * 8192:(fc % 2) * 8192 + 8192]
                                .rearrange("p (h t) -> p h t", h=NH),
                                in_=k.w13[:, :, fc * 512:(fc + 1) * 512].rearrange("h p t -> p h t")
                                ).then_inc(k.sem(f"dw{fc%2}"), 16)
                    g.dma_start(out=k.A[:, 16384 + (fc % 2) * 8192:16384 + (fc % 2) * 8192 + 8192]
                                .rearrange("p (h t) -> p h t", h=NH),
                                in_=k.w13[:, :, F0 + fc * 512:F0 + fc * 512 + 512].rearrange("h p t -> p h t")
                                ).then_inc(k.sem(f"dw{fc%2}"), 16)
                    k.bump(None, f"dw{fc%2}", 32, mark=f"f0w_{fc}")

            self.phase("af0", {"tensor": tenf, "vector": vecf, "scalar": scaf,
                               "sync": syncf, "gpsimd": gpsf})

    def phase_af1(self):
        k = self
        QH = k.Cp[:, 0:2048]
        V1 = k.Cp[:, 2048:4096]
        PSBs = [k.Cp[:, 4096:6144], k.Cp[:, 6144:8192]]
        OS = k.Cp[:, 18432:18944]
        K1 = k.Cp[:, 8192:10240]
        PT = k.Cp[:, 10240:18432]
        HF = k.A[:, 16384:18432]
        GG = k.P32[:, 0:512]
        UU = k.P32[:, 512:1024]
        TT = k.P32[:, 1024:1536]
        ZZ = k.P32[:, 1536:2048]
        negm_b = [k.st1[:, 0:1], k.st1[:, 6:7]]
        za = k.st1[:, 1:2]; rz = k.st1[:, 4:5]
        WGU = [k.A[:, i * 4096:(i + 1) * 4096] for i in range(4)]
        NU = NFT * 4

        with self.nc.psum_tensor("SCm1", [128, 2048], F32) as SC, \
             self.nc.psum_tensor("PPm1", [128, 1024], BF16) as PP, \
             self.nc.psum_tensor("OTm1", [128, 512], F32) as OT, \
             self.nc.psum_tensor("GPm1", [128, 512], F32) as GP, \
             self.nc.psum_tensor("UPm1", [128, 512], F32) as UP:

            def transp(t, h, qt, ppq):
                PSB = PSBs[qt % 2]
                k.wt(t, f"m1pn_{h}_{qt}")
                for b in range(2):
                    if ppq:
                        k.wt(t, ppq[0])
                    for j in range(8):
                        ins = t.transpose(PP[:, j * 128:(j + 1) * 128],
                                          PSB[:, (b * 8 + j) * 128:(b * 8 + j) * 128 + 128], k.ident[:])
                    k.bump(ins, "sT", mark=f"m1tpb_{h}_{qt}_{b}")
                    nm = "S" if b % 2 == 0 else "V"
                    ppq[:] = [f"m1pte{nm}_{h}_{qt}_{b}"]
                k.marks[f"m1tp_{h}_{qt}"] = k.marks[f"m1tpb_{h}_{qt}_1"]

            def ptevac(eng, semn, h, qt, parity):
                for b in range(2):
                    if b % 2 != parity:
                        continue
                    if qt % 4 == 0:
                        if qt > 0:
                            k.wt(eng, f"m1pv_{h}_{qt//4 - 1}")
                        elif h > 0:
                            k.wt(eng, f"m1pv_{h-1}_3")
                    k.wt(eng, f"m1tpb_{h}_{qt}_{b}")
                    dst = PT.rearrange("p (n t) -> p n t", n=16)[:, b * 8:b * 8 + 8,
                                                                 (qt % 4) * 128:(qt % 4) * 128 + 128]
                    src = PP[:].rearrange("p (n t) -> p n t", n=8)
                    ins = eng.copy(dst, src) if semn == "sS" else eng.tensor_copy(dst, src)
                    nm = "S" if parity == 0 else "V"
                    k.bump(ins, semn, mark=f"m1pte{nm}_{h}_{qt}_{b}")

            def pv(t, h, g):
                for qq in range(4 * g, 4 * g + 4):
                    for b in range(2):
                        nm = "S" if b % 2 == 0 else "V"
                        k.wt(t, f"m1pte{nm}_{h}_{qq}_{b}")
                if not (h == 0 and g == 0):
                    k.wt(t, f"m1oev_{h}_{g-1}" if g > 0 else f"m1oev_{h-1}_3")
                for kt in range(16):
                    ins = t.matmul(OT[:], V1[:, kt * 128:(kt + 1) * 128],
                                   PT[:, kt * 512:(kt + 1) * 512],
                                   start=(kt == 0), stop=(kt == 15))
                k.bump(ins, "sT", mark=f"m1pv_{h}_{g}")

            def syncf(sy):
                sy.dma_start(out=k.Bp[:].rearrange("p (h t) -> p h t", h=NH),
                             in_=k.xn_h.rearrange("h p t -> p h t")).then_inc(k.sem("ld"), 16)
                k.cnt["ld"] = k.cnt.get("ld", 0) + 16
                sy.dma_start(out=V1[:].rearrange("p (n t) -> p n t", n=16),
                             in_=k.v_hb.rearrange("n p t -> p n t")[:, :, 0:128]
                             ).then_inc(k.sem("ld"), 16)
                k.cnt["ld"] = k.cnt.get("ld", 0) + 16
                k.wtv(sy, "ld", k.cnt.get("ld", 0))
                k.bump(sy.nop(), "sY", mark="m1_pre")
                un = [0]
                acc = [0.0]

                def ffn_units(nmax):
                    while un[0] < nmax:
                        n = un[0]
                        ft, tc = divmod(n, 4)
                        if tc == 3:
                            k.wt(sy, f"f1h_{ft}_{tc}")
                            sy.dma_start(out=k.hf_h[ft], in_=HF[:]).then_inc(k.sem("st"), 16)
                            k.bump(None, "st", 16)
                            k.wtv(sy, "st", k.cnt.get("st", 0))
                            k.bump(sy.nop(), "sY", mark=f"f1hfree_{ft}")
                        un[0] += 1

                for h in range(HEADS):
                    if h >= 1:
                        k.wt(sy, f"m1hdone_{h-1}")
                    sy.dma_start(out=QH[:], in_=k.q_hb[h]).then_inc(k.sem("dq0"), 16)
                    k.bump(None, "dq0", 16, mark=f"m1qv_{h}")
                    for g in range(4):
                        acc[0] += NU / (HEADS * 4)
                        ffn_units(min(NU, int(acc[0])))
                        k.wt(sy, f"m1oev_{h}_{g}")
                        sy.dma_start(out=k.o_hb[h][:, g * 512:(g + 1) * 512], in_=OS[:]
                                     ).then_inc(k.sem("st"), 16)
                        k.bump(None, "st", 16)
                        k.wtv(sy, "st", k.cnt.get("st", 0))
                        k.bump(sy.nop(), "sY", mark=f"m1ofree_{h}_{g}")
                ffn_units(NU)
                k.bump(sy.nop(), "sY", mark="af1_done")

            def tenf(t):
                k.wt(t, "m1_pre")
                ppq = []
                un = [0]
                acc = [0.0]

                def ffn_units(nmax):
                    while un[0] < nmax:
                        n = un[0]
                        ft, tc = divmod(n, 4)
                        k.wt(t, f"f1w_{ft}")
                        if n >= 1:
                            pf, pc = divmod(n - 1, 4)
                            k.wt(t, f"f1gt_{pf}_{pc}")
                            k.wt(t, f"f1ue_{pf}_{pc}")
                        for ht in range(NH):
                            t.matmul(GP[:], WGU[ft % 4][:, ht * 128:(ht + 1) * 128],
                                     k.Bp[:, ht * 2048 + tc * 512: ht * 2048 + tc * 512 + 512],
                                     start=(ht == 0), stop=(ht == NH - 1))
                        for ht in range(NH):
                            ins = t.matmul(UP[:], WGU[ft % 4][:, 2048 + ht * 128: 2048 + ht * 128 + 128],
                                           k.Bp[:, ht * 2048 + tc * 512: ht * 2048 + tc * 512 + 512],
                                           start=(ht == 0), stop=(ht == NH - 1))
                        k.bump(ins, "sT", mark=f"f1mm_{ft}_{tc}")
                        if tc == 3:
                            k.marks[f"f1ftdone_{ft}"] = k.marks[f"f1mm_{ft}_{tc}"]
                        un[0] += 1

                for h in range(HEADS):
                    k.wt(t, f"m1qv_{h}")
                    for qt in range(NT):
                        if qt > 0:
                            k.wt(t, f"m1exp_{h}_{qt-1}")
                        elif h > 0:
                            k.wt(t, f"m1exp_{h-1}_{NT-1}")
                        for c in range(4):
                            ins = t.matmul(SC[:, c * 512:(c + 1) * 512],
                                           QH[:, qt * 128:(qt + 1) * 128],
                                           K1[:, c * 512:(c + 1) * 512],
                                           start=True, stop=True)
                        k.bump(ins, "sT", mark=f"m1sc_{h}_{qt}")
                        if qt > 0:
                            transp(t, h, qt - 1, ppq)
                            if (qt - 1) % 4 == 3:
                                pv(t, h, (qt - 1) // 4)
                        acc[0] += NU / (HEADS * NT)
                        ffn_units(min(NU, int(acc[0])))
                    transp(t, h, NT - 1, ppq)
                    pv(t, h, 3)
                ffn_units(NU)

            def vecf(v):
                un = [0]
                acc = [0.0]

                def ffn_units(nmax):
                    while un[0] < nmax:
                        n = un[0]
                        ft, tc = divmod(n, 4)
                        k.wt(v, f"f1mm_{ft}_{tc}")
                        ins = v.tensor_scalar_mul(UU[:], UP[:], k.us_s[:, ft:ft + 1])
                        k.bump(ins, "sV", mark=f"f1ue_{ft}_{tc}")
                        k.wt(v, f"f1gt_{ft}_{tc}")
                        ins = v.scalar_tensor_tensor(GG[:], TT[:], 1.0, ZZ[:], OP.add, OP.mult)
                        k.bump(ins, "sV", mark=f"f1gg_{ft}_{tc}")
                        if tc == 0 and ft >= 1:
                            k.wt(v, f"f1hfree_{ft-1}")
                        ins = v.tensor_tensor(HF[:, tc * 512:(tc + 1) * 512], GG[:], UU[:], OP.mult)
                        k.bump(ins, "sV", mark=f"f1h_{ft}_{tc}")
                        un[0] += 1

                for h in range(HEADS):
                    for qt in range(NT):
                        k.wt(v, f"m1sc_{h}_{qt}")
                        ins = v.tensor_reduce(negm_b[qt % 2], SC[:], AX, OP.max, negate=True)
                        k.bump(ins, "sV", mark=f"m1neg_{h}_{qt}")
                        k.wt(v, f"m1exp_{h}_{qt}")
                        ins = v.reciprocal(rz, za)
                        k.bump(ins, "sV", mark=f"m1rz_{h}_{qt}")
                        k.wt(v, f"m1rz_{h}_{qt}")
                        PSB = PSBs[qt % 2]
                        ins = v.tensor_scalar_mul(PSB[:], PSB[:], rz)
                        k.bump(ins, "sV", mark=f"m1pn_{h}_{qt}")
                        if qt > 0:
                            ptevac(v, "sV", h, qt - 1, 1)
                        if qt % 4 == 1 and qt >= 5:
                            g = qt // 4 - 1
                            k.wt(v, f"m1pv_{h}_{g}")
                            if g > 0 or h > 0:
                                k.wt(v, f"m1ofree_{h}_{g-1}" if g > 0 else f"m1ofree_{h-1}_3")
                            ins = v.tensor_copy(OS[:], OT[:])
                            k.bump(ins, "sV", mark=f"m1oev_{h}_{g}")
                        acc[0] += NU / (HEADS * NT)
                        ffn_units(min(NU, int(acc[0])))
                    ptevac(v, "sV", h, NT - 1, 1)
                    k.wt(v, f"m1pv_{h}_3")
                    k.wt(v, f"m1ofree_{h}_2")
                    ins = v.tensor_copy(OS[:], OT[:])
                    k.bump(ins, "sV", mark=f"m1oev_{h}_3")
                    k.marks[f"m1hdone_{h}"] = k.marks[f"m1oev_{h}_3"]
                ffn_units(NU)

            def scaf(sc):
                un = [0]
                acc = [0.0]
                k.wt(sc, "gsh_ready")

                def ffn_units(nmax):
                    while un[0] < nmax:
                        n = un[0]
                        ft, tc = divmod(n, 4)
                        k.wt(sc, f"f1mm_{ft}_{tc}")
                        if n >= 1:
                            pf, pc = divmod(n - 1, 4)
                            k.wt(sc, f"f1gg_{pf}_{pc}")
                        ins = sc.activation(TT[:], GP[:], AF.Tanh, scale=k.gsh_s[:, ft:ft + 1])
                        k.bump(ins, "sS")
                        ins = sc.activation(ZZ[:], GP[:], AF.Copy, scale=k.gsh_s[:, ft:ft + 1])
                        k.bump(ins, "sS", mark=f"f1gt_{ft}_{tc}")
                        un[0] += 1

                for h in range(HEADS):
                    for qt in range(NT):
                        if qt > 0:
                            ptevac(sc, "sS", h, qt - 1, 0)
                        k.wt(sc, f"m1neg_{h}_{qt}")
                        if qt > 1:
                            k.wt(sc, f"m1tp_{h}_{qt-2}")
                        elif h > 0:
                            k.wt(sc, f"m1tp_{h-1}_{NT-2+qt}")
                        ins = sc.activation(PSBs[qt % 2][:], SC[:], AF.Exp,
                                            bias=negm_b[qt % 2], scale=1.0, accum_out=za)
                        k.bump(ins, "sS", mark=f"m1exp_{h}_{qt}")
                        acc[0] += NU / (HEADS * NT)
                        ffn_units(min(NU, int(acc[0])))
                    ptevac(sc, "sS", h, NT - 1, 0)
                ffn_units(NU)

            def gpsf(g):
                for ft in range(NFT):
                    if ft >= 4:
                        k.wt(g, f"f1ftdone_{ft-4}")
                    g.dma_start(out=WGU[ft % 4][:], in_=k.wgu1[ft]).then_inc(k.sem(f"dw{ft%4}"), 16)
                    k.bump(None, f"dw{ft%4}", 16, mark=f"f1w_{ft}")

            self.phase("af1", {"tensor": tenf, "vecf": None} if False else
                       {"tensor": tenf, "vector": vecf, "scalar": scaf,
                        "sync": syncf, "gpsimd": gpsf})

    def phase_wo(self, L):
        k = self
        wo = k.wo0 if L == 0 else k.wo1
        SST = k.Cp[:, 0:2048]
        with self.nc.psum_tensor(f"WOP{L}", [128, 2048], F32) as PB:
            def gpsf(g):
                for hd in range(NH):
                    g.dma_start(out=k.A[:, hd * 2048:(hd + 1) * 2048], in_=wo[hd]
                                ).then_inc(k.sem("cw"), 16)
                    k.cnt["cw"] = k.cnt.get("cw", 0) + 16
                k.wtv(g, "cw", k.cnt.get("cw", 0))
                k.bump(g.nop(), "sGc", mark=f"wo{L}_w")

            def syncf(sy):
                for hd in range(HEADS):
                    sy.dma_start(out=k.Bp[:, hd * 2048:(hd + 1) * 2048], in_=k.o_hb[hd]
                                 ).then_inc(k.sem("ld"), 16)
                    k.cnt["ld"] = k.cnt.get("ld", 0) + 16
                k.wtv(sy, "ld", k.cnt.get("ld", 0))
                k.bump(sy.nop(), "sY", mark=f"wo{L}_o")
                for ts in range(NT):
                    k.wt(sy, f"wo{L}ev_{ts*4+3}")
                    sy.dma_start(out=k.s_hb[ts], in_=SST[:]).then_inc(k.sem("st"), 16)
                    k.bump(None, "st", 16)
                    k.wtv(sy, "st", k.cnt.get("st", 0))
                    k.bump(sy.nop(), "sY", mark=f"wo{L}sfree_{ts}")
                k.bump(sy.nop(), "sY", mark=f"wo{L}_done")

            def tenf(t):
                k.wt(t, f"wo{L}_w")
                k.wt(t, f"wo{L}_o")
                for ts in range(NT):
                    for ec in range(4):
                        idx = ts * 4 + ec
                        if idx >= 4:
                            k.wt(t, f"wo{L}ev_{idx-4}")
                        pq = PB[:, ec * 512:(ec + 1) * 512]
                        for hd in range(NH):
                            ins = t.matmul(pq, k.Bp[:, hd * 2048 + ts * 128: hd * 2048 + ts * 128 + 128],
                                           k.A[:, hd * 2048 + ec * 512: hd * 2048 + ec * 512 + 512],
                                           start=(hd == 0), stop=(hd == NH - 1))
                        k.bump(ins, "sT", mark=f"wo{L}mm_{idx}")

            def vecf(v):
                for ts in range(NT):
                    for ec in range(4):
                        idx = ts * 4 + ec
                        k.wt(v, f"wo{L}mm_{idx}")
                        if ts > 0 and ec == 0:
                            k.wt(v, f"wo{L}sfree_{ts-1}")
                        ins = v.tensor_copy(SST[:, ec * 512:(ec + 1) * 512], PB[:, ec * 512:(ec + 1) * 512])
                        k.bump(ins, "sV", mark=f"wo{L}ev_{idx}")

            self.phase(f"wo{L}", {"tensor": tenf, "vector": vecf, "sync": syncf, "gpsimd": gpsf})

    def phase_ffn0A(self):
        k = self
        W1S = [k.A[:, 0:8192], k.A[:, 8192:16384]]
        W3S = [k.A[:, 16384:24576], k.A[:, 24576:32768]]
        NWB = k.Cp[:, 0:5632]
        GNS = [k.Cp[:, 5632:6144], k.Cp[:, 6144:6656]]
        DUMP = k.Cp[:, 6656:7168]
        RL = k.P32[:, 0:512]
        M1 = k.P32[:, 512:1024]
        GT = k.P32[:, 1024:1536]
        ssp = k.st1[:, 0:1]; mxp = k.st1[:, 1:2]

        with self.nc.psum_tensor("F0P", [128, 2048], F32) as PB:
            X1 = [PB[:, 0:512], PB[:, 512:1024]]
            X3 = [PB[:, 1024:1536], PB[:, 1536:2048]]

            def gpsf(g):
                g.dma_start(out=NWB[:], in_=k.nwbi).then_inc(k.sem("cw"), 16)
                k.cnt["cw"] = k.cnt.get("cw", 0) + 16
                k.wtv(g, "cw", k.cnt.get("cw", 0))
                k.bump(g.nop(), "sGc", mark="f0_nwb")
                for fc in range(NFC):
                    if fc >= 2:
                        k.wt(g, f"f0adone_{fc-2}")
                    g.dma_start(out=W1S[fc % 2][:].rearrange("p (h t) -> p h t", h=NH),
                                in_=k.w13[:, :, fc * 512:(fc + 1) * 512].rearrange("h p t -> p h t")
                                ).then_inc(k.sem(f"dw{fc%2}"), 16)
                    g.dma_start(out=W3S[fc % 2][:].rearrange("p (h t) -> p h t", h=NH),
                                in_=k.w13[:, :, F0 + fc * 512:F0 + fc * 512 + 512].rearrange("h p t -> p h t")
                                ).then_inc(k.sem(f"dw{fc%2}"), 16)
                    k.bump(None, f"dw{fc%2}", 32, mark=f"f0w_{fc}")

            def syncf(sy):
                sy.dma_start(out=k.Bp[:].rearrange("p (h t) -> p h t", h=NH),
                             in_=k.codes_h.rearrange("h p t -> p h t")).then_inc(k.sem("ld"), 16)
                k.cnt["ld"] = k.cnt.get("ld", 0) + 16
                k.wtv(sy, "ld", k.cnt.get("ld", 0))
                k.bump(sy.nop(), "sY", mark="f0_codes")
                for fc in range(NFC):
                    for tt in range(NT):
                        n = fc * NT + tt
                        k.wt(sy, f"f0gn_{fc}_{tt}")
                        sy.dma_start(out=k.gn_hb[tt * 128:(tt + 1) * 128, fc * 512:(fc + 1) * 512],
                                     in_=GNS[n % 2][:]).then_inc(k.sem("st"), 16)
                        k.bump(None, "st", 16)
                        k.wtv(sy, "st", k.cnt.get("st", 0))
                        k.bump(sy.nop(), "sY", mark=f"f0gnfree_{fc}_{tt}")
                k.bump(sy.nop(), "sY", mark="f0A_done")

            def tenf(t):
                k.wt(t, "f0_codes")
                for fc in range(NFC):
                    k.wt(t, f"f0w_{fc}")
                    for tt in range(NT):
                        pp = tt % 2
                        n = fc * NT + tt
                        if n >= 2:
                            pf, pt = divmod(n - 2, NT)
                            k.wt(t, f"f0gt_{pf}_{pt}")
                        for ht in range(NH):
                            t.matmul(X1[pp], k.Bp[:, ht * 2048 + tt * 128: ht * 2048 + tt * 128 + 128],
                                     W1S[fc % 2][:, ht * 512:(ht + 1) * 512],
                                     start=(ht == 0), stop=(ht == NH - 1))
                        for ht in range(NH):
                            ins = t.matmul(X3[pp], k.Bp[:, ht * 2048 + tt * 128: ht * 2048 + tt * 128 + 128],
                                           W3S[fc % 2][:, ht * 512:(ht + 1) * 512],
                                           start=(ht == 0), stop=(ht == NH - 1))
                        k.bump(ins, "sT", mark=f"f0mm_{fc}_{tt}")
                    k.marks[f"f0adone_{fc}"] = k.marks[f"f0mm_{fc}_{NT-1}"]

            def scaf(sc):
                for fc in range(NFC):
                    for tt in range(NT):
                        pp = tt % 2
                        n = fc * NT + tt
                        k.wt(sc, f"f0mm_{fc}_{tt}")
                        ins = sc.activation(RL[:], X1[pp], AF.Relu)
                        k.bump(ins, "sS", mark=f"f0rl_{fc}_{tt}")
                        k.wt(sc, f"f0gt_{fc}_{tt}")
                        if n >= 1:
                            pf, pt = divmod(n - 1, NT)
                            k.wt(sc, f"f0ssacc_{pf}_{pt}")
                        ins = sc.activation(DUMP[:], GT[:], AF.Square, accum_out=ssp)
                        k.bump(ins, "sS", mark=f"f0ss_{fc}_{tt}")

            def vecf(v):
                k.wt(v, "f0_nwb")
                for fc in range(NFC):
                    for tt in range(NT):
                        pp = tt % 2
                        n = fc * NT + tt
                        k.wt(v, f"f0rl_{fc}_{tt}")
                        v.tensor_tensor(M1[:], X1[pp], RL[:], OP.mult)
                        ins = v.scalar_tensor_tensor(GT[:], M1[:], k.sinv3[:, tt:tt + 1], X3[pp],
                                                     OP.mult, OP.mult)
                        k.bump(ins, "sV", mark=f"f0gt_{fc}_{tt}")
                        if n >= 2:
                            pf, pt = divmod(n - 2, NT)
                            k.wt(v, f"f0gnfree_{pf}_{pt}")
                        ins = v.tensor_tensor(GNS[n % 2][:], GT[:], NWB[:, fc * 512:(fc + 1) * 512], OP.mult)
                        k.bump(ins, "sV", mark=f"f0gn_{fc}_{tt}")
                        ins = v.tensor_reduce(mxp, GNS[n % 2][:], AX, OP.max, apply_absolute_value=True)
                        k.bump(ins, "sV", mark=f"f0mxr_{fc}_{tt}")
                        k.wt(v, f"f0ss_{fc}_{tt}")
                        k.wt(v, f"f0mxr_{fc}_{tt}")
                        if fc == 0:
                            v.tensor_copy(k.ssqB[:, tt:tt + 1], ssp)
                            ins = v.tensor_copy(k.mxgB[:, tt:tt + 1], mxp)
                        else:
                            v.tensor_tensor(k.ssqB[:, tt:tt + 1], k.ssqB[:, tt:tt + 1], ssp, OP.add)
                            ins = v.tensor_tensor(k.mxgB[:, tt:tt + 1], k.mxgB[:, tt:tt + 1], mxp, OP.max)
                        k.bump(ins, "sV", mark=f"f0ssacc_{fc}_{tt}")

            self.phase("ffn0A", {"tensor": tenf, "vector": vecf, "scalar": scaf,
                                 "sync": syncf, "gpsimd": gpsf})

    def phase_ffn0S(self):
        k = self
        SD16 = k.P32[:, 0:16]
        RSQ = k.P32[:, 16:32]
        CLP = k.P32[:, 32:48]
        REC = k.P32[:, 48:64]

        def scaf(sc):
            k.wt(sc, "f0A_vecdone")
            ins = sc.activation(SD16[:], k.ssqB[:], AF.Sqrt, bias=k.epsb[:], scale=1.0 / F0)
            k.bump(ins, "sS", mark="f0_sd")

        def vecf(v):
            k.bump(v.nop(), "sV", mark="f0A_vecdone")
            k.wt(v, "f0_sd")
            ins = v.reciprocal(RSQ[:], SD16[:])
            k.bump(ins, "sV", mark="f0_rsq")
            k.wt(v, "f0_rsq")
            v.tensor_tensor(CLP[:], RSQ[:], k.mxgB[:], OP.mult)
            v.tensor_single_scalar(CLP[:], CLP[:], 1e-5, OP.max)
            ins = v.reciprocal(REC[:], CLP[:])
            k.bump(ins, "sV", mark="f0_rec")
            k.wt(v, "f0_rec")
            v.tensor_single_scalar(k.i2v[:], CLP[:], 1.0 / 127.0, OP.mult)
            v.tensor_tensor(k.aa8[:], RSQ[:], REC[:], OP.mult)
            ins = v.tensor_single_scalar(k.aa8[:], k.aa8[:], 127.0, OP.mult)
            k.bump(ins, "sV", mark="f0_stats")

        self.phase("ffn0S", {"scalar": scaf, "vector": vecf})

    def phase_ffn0B(self):
        k = self
        iq2 = [k.Bp[:, ft * 1024:(ft + 1) * 1024] for ft in range(32)] + \
              [k.A[:, 20480 + (ft - 32) * 1024: 20480 + (ft - 31) * 1024] for ft in range(32, NFT)]
        GNL = [k.Cp[:, 0:512], k.Cp[:, 512:1024], k.Cp[:, 4096:4608], k.Cp[:, 4608:5120]]
        IQS = [k.Cp[:, 1024:1536], k.Cp[:, 3584:4096]]
        GF = k.Cp[:, 1536:2560]
        GS = k.Cp[:, 2560:3584]
        W2L = [k.A[:, 0:5632], k.A[:, 5632:11264]]
        YY = k.P32[:, 0:512]
        FF = k.P32[:, 512:1024]
        CMP = k.P32[:, 1024:1536]

        with self.nc.psum_tensor("F0D", [128, 2048], F32) as DN, \
             self.nc.psum_tensor("F0T", [128, 2048], BF16) as TP, \
             self.nc.psum_tensor("F0T2", [128, 1024], BF16) as TP2:
            tph = [TP[:, 0:1024], TP[:, 1024:2048]]

            def syncf(sy):
                for tb in range(2):
                    for tt8 in range(8):
                        for fc in range(NFC):
                            n = (tb * 8 + tt8) * NFC + fc
                            if n >= 4:
                                k.wt(sy, f"f0y_{n-4}")
                            tt = tb * 8 + tt8
                            sy.dma_start(out=GNL[n % 4][:],
                                         in_=k.gn_hb[tt * 128:(tt + 1) * 128, fc * 512:(fc + 1) * 512]
                                         ).then_inc(k.sem(f"dx{n%4}"), 16)
                            k.bump(None, f"dx{n%4}", 16, mark=f"f0gnl_{n}")
                    for dblk in range(NH):
                        k.wt(sy, f"f0gse_{tb}_{dblk}")
                        for b in range(8):
                            tsub = tb * 8 + b
                            sy.dma_start(out=k.g_hb[tsub][:, dblk * 128:(dblk + 1) * 128],
                                         in_=GS[:, b * 128:(b + 1) * 128]).then_inc(k.sem("st"), 16)
                            k.bump(None, "st", 16)
                        k.wtv(sy, "st", k.cnt.get("st", 0))
                        k.bump(sy.nop(), "sY", mark=f"f0gsfree_{tb}_{dblk}")
                k.wtv(sy, "st", k.cnt.get("st", 0))
                k.bump(sy.nop(), "sY", mark="f0B_done")

            def gpsf(g):
                for tb in range(2):
                    for dblk in range(NH):
                        n = tb * NH + dblk
                        if n >= 2:
                            pn = n - 2
                            k.wt(g, f"f0dn_{pn//NH}_{pn%NH}")
                        g.dma_start(out=W2L[n % 2][:], in_=k.w2p[dblk]).then_inc(k.sem(f"dw{n%2}"), 16)
                        k.bump(None, f"dw{n%2}", 16, mark=f"f0w2_{tb}_{dblk}")

            def vecf(v):
                for tb in range(2):
                    for tt8 in range(8):
                        tt = tb * 8 + tt8
                        for fc in range(NFC):
                            n = (tb * 8 + tt8) * NFC + fc
                            k.wt(v, f"f0gnl_{n}")
                            v.tensor_scalar(YY[:], GNL[n % 4][:], k.aa8[:, tt:tt + 1], 0.5,
                                            OP.mult, OP.add)
                            ins = v.tensor_copy(k.PI[:, 0:512], YY[:])
                            k.bump(ins, "sV", mark=f"f0y_{n}")
                            v.tensor_copy(FF[:], k.PI[:, 0:512])
                            v.tensor_tensor(CMP[:], FF[:], YY[:], OP.is_gt)
                            v.tensor_tensor(FF[:], FF[:], CMP[:], OP.subtract)
                            if n >= 2:
                                k.wt(v, f"f0tiq_{n-2}")
                            ins = v.tensor_scalar_mul(IQS[n % 2][:], FF[:], k.i2v[:, tt:tt + 1])
                            k.bump(ins, "sV", mark=f"f0iq_{n}")

            def tenf(t):
                for tb in range(2):
                    for tt8 in range(8):
                        for fc in range(NFC):
                            n = (tb * 8 + tt8) * NFC + fc
                            k.wt(t, f"f0iq_{n}")
                            if n >= 2:
                                k.wt(t, f"f0iqe_{n-2}")
                            for b in range(4):
                                ins = t.transpose(tph[n % 2][:, b * 128:(b + 1) * 128],
                                                  IQS[n % 2][:, b * 128:(b + 1) * 128], k.ident[:])
                            k.bump(ins, "sT", mark=f"f0tiq_{n}")
                    for dblk in range(NH):
                        nn = tb * NH + dblk
                        k.wt(t, f"f0w2_{tb}_{dblk}")
                        k.wt(t, f"f0iqe_{(tb*8+7)*NFC + NFC - 1}")
                        if nn >= 2:
                            pn = nn - 2
                            k.wt(t, f"f0sil_{pn//NH}_{pn%NH}")
                        half = nn % 2
                        for ft in range(NFT):
                            for tc in range(2):
                                ins = t.matmul(DN[:, half * 1024 + tc * 512: half * 1024 + tc * 512 + 512],
                                               W2L[nn % 2][:, ft * 128:(ft + 1) * 128],
                                               iq2[ft][:, tc * 512:(tc + 1) * 512],
                                               start=(ft == 0), stop=(ft == NFT - 1))
                        k.bump(ins, "sT", mark=f"f0dn_{tb}_{dblk}")
                        k.wt(t, f"f0sil_{tb}_{dblk}")
                        if nn >= 1:
                            pn = nn - 1
                            k.wt(t, f"f0gte_{pn//NH}_{pn%NH}")
                        for b in range(8):
                            ins = t.transpose(TP2[:, (b % 8) * 128:(b % 8) * 128 + 128],
                                              GF[:, b * 128:(b + 1) * 128], k.ident[:])
                        k.bump(ins, "sT", mark=f"f0tg_{tb}_{dblk}")

            def scaf(sc):
                for tb in range(2):
                    for tt8 in range(8):
                        tt = tb * 8 + tt8
                        for fc in range(NFC):
                            n = (tb * 8 + tt8) * NFC + fc
                            k.wt(sc, f"f0tiq_{n}")
                            if tb == 1:
                                k.wt(sc, "f0dn_0_15")
                            if fc < 8:
                                base = k.Bp[:, fc * 4096:(fc + 1) * 4096]
                            else:
                                base = k.A[:, 20480 + (fc - 8) * 4096:20480 + (fc - 7) * 4096]
                            dst = base.rearrange("p (b t) -> p b t", b=4)[:, :, tt8 * 128:(tt8 + 1) * 128]
                            src = tph[n % 2][:, 0:512].rearrange("p (b t) -> p b t", b=4)
                            ins = sc.copy(dst, src)
                            k.bump(ins, "sS", mark=f"f0iqe_{n}")
                    for dblk in range(NH):
                        nn = tb * NH + dblk
                        k.wt(sc, f"f0dn_{tb}_{dblk}")
                        half = nn % 2
                        ins = sc.activation(GF[:], DN[:, half * 1024:half * 1024 + 1024], AF.Silu)
                        k.bump(ins, "sS", mark=f"f0sil_{tb}_{dblk}")
                        k.wt(sc, f"f0tg_{tb}_{dblk}")
                        if nn >= 1:
                            pn = nn - 1
                            k.wt(sc, f"f0gsfree_{pn//NH}_{pn%NH}")
                        ins = sc.copy(GS[:], TP2[:])
                        k.bump(ins, "sS", mark=f"f0gte_{tb}_{dblk}")
                        k.marks[f"f0gse_{tb}_{dblk}"] = k.marks[f"f0gte_{tb}_{dblk}"]

            self.phase("ffn0B", {"tensor": tenf, "vector": vecf, "scalar": scaf,
                                 "sync": syncf, "gpsimd": gpsf})

    def phase_ffn1gu(self):
        k = self
        WGU = [k.Cp[:, i * 4096:(i + 1) * 4096] for i in range(4)]
        WG = [WGU[i][:, 0:2048] for i in range(4)]
        WU = [WGU[i][:, 2048:4096] for i in range(4)]
        GG = k.P32[:, 0:2048]
        UU = k.P32[:, 2048:4096]
        HF = k.Cp[:, 16384:18432]

        with self.nc.psum_tensor("F1G", [128, 2048], F32) as GP, \
             self.nc.psum_tensor("F1U", [128, 2048], F32) as UP:
            def syncf(sy):
                sy.dma_start(out=k.Bp[:].rearrange("p (h t) -> p h t", h=NH),
                             in_=k.xn_h.rearrange("h p t -> p h t")).then_inc(k.sem("ld"), 16)
                k.cnt["ld"] = k.cnt.get("ld", 0) + 16
                k.wtv(sy, "ld", k.cnt.get("ld", 0))
                k.bump(sy.nop(), "sY", mark="f1_xn")
                for ft in range(NFT):
                    k.wt(sy, f"f1h_{ft}")
                    sy.dma_start(out=k.hf_h[ft], in_=HF[:]).then_inc(k.sem("st"), 16)
                    k.bump(None, "st", 16)
                    k.wtv(sy, "st", k.cnt.get("st", 0))
                    k.bump(sy.nop(), "sY", mark=f"f1hfree_{ft}")
                k.bump(sy.nop(), "sY", mark="f1gu_done")

            def gpsf(g):
                for ft in range(NFT):
                    if ft >= 4:
                        k.wt(g, f"f1mm_{ft-4}")
                    g.dma_start(out=WGU[ft % 4][:], in_=k.wgu1[ft]).then_inc(k.sem(f"dw{ft%4}"), 16)
                    k.bump(None, f"dw{ft%4}", 16, mark=f"f1w_{ft}")

            def tenf(t):
                k.wt(t, "f1_xn")
                for ft in range(NFT):
                    k.wt(t, f"f1w_{ft}")
                    if ft >= 1:
                        k.wt(t, f"f1ue_{ft-1}")
                        k.wt(t, f"f1gg_{ft-1}")
                    for ht in range(NH):
                        for tc in range(4):
                            t.matmul(GP[:, tc * 512:(tc + 1) * 512],
                                     WG[ft % 4][:, ht * 128:(ht + 1) * 128],
                                     k.Bp[:, ht * 2048 + tc * 512: ht * 2048 + tc * 512 + 512],
                                     start=(ht == 0), stop=(ht == NH - 1))
                    for ht in range(NH):
                        for tc in range(4):
                            ins = t.matmul(UP[:, tc * 512:(tc + 1) * 512],
                                           WU[ft % 4][:, ht * 128:(ht + 1) * 128],
                                           k.Bp[:, ht * 2048 + tc * 512: ht * 2048 + tc * 512 + 512],
                                           start=(ht == 0), stop=(ht == NH - 1))
                    k.bump(ins, "sT", mark=f"f1mm_{ft}")

            def scaf(sc):
                for ft in range(NFT):
                    k.wt(sc, f"f1mm_{ft}")
                    if ft >= 1:
                        k.wt(sc, f"f1h_{ft-1}")
                    ins = sc.activation(GG[:], GP[:], AF.Silu, scale=k.gs_s[:, ft:ft + 1])
                    k.bump(ins, "sS", mark=f"f1gg_{ft}")

            def vecf(v):
                for ft in range(NFT):
                    k.wt(v, f"f1mm_{ft}")
                    ins = v.tensor_scalar_mul(UU[:], UP[:], k.us_s[:, ft:ft + 1])
                    k.bump(ins, "sV", mark=f"f1ue_{ft}")
                    k.wt(v, f"f1gg_{ft}")
                    if ft >= 1:
                        k.wt(v, f"f1hfree_{ft-1}")
                    ins = v.tensor_tensor(HF[:], GG[:], UU[:], OP.mult)
                    k.bump(ins, "sV", mark=f"f1h_{ft}")

            self.phase("ffn1gu", {"tensor": tenf, "vector": vecf, "scalar": scaf,
                                  "sync": syncf, "gpsimd": gpsf})

    def phase_ffn1dn(self):
        k = self
        hfm = [k.Bp[:, ft * 1024:(ft + 1) * 1024] for ft in range(32)] + \
              [k.A[:, 20480 + (ft - 32) * 1024: 20480 + (ft - 31) * 1024] for ft in range(32, NFT)]
        WD = [k.A[:, 0:5632], k.A[:, 5632:11264]]
        GF = k.Cp[:, 0:1024]
        GS = k.Cp[:, 1024:2048]

        with self.nc.psum_tensor("F1D", [128, 2048], F32) as DN, \
             self.nc.psum_tensor("F1T", [128, 1024], BF16) as TP:
            def syncf(sy):
                for tb in range(2):
                    if tb == 1:
                        k.wt(sy, "f1dn_0_15")
                    for ft in range(NFT):
                        n = tb * NFT + ft
                        sy.dma_start(out=hfm[ft][:], in_=k.hf_h[ft][:, tb * 1024:(tb + 1) * 1024]
                                     ).then_inc(k.sem(f"dx{n%2}"), 16)
                        k.bump(None, f"dx{n%2}", 16)
                    k.wtv(sy, "dx0", k.cnt.get("dx0", 0))
                    k.wtv(sy, "dx1", k.cnt.get("dx1", 0))
                    k.bump(sy.nop(), "sY", mark=f"f1hl_all_{tb}")
                    for dblk in range(NH):
                        k.wt(sy, f"f1gte_{tb}_{dblk}")
                        for b in range(8):
                            tsub = tb * 8 + b
                            sy.dma_start(out=k.g_hb[tsub][:, dblk * 128:(dblk + 1) * 128],
                                         in_=GS[:, b * 128:(b + 1) * 128]).then_inc(k.sem("st"), 16)
                            k.bump(None, "st", 16)
                        k.wtv(sy, "st", k.cnt.get("st", 0))
                        k.bump(sy.nop(), "sY", mark=f"f1gsfree_{tb}_{dblk}")
                k.wtv(sy, "st", k.cnt.get("st", 0))
                k.bump(sy.nop(), "sY", mark="f1dn_done")

            def gpsf(g):
                for tb in range(2):
                    for dblk in range(NH):
                        n = tb * NH + dblk
                        if n >= 2:
                            pn = n - 2
                            k.wt(g, f"f1dn_{pn//NH}_{pn%NH}")
                        g.dma_start(out=WD[n % 2][:], in_=k.wd1[dblk]).then_inc(k.sem(f"dw{n%2}"), 16)
                        k.bump(None, f"dw{n%2}", 16, mark=f"f1wd_{tb}_{dblk}")

            def tenf(t):
                for tb in range(2):
                    k.wt(t, f"f1hl_all_{tb}")
                    for dblk in range(NH):
                        nn = tb * NH + dblk
                        k.wt(t, f"f1wd_{tb}_{dblk}")
                        if nn >= 2:
                            pn = nn - 2
                            k.wt(t, f"f1sil_{pn//NH}_{pn%NH}")
                        half = nn % 2
                        for ft in range(NFT):
                            for tc in range(2):
                                ins = t.matmul(DN[:, half * 1024 + tc * 512: half * 1024 + tc * 512 + 512],
                                               WD[nn % 2][:, ft * 128:(ft + 1) * 128],
                                               hfm[ft][:, tc * 512:(tc + 1) * 512],
                                               start=(ft == 0), stop=(ft == NFT - 1))
                        k.bump(ins, "sT", mark=f"f1dn_{tb}_{dblk}")
                        k.wt(t, f"f1sil_{tb}_{dblk}")
                        if nn >= 1:
                            pn = nn - 1
                            k.wt(t, f"f1gte_{pn//NH}_{pn%NH}")
                        for b in range(8):
                            ins = t.transpose(TP[:, b * 128:(b + 1) * 128],
                                              GF[:, b * 128:(b + 1) * 128], k.ident[:])
                        k.bump(ins, "sT", mark=f"f1tg_{tb}_{dblk}")

            def scaf(sc):
                for tb in range(2):
                    for dblk in range(NH):
                        nn = tb * NH + dblk
                        k.wt(sc, f"f1dn_{tb}_{dblk}")
                        half = nn % 2
                        ins = sc.activation(GF[:], DN[:, half * 1024:half * 1024 + 1024],
                                            AF.Silu, scale=k.ds_s[:, dblk:dblk + 1])
                        k.bump(ins, "sS", mark=f"f1sil_{tb}_{dblk}")
                        k.wt(sc, f"f1tg_{tb}_{dblk}")
                        if nn >= 1:
                            pn = nn - 1
                            k.wt(sc, f"f1gsfree_{pn//NH}_{pn%NH}")
                        ins = sc.copy(GS[:], TP[:])
                        k.bump(ins, "sS", mark=f"f1gte_{tb}_{dblk}")

            self.phase("ffn1dn", {"tensor": tenf, "scalar": scaf, "sync": syncf, "gpsimd": gpsf})

    def phase_final(self):
        k = self
        XF = [k.P32[:, 0:2048], k.P32[:, 2048:4096]]
        SGF = k.P32[:, 4096:6144]
        SGS = [k.Cp[:, 0:2048], k.Cp[:, 2048:4096]]
        GGS = [k.Cp[:, 4096:6144], k.Cp[:, 6144:8192]]

        def syncf(sy):
            for i in range(NT):
                sy.dma_start(out=XF[i % 2][:], in_=k.x1_h[i]).then_inc(k.sem(f"dx{i%2}"), 16)
                sy.dma_start(out=SGS[i % 2][:], in_=k.s_hb[i]).then_inc(k.sem(f"dx{i%2}"), 16)
                sy.dma_start(out=GGS[i % 2][:], in_=k.g_hb[i]).then_inc(k.sem(f"dx{i%2}"), 16)
                k.bump(None, f"dx{i%2}", 48, mark=f"fin_ld_{i}")
                k.wt(sy, f"fin_x2_{i}")
                sy.dma_start(out=k.out[i * 128:(i + 1) * 128, :], in_=XF[i % 2][:]
                             ).then_inc(k.sem("out"), 16)
                k.cnt["out"] = k.cnt.get("out", 0) + 16
                k.wtv(sy, "out", k.cnt.get("out", 0))
            k.wtv(sy, "out", k.cnt.get("out", 0))

        def vecf(v):
            for i in range(NT):
                k.wt(v, f"fin_ld_{i}")
                v.tensor_tensor(SGF[:], SGS[i % 2][:], GGS[i % 2][:], OP.mult)
                ins = v.tensor_tensor(XF[i % 2][:], XF[i % 2][:], SGF[:], OP.add)
                k.bump(ins, "sV", mark=f"fin_x2_{i}")

        self.phase("final", {"vector": vecf, "sync": syncf})

    def phase_flush(self):
        k = self

        def syncf(sy):
            if k.cnt.get("st", 0):
                k.wtv(sy, "st", k.cnt.get("st", 0))

        self.phase("flush", {"sync": syncf})


def _prep_inputs(ins):
    f32 = np.float32
    wqkvT = np.ascontiguousarray(ins["L0_attn_w_qkv"].astype(f32).T)
    woT0 = np.ascontiguousarray(ins["L0_attn_w_o"].astype(f32).T)
    w13T = np.ascontiguousarray(ins["L0_ffn_w13"].astype(f32).T)
    w2T = np.ascontiguousarray(ins["L0_ffn_w2"].astype(f32).T)
    qwT = np.ascontiguousarray(ins["L1_attn_q_w"].astype(f32).T)
    kwT = np.ascontiguousarray(ins["L1_attn_k_w"].astype(f32).T)
    vwT = np.ascontiguousarray(ins["L1_attn_v_w"].astype(f32).T)
    owT = np.ascontiguousarray(ins["L1_attn_o_w"].astype(f32).T)
    gwT = np.ascontiguousarray(ins["L1_ffn_gate_w"].astype(f32).T)
    uwT = np.ascontiguousarray(ins["L1_ffn_up_w"].astype(f32).T)
    dwT = np.ascontiguousarray(ins["L1_ffn_down_w"].astype(f32).T)

    d = {}
    d["wqk0"] = pack_lhsT(wqkvT[:, 0:4096])
    d["wv0"] = pack_lhsT(wqkvT[:, 4096:6144])
    d["wo0"] = pack_rhs(woT0)
    d["w13"] = pack_rhs(w13T)
    d["w2p"] = pack_lhsT(w2T)
    d["nwbi"] = np.ascontiguousarray(
        np.broadcast_to(ins["L0_ffn_norm_w"].astype(f32)[None, :], (128, F0)))
    d["wq1"] = pack_lhsT(qwT)
    d["wk1"] = pack_lhsT(kwT)
    d["wv1"] = pack_lhsT(vwT)
    d["wo1"] = pack_rhs(owT)
    d["wgu1"] = np.ascontiguousarray(
        np.concatenate([pack_lhsT(gwT), pack_lhsT(uwT)], axis=2))
    d["wd1"] = pack_lhsT(dwT)
    d["gsp"] = np.ascontiguousarray(ins["L1_ffn_gate_s"].astype(f32).reshape(NFT, 128).T)
    d["usp"] = np.ascontiguousarray(ins["L1_ffn_up_s"].astype(f32).reshape(NFT, 128).T)
    d["dsp"] = np.ascontiguousarray(ins["L1_ffn_down_s"].astype(f32).reshape(NH, 128).T)
    d["iden"] = np.eye(128, dtype=BF)
    d["mskin"] = make_masks()
    cst = np.zeros((128, 8), np.float32)
    cst[:, 0] = 1e-5
    d["cst"] = cst
    return d


def kernel(**inputs):
    shared = _prep_inputs(inputs)
    x_all = np.ascontiguousarray(inputs["L0_attn_hidden_states"].astype(np.float32))
    kb = KB()
    nc = kb.build()
    in_maps = []
    for b in range(8):
        m = dict(shared)
        m["x0"] = x_all[b]
        in_maps.append(m)
    res = run_bass_kernel_spmd(nc, in_maps, core_ids=list(range(8)))
    return np.stack([r["outp"] for r in res.results], axis=0).astype(np.float32)


# version 18
# speedup vs baseline: 1.1944x; 1.0216x over previous
import numpy as np
import ml_dtypes
from contextlib import ExitStack

import concourse.bass as bass
import concourse.mybir as mybir
from concourse.alu_op_type import AluOpType as OP
from concourse.bass_utils import run_bass_kernel_spmd

F32 = mybir.dt.float32
BF16 = mybir.dt.bfloat16
I32 = mybir.dt.int32
AF = mybir.ActivationFunctionType
AX = mybir.AxisListType.X

S = 2048
D = 2048
NT = 16
NH = 16
HEADS = 16
F0 = 5632
NFT = 44
NFC = 11
SCALE = 1.0 / float(np.sqrt(np.float32(128.0)))
NEGBIG = -1.0e30
BF = ml_dtypes.bfloat16
ENGS = ("tensor", "vector", "scalar", "sync", "gpsimd")
DBG_SKIP = set()


def pack_lhsT(wT):
    H, O = wT.shape
    return np.ascontiguousarray(
        wT.reshape(H // 128, 128, O // 128, 128).transpose(2, 1, 0, 3).reshape(O // 128, 128, H))


def pack_rhs(wT):
    H, F = wT.shape
    return np.ascontiguousarray(wT.reshape(H // 128, 128, F))


def make_masks():
    out = np.zeros((128, 4 * 512), dtype=BF)
    r = np.arange(128)[:, None]
    c = np.arange(512)[None, :]
    for oi in range(4):
        out[:, oi * 512:(oi + 1) * 512] = np.where(c <= oi * 128 + r, 0.0, NEGBIG).astype(BF)
    return out


class _MockIns:
    def then_inc(self, *a, **kw):
        return self


class _MockEng:
    def __getattr__(self, name):
        def f(*a, **kw):
            return _MockIns()
        return f


AUTOSEM = {"vector": "sVa", "scalar": "sSa", "gpsimd": "sGa"}
AUTOMAP = {"sV": "sVa", "sS": "sSa", "sG": "sGa"}
_FENCE_SKIP = {"wait_ge", "dma_start", "dma_start_transpose", "indirect_dma_start",
               "nop", "sem_inc", "sem_clear"}


class _Fence:

    def __init__(self, kb, eng, sname):
        self._kb = kb
        self._eng = eng
        self._sname = sname

    def __getattr__(self, a):
        real = getattr(self._eng, a)
        if a in _FENCE_SKIP or not callable(real):
            return real

        def f(*args, **kw):
            kb = self._kb
            sn = self._sname
            kb.sem(sn)
            self._eng.wait_ge(kb.sem_h[sn], kb.cnt.get(sn, 0))
            ins = real(*args, **kw)
            kb.cnt[sn] = kb.cnt.get(sn, 0) + 1
            if not isinstance(ins, _MockIns):
                ins.then_inc(kb.sem_h[sn], 1)
            return ins
        return f


class KB:
    def __init__(self, dbg_upto=None, dbg=False):
        self.nc = bass.Bass("TRN2", target_bir_lowering=False)
        self.dbg = dbg
        self.dbg_upto = 99 if dbg_upto is None else dbg_upto
        self.ctx = ExitStack()
        self.sem_h = {}
        self.cnt = {}
        self.marks = {}
        self.mock = False

    def sem(self, n):
        if n not in self.sem_h:
            self.sem_h[n] = self.ctx.enter_context(self.nc.semaphore(n))
        self.cnt.setdefault(n, 0)
        return self.sem_h[n]

    def bump(self, ins, n, amt=1, mark=None):
        if n in AUTOMAP:
            sn = AUTOMAP[n]
            self.sem(sn)
            v = self.cnt.get(sn, 0)
            if mark is not None:
                self.marks[mark] = (sn, v)
            return v
        self.sem(n)
        self.cnt[n] += amt
        if mark is not None:
            self.marks[mark] = (n, self.cnt[n])
        if ins is not None and not isinstance(ins, _MockIns):
            ins.then_inc(self.sem_h[n], amt)
        return self.cnt[n]

    def wt(self, eng, mark):
        got = self.marks.get(mark)
        if got is None:
            if self.mock:
                return
            raise KeyError(f"unresolved mark {mark}")
        n, v = got
        eng.wait_ge(self.sem(n), v)

    def wtv(self, eng, n, v):
        eng.wait_ge(self.sem(n), v)

    def phase(self, name, bodies):
        def wrap(e, engobj):
            return _Fence(self, engobj, AUTOSEM[e]) if e in AUTOSEM else engobj

        snap = dict(self.cnt)
        self.mock = True
        for e in ENGS:
            if e in bodies:
                bodies[e](wrap(e, _MockEng()))
        mock_end = dict(self.cnt)
        self.cnt = snap
        self.mock = False
        with self.nc.Block(name, no_gpsimd_drain=False) as blk:
            for e in ENGS:
                if e in bodies:
                    body = bodies[e]
                    getattr(blk, e)(lambda eng, b=body, e2=e: b(wrap(e2, eng)))
        for kk2 in set(mock_end) | set(self.cnt):
            assert mock_end.get(kk2, 0) == self.cnt.get(kk2, 0), (
                f"phase {name}: count drift on {kk2}: mock={mock_end.get(kk2, 0)} "
                f"real={self.cnt.get(kk2, 0)}")

    def build(self):
        nc = self.nc
        dbgkind = "ExternalOutput" if self.dbg else "Internal"
        self.x0 = nc.dram_tensor("x0", [S, D], F32, kind="ExternalInput").ap()
        self.wqk0 = nc.dram_tensor("wqk0", [32, 128, D], F32, kind="ExternalInput").ap()
        self.wv0 = nc.dram_tensor("wv0", [16, 128, D], F32, kind="ExternalInput").ap()
        self.wo0 = nc.dram_tensor("wo0", [NH, 128, D], F32, kind="ExternalInput").ap()
        self.w13 = nc.dram_tensor("w13", [NH, 128, 2 * F0], F32, kind="ExternalInput").ap()
        self.w2p = nc.dram_tensor("w2p", [NH, 128, F0], F32, kind="ExternalInput").ap()
        self.nwbi = nc.dram_tensor("nwbi", [128, F0], F32, kind="ExternalInput").ap()
        self.wq1 = nc.dram_tensor("wq1", [NH, 128, D], F32, kind="ExternalInput").ap()
        self.wk1 = nc.dram_tensor("wk1", [1, 128, D], F32, kind="ExternalInput").ap()
        self.wv1 = nc.dram_tensor("wv1", [1, 128, D], F32, kind="ExternalInput").ap()
        self.wo1 = nc.dram_tensor("wo1", [NH, 128, D], F32, kind="ExternalInput").ap()
        self.wgu1 = nc.dram_tensor("wgu1", [NFT, 128, 2 * D], F32, kind="ExternalInput").ap()
        self.wd1 = nc.dram_tensor("wd1", [NH, 128, F0], F32, kind="ExternalInput").ap()
        self.gsp = nc.dram_tensor("gsp", [128, NFT], F32, kind="ExternalInput").ap()
        self.usp = nc.dram_tensor("usp", [128, NFT], F32, kind="ExternalInput").ap()
        self.dsp = nc.dram_tensor("dsp", [128, NH], F32, kind="ExternalInput").ap()
        self.iden = nc.dram_tensor("iden", [128, 128], BF16, kind="ExternalInput").ap()
        self.mskin = nc.dram_tensor("mskin", [128, 2048], BF16, kind="ExternalInput").ap()
        self.cst = nc.dram_tensor("cst", [128, 8], F32, kind="ExternalInput").ap()

        self.out = nc.dram_tensor("outp", [S, D], F32, kind="ExternalOutput").ap()

        self.codes_h = nc.dram_tensor("codes_h", [NH, 128, S], BF16, kind=dbgkind).ap()
        self.xn_h = nc.dram_tensor("xn_h", [NH, 128, S], BF16, kind=dbgkind).ap()
        self.q_hb = nc.dram_tensor("q_hb", [HEADS, 128, S], BF16, kind=dbgkind).ap()
        self.k_hb = nc.dram_tensor("k_hb", [NH, 128, S], BF16, kind=dbgkind).ap()
        self.o_hb = nc.dram_tensor("o_hb", [HEADS, 128, S], BF16, kind=dbgkind).ap()
        self.v_hb = nc.dram_tensor("v_hb", [NT, 128, D], BF16, kind=dbgkind).ap()
        self.s_hb = nc.dram_tensor("s_hb", [NT, 128, D], BF16, kind=dbgkind).ap()
        self.g_hb = nc.dram_tensor("g_hb", [NT, 128, D], BF16, kind=dbgkind).ap()
        self.x1_h = nc.dram_tensor("x1_h", [NT, 128, D], F32, kind=dbgkind).ap()
        self.gn_hb = nc.dram_tensor("gn_hb", [S, F0], BF16, kind=dbgkind).ap()
        self.hf_h = nc.dram_tensor("hf_h", [NFT, 128, S], BF16, kind=dbgkind).ap()

        ctx = self.ctx
        self.P16 = ctx.enter_context(nc.sbuf_tensor("P16", [128, 86016], BF16))
        self.P32 = ctx.enter_context(nc.sbuf_tensor("P32", [128, 8192], F32))
        self.PI = ctx.enter_context(nc.sbuf_tensor("PI", [128, 1024], I32))
        self.ident = ctx.enter_context(nc.sbuf_tensor("identS", [128, 128], BF16))
        self.sinv = ctx.enter_context(nc.sbuf_tensor("sinv", [128, NT], F32))
        self.sinv3 = ctx.enter_context(nc.sbuf_tensor("sinv3", [128, NT], F32))
        self.ssqB = ctx.enter_context(nc.sbuf_tensor("ssqB", [128, NT], F32))
        self.mxgB = ctx.enter_context(nc.sbuf_tensor("mxgB", [128, NT], F32))
        self.aa8 = ctx.enter_context(nc.sbuf_tensor("aa8", [128, NT], F32))
        self.i2v = ctx.enter_context(nc.sbuf_tensor("i2v", [128, NT], F32))
        self.gs_s = ctx.enter_context(nc.sbuf_tensor("gs_s", [128, NFT], F32))
        self.gsh_s = ctx.enter_context(nc.sbuf_tensor("gsh_s", [128, NFT], F32))
        self.us_s = ctx.enter_context(nc.sbuf_tensor("us_s", [128, NFT], F32))
        self.ds_s = ctx.enter_context(nc.sbuf_tensor("ds_s", [128, NH], F32))
        self.epsb = ctx.enter_context(nc.sbuf_tensor("epsb", [128, 1], F32))
        self.st1 = ctx.enter_context(nc.sbuf_tensor("st1", [128, 8], F32))
        self.st2 = ctx.enter_context(nc.sbuf_tensor("st2", [128, 8], F32))
        self.st3 = ctx.enter_context(nc.sbuf_tensor("st3", [128, 8], F32))
        self.st4 = ctx.enter_context(nc.sbuf_tensor("st4", [128, 8], F32))

        self.A = self.P16[:, 0:32768]
        self.Bp = self.P16[:, 32768:65536]
        self.Cp = self.P16[:, 65536:86016]

        self.phase_setup()
        steps = [self.phase_ln0_, self.phase_qkv0_, self.phase_af0, self.phase_wo0_,
                 self.phase_ffn0S, self.phase_ffn0B, self.phase_ln1_, self.phase_qkv1_,
                 self.phase_af1, self.phase_wo1_, self.phase_ffn1dn, self.phase_final]
        for i, f in enumerate(steps):
            if self.dbg_upto >= i + 1:
                f()
        if self.dbg_upto < len(steps):
            self.phase_flush()
        self.ctx.close()
        return self.nc

    def phase_ln0_(self):
        self.phase_ln(0)

    def phase_ln1_(self):
        self.phase_ln(1)

    def phase_qkv0_(self):
        self.phase_qkv(0)

    def phase_qkv1_(self):
        self.phase_qkv(1)

    def phase_wo0_(self):
        self.phase_wo(0)

    def phase_wo1_(self):
        self.phase_wo(1)

    def phase_setup(self):
        k = self

        def syncf(sy):
            for dst, src in [(k.ident[:], k.iden), (k.msk(), k.mskin),
                             (k.gs_s[:], k.gsp), (k.us_s[:], k.usp),
                             (k.ds_s[:], k.dsp), (k.st3[:], k.cst)]:
                sy.dma_start(out=dst, in_=src).then_inc(k.sem("ld"), 16)
                k.cnt["ld"] = k.cnt.get("ld", 0) + 16
            k.wtv(sy, "ld", k.cnt.get("ld", 0))
            k.bump(sy.nop(), "sY", mark="setup_done")

        def vecf(v):
            k.wt(v, "setup_done")
            k.bump(v.tensor_copy(k.epsb[:], k.st3[:, 0:1]), "sV", mark="eps_ready")
            k.bump(v.tensor_scalar_mul(k.gsh_s[:], k.gs_s[:], 0.5), "sV", mark="gsh_ready")

        self.phase("setup", {"sync": syncf, "vector": vecf})

    def msk(self):
        return self.Cp[:, 18432:20480]

    def phase_ln(self, L):
        k = self
        XF = [k.P32[:, 0:2048], k.P32[:, 2048:4096]]
        XNF = k.P32[:, 4096:6144]
        RS1 = k.P32[:, 6144:7168]
        RS2 = k.P32[:, 7168:8192]
        SG2 = k.P32[:, 6144:8192]
        XN16 = k.Cp[:, 0:2048]
        SC16 = k.Cp[:, 2048:4096]
        CFM = k.Cp[:, 4096:6144]
        SGS = [k.Cp[:, 6144:8192], k.Cp[:, 8192:10240]]
        GGS = [k.Cp[:, 10240:12288], k.Cp[:, 12288:14336]]
        DUMP = k.Cp[:, 14336:16384]
        stp = [k.st1, k.st4]

        def SL(p):
            st = stp[p % 2]
            return dict(sums=st[:, 0:1], ssqv=st[:, 1:2], mean=st[:, 2:3], msq=st[:, 3:4],
                        varr=st[:, 4:5], sd=st[:, 5:6], rstd=st[:, 6:7], ssn=st[:, 7:8])
        rmx = k.st2[:, 0:1]; rm2 = k.st2[:, 1:2]; rcp = k.st2[:, 2:3]; rcp127 = k.st2[:, 3:4]
        tmp1 = k.st2[:, 4:5]
        Afm = k.A.rearrange("p (h t) -> p h t", h=NH)

        with self.nc.psum_tensor(f"TP{L}", [128, 2048], BF16) as TP:
            halves = [TP[:, 0:1024], TP[:, 1024:2048]]

            def syncf(sy):
                def load(i):
                    if i >= 2:
                        k.wt(sy, f"ln{L}ssq_{i-2}")
                        k.wt(sy, f"ln{L}xn_{i-2}")
                        if L == 1:
                            k.wt(sy, f"ln{L}stchk_{i-2}")
                    sy.dma_start(out=XF[i % 2][:], in_=k.x0[i * 128:(i + 1) * 128, :]
                                 ).then_inc(k.sem(f"dx{i%2}"), 16)
                    amt = 16
                    if L == 1:
                        sy.dma_start(out=SGS[i % 2][:], in_=k.s_hb[i]).then_inc(k.sem(f"dx{i%2}"), 16)
                        sy.dma_start(out=GGS[i % 2][:], in_=k.g_hb[i]).then_inc(k.sem(f"dx{i%2}"), 16)
                        amt = 48
                    k.bump(None, f"dx{i%2}", amt, mark=f"ln{L}ld_{i}")

                def spill(i):
                    if L == 1:
                        k.wt(sy, f"ln{L}x1_{i}")
                        sy.dma_start(out=k.x1_h[i], in_=XF[i % 2][:]).then_inc(k.sem("st"), 16)
                        k.bump(None, "st", 16)
                        k.wt(sy, f"ln{L}xnev_{i}_1")
                        sy.dma_start(out=k.xn_h.rearrange("h p t -> p h t")[:, :, i * 128:(i + 1) * 128],
                                     in_=Afm[:, :, i * 128:(i + 1) * 128]).then_inc(k.sem("st"), 16)
                        k.bump(None, "st", 16)
                        k.wtv(sy, "st", k.cnt.get("st", 0))
                        k.bump(sy.nop(), "sY", mark=f"ln{L}stchk_{i}")
                    else:
                        k.wt(sy, f"ln{L}cdev_{i}_1")
                        sy.dma_start(out=k.codes_h.rearrange("h p t -> p h t")[:, :, i * 128:(i + 1) * 128],
                                     in_=CFM.rearrange("p (h t) -> p h t", h=NH)).then_inc(k.sem("st"), 16)
                        k.bump(None, "st", 16)
                        k.wtv(sy, "st", k.cnt.get("st", 0))
                        k.bump(sy.nop(), "sY", mark=f"ln{L}stchk_{i}")

                for i in range(NT + 1):
                    if i < NT:
                        load(i)
                    if i >= 1:
                        spill(i - 1)
                k.wtv(sy, "st", k.cnt.get("st", 0))
                k.bump(sy.nop(), "sY", mark=f"ln{L}_done")

            def vecf(v):
                def A(i):
                    x = XF[i % 2]; S = SL(i)
                    k.wt(v, f"ln{L}ld_{i}")
                    if i >= 2:
                        k.wt(v, f"ln{L}mean_{i-2}")
                    if L == 1:
                        v.tensor_tensor(SG2[:], SGS[i % 2][:], GGS[i % 2][:], OP.mult)
                        ins = v.tensor_tensor(x[:], x[:], SG2[:], OP.add)
                        k.bump(ins, "sV", mark=f"ln{L}x1_{i}")
                    ins = v.tensor_reduce(S['sums'], x[:], AX, OP.add)
                    k.bump(ins, "sV", mark=f"ln{L}sum_{i}")

                def B(i):
                    x = XF[i % 2]; S = SL(i)
                    k.wt(v, f"ln{L}msq_{i}")
                    v.tensor_tensor(S['varr'], S['ssn'], S['msq'], OP.subtract)
                    ins = v.tensor_tensor(S['varr'], S['varr'], k.epsb[:], OP.add)
                    k.bump(ins, "sV", mark=f"ln{L}var_{i}")
                    k.wt(v, f"ln{L}sd_{i}")
                    ins = v.reciprocal(S['rstd'], S['sd'])
                    k.bump(ins, "sV", mark=f"ln{L}rstd_{i}")
                    k.wt(v, f"ln{L}rstd_{i}")
                    ins = v.tensor_scalar(XNF[:], x[:], S['mean'], S['rstd'], OP.subtract, OP.mult)
                    k.bump(ins, "sV", mark=f"ln{L}xn_{i}")
                    if L == 1 or "codes" in DBG_SKIP:
                        k.marks[f"ln{L}xfree_{i}"] = k.marks[f"ln{L}xn_{i}"]
                    else:
                        ins = v.tensor_reduce(rmx, XNF[:], AX, OP.max, apply_absolute_value=True)
                        k.bump(ins, "sV", mark=f"ln{L}rmx_{i}")
                        k.wt(v, f"ln{L}rmx_{i}")
                        v.tensor_single_scalar(rm2, rmx, 1e-5, OP.max)
                        ins = v.reciprocal(rcp, rm2)
                        k.bump(ins, "sV", mark=f"ln{L}rcp_{i}")
                        k.wt(v, f"ln{L}rcp_{i}")
                        v.tensor_single_scalar(rcp127, rcp, 127.0, OP.mult)
                        v.tensor_single_scalar(k.sinv[:, i:i + 1], rm2, 1.0 / 127.0, OP.mult)
                        v.tensor_tensor(tmp1, k.sinv[:, i:i + 1], k.sinv[:, i:i + 1], OP.mult)
                        v.tensor_tensor(k.sinv3[:, i:i + 1], tmp1, k.sinv[:, i:i + 1], OP.mult)
                        k.wt(v, f"ln{L}xn16_{i}")
                        for hh in range(2):
                            xnh = XNF[:, hh * 1024:(hh + 1) * 1024]
                            v.tensor_scalar(RS1[:], xnh, rcp127, 0.5, OP.mult, OP.add)
                            v.tensor_copy(k.PI[:], RS1[:])
                            v.tensor_copy(RS2[:], k.PI[:])
                            v.tensor_tensor(xnh, RS2[:], RS1[:], OP.is_gt)
                            ins = v.tensor_tensor(SC16[:, hh * 1024:(hh + 1) * 1024],
                                                  RS2[:], xnh, OP.subtract)
                        k.bump(ins, "sV", mark=f"ln{L}codes_{i}")
                        k.marks[f"ln{L}xfree_{i}"] = k.marks[f"ln{L}codes_{i}"]

                for i in range(NT + 1):
                    if i < NT:
                        A(i)
                    if i >= 1:
                        B(i - 1)

            def scaf(sc):
                def C1(i):
                    S = SL(i)
                    k.wt(sc, f"ln{L}ld_{i}")
                    if L == 1:
                        k.wt(sc, f"ln{L}x1_{i}")
                    ins = sc.activation(DUMP[:], XF[i % 2][:], AF.Square, accum_out=S['ssqv'])
                    k.bump(ins, "sS", mark=f"ln{L}ssq_{i}")

                def C2(i):
                    S = SL(i)
                    k.wt(sc, f"ln{L}sum_{i}")
                    if i >= 2:
                        k.wt(sc, f"ln{L}xn_{i-2}")
                    ins = sc.mul(S['mean'], S['sums'], 1.0 / D)
                    k.bump(ins, "sS", mark=f"ln{L}mean_{i}")
                    sc.mul(S['ssn'], S['ssqv'], 1.0 / D)
                    k.wt(sc, f"ln{L}mean_{i}")
                    ins = sc.activation(S['msq'], S['mean'], AF.Square)
                    k.bump(ins, "sS", mark=f"ln{L}msq_{i}")

                def DD(i):
                    S = SL(i)
                    k.wt(sc, f"ln{L}var_{i}")
                    ins = sc.activation(S['sd'], S['varr'], AF.Sqrt)
                    k.bump(ins, "sS", mark=f"ln{L}sd_{i}")
                    k.wt(sc, f"ln{L}xn_{i}")
                    ins = sc.copy(XN16[:], XNF[:])
                    k.bump(ins, "sS", mark=f"ln{L}xn16_{i}")
                    if "xpose" not in DBG_SKIP:
                        for b in range(2):
                            k.wt(sc, f"ln{L}txn_{i}_{b}")
                            dst = Afm[:, b * 8:(b + 1) * 8, i * 128:(i + 1) * 128]
                            ins = sc.copy(dst, halves[b].rearrange("p (h t) -> p h t", h=8))
                            k.bump(ins, "sS", mark=f"ln{L}xnev_{i}_{b}")
                    if L == 0 and "codes" not in DBG_SKIP and "xpose" not in DBG_SKIP:
                        if i >= 1:
                            k.wt(sc, f"ln{L}stchk_{i-1}")
                        for b in range(2):
                            k.wt(sc, f"ln{L}tcd_{i}_{b}")
                            dst = CFM.rearrange("p (h t) -> p h t", h=NH)[:, b * 8:(b + 1) * 8, :]
                            ins = sc.copy(dst, halves[b].rearrange("p (h t) -> p h t", h=8))
                            k.bump(ins, "sS", mark=f"ln{L}cdev_{i}_{b}")

                for i in range(NT + 1):
                    if i < NT:
                        C1(i)
                        C2(i)
                    if i >= 1:
                        DD(i - 1)

            def tenf(t):
                if "xpose" in DBG_SKIP:
                    return
                tphist = {}
                for i in range(NT):
                    k.wt(t, f"ln{L}xn16_{i}")
                    for b in range(2):
                        if b in tphist:
                            k.wt(t, tphist[b])
                        for j in range(8):
                            ins = t.transpose(halves[b][:, j * 128:(j + 1) * 128],
                                              XN16[:, (b * 8 + j) * 128:(b * 8 + j) * 128 + 128],
                                              k.ident[:])
                        k.bump(ins, "sT", mark=f"ln{L}txn_{i}_{b}")
                        tphist[b] = f"ln{L}xnev_{i}_{b}"
                    if L == 0 and "codes" not in DBG_SKIP:
                        k.wt(t, f"ln{L}codes_{i}")
                        for b in range(2):
                            k.wt(t, tphist[b])
                            for j in range(8):
                                ins = t.transpose(halves[b][:, j * 128:(j + 1) * 128],
                                                  SC16[:, (b * 8 + j) * 128:(b * 8 + j) * 128 + 128],
                                                  k.ident[:])
                            k.bump(ins, "sT", mark=f"ln{L}tcd_{i}_{b}")
                            tphist[b] = f"ln{L}cdev_{i}_{b}"

            self.phase(f"ln{L}", {"tensor": tenf, "vector": vecf, "scalar": scaf, "sync": syncf})

    def phase_qkv(self, L):
        k = self
        WS = [k.Cp[:, 0:2048], k.Cp[:, 2048:4096]]
        QS = k.Cp[:, 4096:6144]
        VTS = k.Cp[:, 6144:8192]
        if L == 0:
            obs = [("q", i, k.wqk0[i]) for i in range(16)] + \
                  [("k", i, k.wqk0[16 + i]) for i in range(16)] + \
                  [("v", i, k.wv0[i]) for i in range(16)]
        else:
            obs = [("q", i, k.wq1[i]) for i in range(16)] + \
                  [("k", 0, k.wk1[0]), ("v", 0, k.wv1[0])]

        with self.nc.psum_tensor(f"QP{L}", [128, 2048], F32) as PB, \
             self.nc.psum_tensor(f"QT{L}", [128, 2048], BF16) as TP:
            tphalves = [TP[:, 0:1024], TP[:, 1024:2048]]

            def gpsf(g):
                for n, (kind, ob, src) in enumerate(obs):
                    if n >= 2:
                        k.wt(g, f"qkv{L}free_{n-2}")
                    g.dma_start(out=WS[n % 2][:], in_=src).then_inc(k.sem(f"dw{n%2}"), 16)
                    k.bump(None, f"dw{n%2}", 16, mark=f"qkv{L}w_{n}")

            def tenf(t):
                tphist = {}
                for n, (kind, ob, src) in enumerate(obs):
                    k.wt(t, f"qkv{L}w_{n}")
                    for tc in range(4):
                        if n >= 1:
                            k.wt(t, f"qkv{L}ev_{n-1}_{tc}")
                        pq = PB[:, tc * 512:(tc + 1) * 512]
                        for ht in range(NH):
                            ins = t.matmul(pq, WS[n % 2][:, ht * 128:(ht + 1) * 128],
                                           k.A[:, ht * 2048 + tc * 512: ht * 2048 + tc * 512 + 512],
                                           start=(ht == 0), stop=(ht == NH - 1))
                        k.bump(ins, "sT", mark=f"qkv{L}mm_{n}_{tc}")
                    if kind == "v":
                        k.wt(t, f"qkv{L}ev_{n}_3")
                        for half in range(2):
                            if half in tphist:
                                k.wt(t, tphist[half])
                            for j in range(8):
                                tt = half * 8 + j
                                ins = t.transpose(tphalves[half][:, j * 128:(j + 1) * 128],
                                                  QS[:, tt * 128:(tt + 1) * 128], k.ident[:])
                            k.bump(ins, "sT", mark=f"qkv{L}vt_{n}_{half}")
                            tphist[half] = f"qkv{L}vte_{n}_{half}"

            def vecf(v):
                for n, (kind, ob, src) in enumerate(obs):
                    for tc in range(4):
                        k.wt(v, f"qkv{L}mm_{n}_{tc}")
                        pq = PB[:, tc * 512:(tc + 1) * 512]
                        if kind == "q":
                            if n >= 1 and tc == 0 and obs[n - 1][0] in ("q", "k"):
                                k.wt(v, f"qkv{L}qst_{n-1}")
                            ins = v.tensor_single_scalar(QS[:, tc * 512:(tc + 1) * 512], pq, SCALE, OP.mult)
                        elif kind == "k":
                            if L == 0:
                                if tc == 0:
                                    pv = [m for m, (kk2, _, _) in enumerate(obs[:n]) if kk2 in ("q", "k")]
                                    if pv:
                                        k.wt(v, f"qkv{L}qst_{pv[-1]}")
                                dst = QS[:, tc * 512:(tc + 1) * 512]
                            else:
                                dst = k.Cp[:, 8192 + tc * 512: 8192 + tc * 512 + 512]
                            ins = v.tensor_copy(dst, pq)
                        else:
                            if tc == 0:
                                users = ("q", "k", "v") if L == 0 else ("q", "v")
                                pv = [m for m, (kk2, _, _) in enumerate(obs[:n]) if kk2 in users]
                                if pv:
                                    pn = pv[-1]
                                    if obs[pn][0] == "v":
                                        k.wt(v, f"qkv{L}vt_{pn}_1")
                                    else:
                                        k.wt(v, f"qkv{L}qst_{pn}")
                            ins = v.tensor_copy(QS[:, tc * 512:(tc + 1) * 512], pq)
                        k.bump(ins, "sV", mark=f"qkv{L}ev_{n}_{tc}")
                    k.marks[f"qkv{L}free_{n}"] = k.marks[f"qkv{L}ev_{n}_3"]

            def scaf(sc):
                for n, (kind, ob, src) in enumerate(obs):
                    if kind != "v":
                        continue
                    for half in range(2):
                        k.wt(sc, f"qkv{L}vt_{n}_{half}")
                        prevv = [m for m, (kk, _, _) in enumerate(obs[:n]) if kk == "v"]
                        if prevv:
                            k.wt(sc, f"qkv{L}vstf_{prevv[-1]}_{half}")
                        ins = sc.copy(VTS[:, half * 1024:(half + 1) * 1024], tphalves[half][:])
                        k.bump(ins, "sS", mark=f"qkv{L}vte_{n}_{half}")

            def syncf(sy):
                for n, (kind, ob, src) in enumerate(obs):
                    if kind == "q" or (kind == "k" and L == 0):
                        k.wt(sy, f"qkv{L}ev_{n}_3")
                        dst = k.q_hb[ob] if kind == "q" else k.k_hb[ob]
                        sy.dma_start(out=dst, in_=QS[:]).then_inc(k.sem("st"), 16)
                        k.bump(None, "st", 16)
                        k.wtv(sy, "st", k.cnt.get("st", 0))
                        k.bump(sy.nop(), "sY", mark=f"qkv{L}qst_{n}")
                    elif kind == "v":
                        for half in range(2):
                            k.wt(sy, f"qkv{L}vte_{n}_{half}")
                            dst = k.v_hb.rearrange("n p t -> p n t")[:, half * 8:(half + 1) * 8,
                                                                     ob * 128:(ob + 1) * 128]
                            srcap = VTS[:, half * 1024:(half + 1) * 1024].rearrange("p (n t) -> p n t", n=8)
                            sy.dma_start(out=dst, in_=srcap).then_inc(k.sem("st"), 16)
                            k.bump(None, "st", 16)
                            k.wtv(sy, "st", k.cnt.get("st", 0))
                            k.bump(sy.nop(), "sY", mark=f"qkv{L}vstf_{n}_{half}")
                k.wtv(sy, "st", k.cnt.get("st", 0))
                k.bump(sy.nop(), "sY", mark=f"qkv{L}_done")

            self.phase(f"qkv{L}", {"tensor": tenf, "vector": vecf, "scalar": scaf,
                                   "sync": syncf, "gpsimd": gpsf})

    def phase_att(self, L):
        k = self
        causal = (L == 0)
        QH = [k.Cp[:, 0:2048], k.Cp[:, 2048:4096]]
        VH = [k.Cp[:, 4096:6144], k.Cp[:, 6144:8192]]
        PSBs = ([k.Cp[:, 8192:10240], k.Cp[:, 18432:20480]] if L == 1
                else [k.Cp[:, 8192:10240], k.Cp[:, 8192:10240]])
        PT = k.Cp[:, 10240:18432]
        ngp = [k.st2[:, j:j + 1] for j in range(4)]
        negm_b = [k.st1[:, 0:1], k.st1[:, 6:7]]
        za = k.st1[:, 1:2]; zb = k.st1[:, 2:3]
        zs = k.st1[:, 3:4]; rz = k.st1[:, 4:5]; tmp = k.st1[:, 5:6]

        def nchf(qt):
            return qt // 4 + 1 if causal else 4

        def nregf(qt):
            return 1 if nchf(qt) <= 2 else 2

        regs_of = {}
        rr = 0
        for h in range(HEADS):
            for qt in range(NT):
                n = nregf(qt)
                regs_of[(h, qt)] = [rr, (rr + 1) % 3][:n]
                rr = (rr + n) % 3

        with self.nc.psum_tensor(f"SR0{L}", [128, 1024], F32) as SR0, \
             self.nc.psum_tensor(f"SR1{L}", [128, 1024], F32) as SR1, \
             self.nc.psum_tensor(f"SR2{L}", [128, 1024], F32) as SR2, \
             self.nc.psum_tensor(f"PP{L}", [128, 1024], BF16) as PP, \
             self.nc.psum_tensor(f"OT{L}", [128, 512], F32) as OT:
            SCR = [SR0, SR1, SR2]

            def scslice(h, qt, c):
                rg = regs_of[(h, qt)][c // 2]
                return SCR[rg][:, (c % 2) * 512:(c % 2 + 1) * 512]

            def transp(t, h, qt, ppq):
                nkb = nchf(qt) * 4
                PSB = PSBs[qt % 2]
                k.wt(t, f"a{L}pn_{h}_{qt}")
                for b in range((nkb + 7) // 8):
                    if ppq:
                        for m in ppq[0]:
                            k.wt(t, m)
                    j0, j1 = b * 8, min(nkb, b * 8 + 8)
                    for j in range(j0, j1):
                        ins = t.transpose(PP[:, (j - j0) * 128:(j - j0 + 1) * 128],
                                          PSB[:, j * 128:(j + 1) * 128], k.ident[:])
                    k.bump(ins, "sT", mark=f"a{L}tpb_{h}_{qt}_{b}")
                    nm = "S" if b % 2 == 0 else "V"
                    ppq[:] = [[f"a{L}pte{nm}_{h}_{qt}_{b}"]]
                k.marks[f"a{L}tp_{h}_{qt}"] = k.marks[f"a{L}tpb_{h}_{qt}_{(nkb-1)//8}"]

            def syncf(sy):
                for h in range(HEADS):
                    if h >= 2:
                        k.wt(sy, f"a{L}hdone_{h-2}")
                    sy.dma_start(out=QH[h % 2][:], in_=k.q_hb[h]).then_inc(k.sem(f"dq{h%2}"), 16)
                    c0 = h * 128 if L == 0 else 0
                    sy.dma_start(out=VH[h % 2][:].rearrange("p (n t) -> p n t", n=16),
                                 in_=k.v_hb.rearrange("n p t -> p n t")[:, :, c0:c0 + 128]
                                 ).then_inc(k.sem(f"dq{h%2}"), 16)
                    k.bump(None, f"dq{h%2}", 32, mark=f"a{L}qv_{h}")

            def tenf(t):
                ppq = []
                regfree = {}
                for h in range(HEADS):
                    k.wt(t, f"a{L}qv_{h}")
                    for qt in range(NT):
                        nch = nchf(qt)
                        kbase = h * 2048 if L == 0 else 0
                        regs = regs_of[(h, qt)]
                        for i, rg in enumerate(regs):
                            if rg in regfree:
                                k.wt(t, regfree[rg])
                            for c in range(2 * i, min(nch, 2 * i + 2)):
                                ins = t.matmul(scslice(h, qt, c),
                                               QH[h % 2][:, qt * 128:(qt + 1) * 128],
                                               k.Bp[:, kbase + c * 512: kbase + c * 512 + 512],
                                               start=True, stop=True)
                                k.bump(ins, "sT", mark=f"a{L}scc_{h}_{qt}_{c}")
                            regfree[rg] = f"a{L}exp_{h}_{qt}_{i}"
                        if qt > 0:
                            transp(t, h, qt - 1, ppq)
                            if (qt - 1) % 4 == 3:
                                self._att_pv(t, L, h, (qt - 1) // 4, causal, VH, PT, OT)
                    transp(t, h, NT - 1, ppq)
                    self._att_pv(t, L, h, 3, causal, VH, PT, OT)

            def do_rzpn(v, h, qt):
                nch = nchf(qt)
                nreg = nregf(qt)
                if nreg == 2:
                    k.wt(v, f"a{L}exp_{h}_{qt}_1")
                    ins = v.tensor_tensor(zs, za, zb, OP.add)
                    k.bump(ins, "sV")
                    k.wtv(v, "sVa", k.cnt["sVa"])
                    ins = v.reciprocal(rz, zs)
                else:
                    k.wt(v, f"a{L}exp_{h}_{qt}_0")
                    ins = v.reciprocal(rz, za)
                k.bump(ins, "sV", mark=f"a{L}rz_{h}_{qt}")
                k.wt(v, f"a{L}rz_{h}_{qt}")
                PSB = PSBs[qt % 2]
                ins = v.tensor_scalar_mul(PSB[:, 0:nch * 512], PSB[:, 0:nch * 512], rz)
                k.bump(ins, "sV", mark=f"a{L}pn_{h}_{qt}")

            def vecf(v):
                for h in range(HEADS):
                    for qt in range(NT):
                        nch = nchf(qt)
                        nreg = nregf(qt)
                        regs = regs_of[(h, qt)]
                        off = qt % 4
                        if qt > 0:
                            do_rzpn(v, h, qt - 1)
                        if qt > 1:
                            self._ptevac_v(v, L, h, qt - 2, nchf, PP, PT)
                        for i, rg in enumerate(regs):
                            sp = min(nch, 2 * i + 2) - 2 * i
                            for c in range(2 * i, 2 * i + sp):
                                k.wt(v, f"a{L}scc_{h}_{qt}_{c}")
                            if causal and 2 * i + sp == nch:
                                sl = scslice(h, qt, nch - 1)
                                v.tensor_tensor(sl, sl, k.msk()[:, off * 512:(off + 1) * 512], OP.add)
                            ins = v.tensor_reduce(ngp[i], SCR[rg][:, 0:sp * 512], AX, OP.max, negate=True)
                        k.bump(ins, "sV")
                        k.wtv(v, "sVa", k.cnt["sVa"])
                        negm = negm_b[qt % 2]
                        if nreg == 1:
                            ins = v.tensor_copy(negm, ngp[0])
                        else:
                            ins = v.tensor_tensor(negm, ngp[0], ngp[1], OP.min)
                        k.bump(ins, "sV", mark=f"a{L}neg_{h}_{qt}")
                        if qt % 4 == 1 and qt >= 5:
                            g = qt // 4 - 1
                            k.wt(v, f"a{L}pv_{h}_{g}")
                            ins = v.tensor_copy(k.A[:, h * 2048 + g * 512: h * 2048 + g * 512 + 512], OT[:])
                            k.bump(ins, "sV", mark=f"a{L}oev_{h}_{g}")
                    do_rzpn(v, h, NT - 1)
                    self._ptevac_v(v, L, h, NT - 2, nchf, PP, PT)
                    self._ptevac_v(v, L, h, NT - 1, nchf, PP, PT)
                    k.wt(v, f"a{L}pv_{h}_3")
                    ins = v.tensor_copy(k.A[:, h * 2048 + 3 * 512: h * 2048 + 3 * 512 + 512], OT[:])
                    k.bump(ins, "sV", mark=f"a{L}oev_{h}_3")
                    k.marks[f"a{L}hdone_{h}"] = k.marks[f"a{L}oev_{h}_3"]

            def scaf(sc):
                for h in range(HEADS):
                    for qt in range(NT):
                        nch = nchf(qt)
                        regs = regs_of[(h, qt)]
                        PSB = PSBs[qt % 2]
                        if L == 0 and qt > 0:
                            self._ptevac_s(sc, L, h, qt - 1, nchf, PP, PT)
                        k.wt(sc, f"a{L}neg_{h}_{qt}")
                        if L == 0:
                            if qt > 0:
                                k.wt(sc, f"a{L}tp_{h}_{qt-1}")
                            elif h > 0:
                                k.wt(sc, f"a{L}tp_{h-1}_{NT-1}")
                        else:
                            if qt > 1:
                                k.wt(sc, f"a{L}tp_{h}_{qt-2}")
                            elif h > 0:
                                k.wt(sc, f"a{L}tp_{h-1}_{NT-2+qt}")
                        for i, rg in enumerate(regs):
                            sp = min(nch, 2 * i + 2) - 2 * i
                            ins = sc.activation(PSB[:, 2 * i * 512:(2 * i + sp) * 512],
                                                SCR[rg][:, 0:sp * 512], AF.Exp,
                                                bias=negm_b[qt % 2], scale=1.0,
                                                accum_out=(za if i == 0 else zb))
                            k.bump(ins, "sS", mark=f"a{L}exp_{h}_{qt}_{i}")
                            if L == 1 and i == 0 and qt > 0:
                                self._ptevac_s(sc, L, h, qt - 1, nchf, PP, PT)
                    self._ptevac_s(sc, L, h, NT - 1, nchf, PP, PT)

            self.phase(f"att{L}", {"tensor": tenf, "vector": vecf, "scalar": scaf, "sync": syncf})

    def _ptevac_common(self, eng, semn, L, h, qt, nchf, PP, PT, parity):
        k = self
        nkb = nchf(qt) * 4
        nb = (nkb + 7) // 8
        for b in range(nb):
            if b % 2 != parity:
                continue
            j0, j1 = b * 8, min(nkb, b * 8 + 8)
            if qt % 4 == 0:
                if qt > 0:
                    k.wt(eng, f"a{L}pv_{h}_{qt//4 - 1}")
                elif h > 0:
                    k.wt(eng, f"a{L}pv_{h-1}_3")
            k.wt(eng, f"a{L}tpb_{h}_{qt}_{b}")
            dst = PT.rearrange("p (n t) -> p n t", n=16)[:, j0:j1,
                                                         (qt % 4) * 128:(qt % 4) * 128 + 128]
            src = PP[:, 0:(j1 - j0) * 128].rearrange("p (n t) -> p n t", n=j1 - j0)
            if semn == "sS":
                ins = eng.copy(dst, src)
            else:
                ins = eng.tensor_copy(dst, src)
            nm = "S" if parity == 0 else "V"
            k.bump(ins, semn, mark=f"a{L}pte{nm}_{h}_{qt}_{b}")

    def _ptevac_s(self, sc, L, h, qt, nchf, PP, PT):
        self._ptevac_common(sc, "sS", L, h, qt, nchf, PP, PT, parity=0)

    def _ptevac_v(self, v, L, h, qt, nchf, PP, PT):
        self._ptevac_common(v, "sV", L, h, qt, nchf, PP, PT, parity=1)

    def _att_pv(self, t, L, h, g, causal, VH, PT, OT):
        k = self
        ktn = (4 * g + 4) if causal else 16
        for qq in range(4 * g, 4 * g + 4):
            nkb = ((qq // 4 + 1) if causal else 4) * 4
            for b in range((nkb + 7) // 8):
                nm = "S" if b % 2 == 0 else "V"
                k.wt(t, f"a{L}pte{nm}_{h}_{qq}_{b}")
        if not (h == 0 and g == 0):
            k.wt(t, f"a{L}oev_{h}_{g-1}" if g > 0 else f"a{L}oev_{h-1}_3")
        for kt in range(ktn):
            ins = t.matmul(OT[:], VH[h % 2][:, kt * 128:(kt + 1) * 128],
                           PT[:, kt * 512:(kt + 1) * 512],
                           start=(kt == 0), stop=(kt == ktn - 1))
        k.bump(ins, "sT", mark=f"a{L}pv_{h}_{g}")


    def phase_af0(self):
        k = self
        L = 0
        QH = k.Cp[:, 0:2048]
        KH = k.Cp[:, 2048:4096]
        VH = k.Cp[:, 4096:6144]
        PSB = k.Cp[:, 6144:8192]
        GNS = [k.Cp[:, 8192:8704], k.Cp[:, 8704:9216]]
        DUMP = k.Cp[:, 9216:9728]
        OS = k.Cp[:, 9728:10240]
        PT = k.Cp[:, 10240:18432]
        RL = k.P32[:, 0:512]
        M1 = k.P32[:, 512:1024]
        GT = k.P32[:, 1024:1536]
        NWB = k.P32[:, 1536:7168]
        negm_b = [k.st1[:, 0:1], k.st1[:, 6:7]]
        za = k.st1[:, 1:2]; rz = k.st1[:, 4:5]
        ssp = k.st2[:, 6:7]; mxp = k.st2[:, 7:8]
        NU = NFC * NT

        def nchf(qt):
            return qt // 4 + 1

        with self.nc.psum_tensor("SCm0", [128, 2048], F32) as SC, \
             self.nc.psum_tensor("PPm0", [128, 1024], BF16) as PP, \
             self.nc.psum_tensor("OTm0", [128, 512], F32) as OT, \
             self.nc.psum_tensor("FAm0", [128, 1024], F32) as FA:
            X1 = FA[:, 0:512]
            X3 = FA[:, 512:1024]

            def transp(t, h, qt, ppq):
                nkb = nchf(qt) * 4
                k.wt(t, f"m0pn_{h}_{qt}")
                for b in range((nkb + 7) // 8):
                    if ppq:
                        k.wt(t, ppq[0])
                    j0, j1 = b * 8, min(nkb, b * 8 + 8)
                    for j in range(j0, j1):
                        ins = t.transpose(PP[:, (j - j0) * 128:(j - j0 + 1) * 128],
                                          PSB[:, j * 128:(j + 1) * 128], k.ident[:])
                    k.bump(ins, "sT", mark=f"m0tpb_{h}_{qt}_{b}")
                    nm = "S" if b % 2 == 0 else "V"
                    ppq[:] = [f"m0pte{nm}_{h}_{qt}_{b}"]
                k.marks[f"m0tp_{h}_{qt}"] = k.marks[f"m0tpb_{h}_{qt}_{(nkb-1)//8}"]

            def ptevac(eng, semn, h, qt, parity):
                nkb = nchf(qt) * 4
                for b in range((nkb + 7) // 8):
                    if b % 2 != parity:
                        continue
                    if qt % 4 == 0:
                        if qt > 0:
                            k.wt(eng, f"m0pv_{h}_{qt//4 - 1}")
                        elif h > 0:
                            k.wt(eng, f"m0pv_{h-1}_3")
                    k.wt(eng, f"m0tpb_{h}_{qt}_{b}")
                    j0, j1 = b * 8, min(nkb, b * 8 + 8)
                    dst = PT.rearrange("p (n t) -> p n t", n=16)[:, j0:j1,
                                                                 (qt % 4) * 128:(qt % 4) * 128 + 128]
                    src = PP[:, 0:(j1 - j0) * 128].rearrange("p (n t) -> p n t", n=j1 - j0)
                    ins = eng.copy(dst, src) if semn == "sS" else eng.tensor_copy(dst, src)
                    nm = "S" if parity == 0 else "V"
                    k.bump(ins, semn, mark=f"m0pte{nm}_{h}_{qt}_{b}")

            def pv(t, h, g):
                ktn = 4 * g + 4
                for qq in range(4 * g, 4 * g + 4):
                    nkb = (qq // 4 + 1) * 4
                    for b in range((nkb + 7) // 8):
                        nm = "S" if b % 2 == 0 else "V"
                        k.wt(t, f"m0pte{nm}_{h}_{qq}_{b}")
                if not (h == 0 and g == 0):
                    k.wt(t, f"m0oev_{h}_{g-1}" if g > 0 else f"m0oev_{h-1}_3")
                for kt in range(ktn):
                    ins = t.matmul(OT[:], VH[:, kt * 128:(kt + 1) * 128],
                                   PT[:, kt * 512:(kt + 1) * 512],
                                   start=(kt == 0), stop=(kt == ktn - 1))
                k.bump(ins, "sT", mark=f"m0pv_{h}_{g}")

            def syncf(sy):
                sy.dma_start(out=k.Bp[:].rearrange("p (h t) -> p h t", h=NH),
                             in_=k.codes_h.rearrange("h p t -> p h t")).then_inc(k.sem("ld"), 16)
                k.cnt["ld"] = k.cnt.get("ld", 0) + 16
                sy.dma_start(out=NWB[:], in_=k.nwbi).then_inc(k.sem("ld"), 16)
                k.cnt["ld"] = k.cnt.get("ld", 0) + 16
                k.wtv(sy, "ld", k.cnt.get("ld", 0))
                k.bump(sy.nop(), "sY", mark="m0_pre")
                un = [0]
                acc = [0.0]

                def ffn_units(nmax):
                    while un[0] < nmax:
                        n = un[0]
                        fc, tt = divmod(n, NT)
                        k.wt(sy, f"f0gn_{fc}_{tt}")
                        sy.dma_start(out=k.gn_hb[tt * 128:(tt + 1) * 128, fc * 512:(fc + 1) * 512],
                                     in_=GNS[n % 2][:]).then_inc(k.sem("st"), 16)
                        k.bump(None, "st", 16)
                        k.wtv(sy, "st", k.cnt.get("st", 0))
                        k.bump(sy.nop(), "sY", mark=f"f0gnfree_{fc}_{tt}")
                        un[0] += 1

                for h in range(HEADS):
                    if h >= 1:
                        k.wt(sy, f"m0hdone_{h-1}")
                    sy.dma_start(out=QH[:], in_=k.q_hb[h]).then_inc(k.sem("dq0"), 16)
                    sy.dma_start(out=KH[:], in_=k.k_hb[h]).then_inc(k.sem("dq0"), 16)
                    sy.dma_start(out=VH[:].rearrange("p (n t) -> p n t", n=16),
                                 in_=k.v_hb.rearrange("n p t -> p n t")[:, :, h * 128:(h + 1) * 128]
                                 ).then_inc(k.sem("dq0"), 16)
                    k.bump(None, "dq0", 48, mark=f"m0qv_{h}")
                    for g in range(4):
                        acc[0] += NU / (HEADS * 4)
                        ffn_units(min(NU, int(acc[0])))
                        k.wt(sy, f"m0oev_{h}_{g}")
                        sy.dma_start(out=k.o_hb[h][:, g * 512:(g + 1) * 512], in_=OS[:]
                                     ).then_inc(k.sem("st"), 16)
                        k.bump(None, "st", 16)
                        k.wtv(sy, "st", k.cnt.get("st", 0))
                        k.bump(sy.nop(), "sY", mark=f"m0ofree_{h}_{g}")
                ffn_units(NU)
                k.bump(sy.nop(), "sY", mark="af0_done")

            def tenf(t):
                k.wt(t, "m0_pre")
                ppq = []
                un = [0]
                acc = [0.0]

                def ffn_units(nmax):
                    while un[0] < nmax:
                        n = un[0]
                        fc, tt = divmod(n, NT)
                        k.wt(t, f"f0w_{fc}")
                        if n >= 1:
                            pf, pt2 = divmod(n - 1, NT)
                            k.wt(t, f"f0gt_{pf}_{pt2}")
                        for ht in range(NH):
                            t.matmul(X1, k.Bp[:, ht * 2048 + tt * 128: ht * 2048 + tt * 128 + 128],
                                     k.A[:, (fc % 2) * 8192 + ht * 512:(fc % 2) * 8192 + ht * 512 + 512],
                                     start=(ht == 0), stop=(ht == NH - 1))
                        for ht in range(NH):
                            ins = t.matmul(X3, k.Bp[:, ht * 2048 + tt * 128: ht * 2048 + tt * 128 + 128],
                                           k.A[:, 16384 + (fc % 2) * 8192 + ht * 512:
                                               16384 + (fc % 2) * 8192 + ht * 512 + 512],
                                           start=(ht == 0), stop=(ht == NH - 1))
                        k.bump(ins, "sT", mark=f"f0mm_{fc}_{tt}")
                        if tt == NT - 1:
                            k.marks[f"f0adone_{fc}"] = k.marks[f"f0mm_{fc}_{tt}"]
                        un[0] += 1

                for h in range(HEADS):
                    k.wt(t, f"m0qv_{h}")
                    for qt in range(NT):
                        nch = nchf(qt)
                        if qt > 0:
                            k.wt(t, f"m0exp_{h}_{qt-1}")
                        elif h > 0:
                            k.wt(t, f"m0exp_{h-1}_{NT-1}")
                        for c in range(nch):
                            ins = t.matmul(SC[:, c * 512:(c + 1) * 512],
                                           QH[:, qt * 128:(qt + 1) * 128],
                                           KH[:, c * 512:(c + 1) * 512],
                                           start=True, stop=True)
                        k.bump(ins, "sT", mark=f"m0sc_{h}_{qt}")
                        if qt > 0:
                            transp(t, h, qt - 1, ppq)
                            if (qt - 1) % 4 == 3:
                                pv(t, h, (qt - 1) // 4)
                        acc[0] += NU / (HEADS * NT)
                        ffn_units(min(NU, int(acc[0])))
                    transp(t, h, NT - 1, ppq)
                    pv(t, h, 3)
                ffn_units(NU)

            def vecf(v):
                un = [0]
                acc = [0.0]

                def ffn_units(nmax):
                    while un[0] < nmax:
                        n = un[0]
                        fc, tt = divmod(n, NT)
                        k.wt(v, f"f0rl_{fc}_{tt}")
                        v.tensor_tensor(M1[:], X1, RL[:], OP.mult)
                        ins = v.scalar_tensor_tensor(GT[:], M1[:], k.sinv3[:, tt:tt + 1], X3,
                                                     OP.mult, OP.mult)
                        k.bump(ins, "sV", mark=f"f0gt_{fc}_{tt}")
                        if n >= 2:
                            pf, pt2 = divmod(n - 2, NT)
                            k.wt(v, f"f0gnfree_{pf}_{pt2}")
                        ins = v.tensor_tensor(GNS[n % 2][:], GT[:], NWB[:, fc * 512:(fc + 1) * 512], OP.mult)
                        k.bump(ins, "sV", mark=f"f0gn_{fc}_{tt}")
                        ins = v.tensor_reduce(mxp, GNS[n % 2][:], AX, OP.max, apply_absolute_value=True)
                        k.bump(ins, "sV", mark=f"f0mxr_{fc}_{tt}")
                        k.wt(v, f"f0ss_{fc}_{tt}")
                        k.wt(v, f"f0mxr_{fc}_{tt}")
                        if fc == 0:
                            v.tensor_copy(k.ssqB[:, tt:tt + 1], ssp)
                            ins = v.tensor_copy(k.mxgB[:, tt:tt + 1], mxp)
                        else:
                            v.tensor_tensor(k.ssqB[:, tt:tt + 1], k.ssqB[:, tt:tt + 1], ssp, OP.add)
                            ins = v.tensor_tensor(k.mxgB[:, tt:tt + 1], k.mxgB[:, tt:tt + 1], mxp, OP.max)
                        k.bump(ins, "sV", mark=f"f0ssacc_{fc}_{tt}")
                        un[0] += 1

                for h in range(HEADS):
                    for qt in range(NT):
                        nch = nchf(qt)
                        off = qt % 4
                        k.wt(v, f"m0sc_{h}_{qt}")
                        v.tensor_tensor(SC[:, (nch - 1) * 512:nch * 512],
                                        SC[:, (nch - 1) * 512:nch * 512],
                                        k.msk()[:, off * 512:(off + 1) * 512], OP.add)
                        ins = v.tensor_reduce(negm_b[qt % 2], SC[:, 0:nch * 512], AX, OP.max, negate=True)
                        k.bump(ins, "sV", mark=f"m0neg_{h}_{qt}")
                        k.wt(v, f"m0exp_{h}_{qt}")
                        ins = v.reciprocal(rz, za)
                        k.bump(ins, "sV", mark=f"m0rz_{h}_{qt}")
                        k.wt(v, f"m0rz_{h}_{qt}")
                        ins = v.tensor_scalar_mul(PSB[:, 0:nch * 512], PSB[:, 0:nch * 512], rz)
                        k.bump(ins, "sV", mark=f"m0pn_{h}_{qt}")
                        if qt > 0:
                            ptevac(v, "sV", h, qt - 1, 1)
                        if qt % 4 == 1 and qt >= 5:
                            g = qt // 4 - 1
                            k.wt(v, f"m0pv_{h}_{g}")
                            if g > 0 or h > 0:
                                k.wt(v, f"m0ofree_{h}_{g-1}" if g > 0 else f"m0ofree_{h-1}_3")
                            ins = v.tensor_copy(OS[:], OT[:])
                            k.bump(ins, "sV", mark=f"m0oev_{h}_{g}")
                        acc[0] += NU / (HEADS * NT)
                        ffn_units(min(NU, int(acc[0])))
                    ptevac(v, "sV", h, NT - 1, 1)
                    k.wt(v, f"m0pv_{h}_3")
                    k.wt(v, f"m0ofree_{h}_2")
                    ins = v.tensor_copy(OS[:], OT[:])
                    k.bump(ins, "sV", mark=f"m0oev_{h}_3")
                    k.marks[f"m0hdone_{h}"] = k.marks[f"m0oev_{h}_3"]
                ffn_units(NU)

            def scaf(sc):
                un = [0]
                acc = [0.0]

                def ffn_units(nmax):
                    while un[0] < nmax:
                        n = un[0]
                        fc, tt = divmod(n, NT)
                        k.wt(sc, f"f0mm_{fc}_{tt}")
                        ins = sc.activation(RL[:], X1, AF.Relu)
                        k.bump(ins, "sS", mark=f"f0rl_{fc}_{tt}")
                        k.wt(sc, f"f0gt_{fc}_{tt}")
                        if n >= 1:
                            pf, pt2 = divmod(n - 1, NT)
                            k.wt(sc, f"f0ssacc_{pf}_{pt2}")
                        ins = sc.activation(DUMP[:], GT[:], AF.Square, accum_out=ssp)
                        k.bump(ins, "sS", mark=f"f0ss_{fc}_{tt}")
                        un[0] += 1

                for h in range(HEADS):
                    for qt in range(NT):
                        nch = nchf(qt)
                        if qt > 0:
                            ptevac(sc, "sS", h, qt - 1, 0)
                        k.wt(sc, f"m0neg_{h}_{qt}")
                        if qt > 0:
                            k.wt(sc, f"m0tp_{h}_{qt-1}")
                        elif h > 0:
                            k.wt(sc, f"m0tp_{h-1}_{NT-1}")
                        ins = sc.activation(PSB[:, 0:nch * 512], SC[:, 0:nch * 512], AF.Exp,
                                            bias=negm_b[qt % 2], scale=1.0, accum_out=za)
                        k.bump(ins, "sS", mark=f"m0exp_{h}_{qt}")
                        acc[0] += NU / (HEADS * NT)
                        ffn_units(min(NU, int(acc[0])))
                    ptevac(sc, "sS", h, NT - 1, 0)
                ffn_units(NU)

            def gpsf(g):
                for fc in range(NFC):
                    if fc >= 2:
                        k.wt(g, f"f0adone_{fc-2}")
                    g.dma_start(out=k.A[:, (fc % 2) * 8192:(fc % 2) * 8192 + 8192]
                                .rearrange("p (h t) -> p h t", h=NH),
                                in_=k.w13[:, :, fc * 512:(fc + 1) * 512].rearrange("h p t -> p h t")
                                ).then_inc(k.sem(f"dw{fc%2}"), 16)
                    g.dma_start(out=k.A[:, 16384 + (fc % 2) * 8192:16384 + (fc % 2) * 8192 + 8192]
                                .rearrange("p (h t) -> p h t", h=NH),
                                in_=k.w13[:, :, F0 + fc * 512:F0 + fc * 512 + 512].rearrange("h p t -> p h t")
                                ).then_inc(k.sem(f"dw{fc%2}"), 16)
                    k.bump(None, f"dw{fc%2}", 32, mark=f"f0w_{fc}")

            self.phase("af0", {"tensor": tenf, "vector": vecf, "scalar": scaf,
                               "sync": syncf, "gpsimd": gpsf})

    def phase_af1(self):
        k = self
        QH = k.Cp[:, 0:2048]
        V1 = k.Cp[:, 2048:4096]
        PSBs = [k.Cp[:, 4096:6144], k.Cp[:, 6144:8192]]
        OS = k.Cp[:, 18432:18944]
        K1 = k.Cp[:, 8192:10240]
        PT = k.Cp[:, 10240:18432]
        HF = k.A[:, 16384:18432]
        GG = k.P32[:, 0:512]
        UU = k.P32[:, 512:1024]
        TT = k.P32[:, 1024:1536]
        ZZ = k.P32[:, 1536:2048]
        negm_b = [k.st1[:, 0:1], k.st1[:, 6:7]]
        za = k.st1[:, 1:2]; rz = k.st1[:, 4:5]
        WGU = [k.A[:, i * 4096:(i + 1) * 4096] for i in range(4)]
        NU = NFT * 4

        with self.nc.psum_tensor("SCm1", [128, 2048], F32) as SC, \
             self.nc.psum_tensor("PPm1", [128, 1024], BF16) as PP, \
             self.nc.psum_tensor("OTm1", [128, 512], F32) as OT, \
             self.nc.psum_tensor("GPm1", [128, 512], F32) as GP, \
             self.nc.psum_tensor("UPm1", [128, 512], F32) as UP:

            def transp(t, h, qt, ppq):
                PSB = PSBs[qt % 2]
                k.wt(t, f"m1pn_{h}_{qt}")
                for b in range(2):
                    if ppq:
                        k.wt(t, ppq[0])
                    for j in range(8):
                        ins = t.transpose(PP[:, j * 128:(j + 1) * 128],
                                          PSB[:, (b * 8 + j) * 128:(b * 8 + j) * 128 + 128], k.ident[:])
                    k.bump(ins, "sT", mark=f"m1tpb_{h}_{qt}_{b}")
                    nm = "S" if b % 2 == 0 else "V"
                    ppq[:] = [f"m1pte{nm}_{h}_{qt}_{b}"]
                k.marks[f"m1tp_{h}_{qt}"] = k.marks[f"m1tpb_{h}_{qt}_1"]

            def ptevac(eng, semn, h, qt, parity):
                for b in range(2):
                    if b % 2 != parity:
                        continue
                    if qt % 4 == 0:
                        if qt > 0:
                            k.wt(eng, f"m1pv_{h}_{qt//4 - 1}")
                        elif h > 0:
                            k.wt(eng, f"m1pv_{h-1}_3")
                    k.wt(eng, f"m1tpb_{h}_{qt}_{b}")
                    dst = PT.rearrange("p (n t) -> p n t", n=16)[:, b * 8:b * 8 + 8,
                                                                 (qt % 4) * 128:(qt % 4) * 128 + 128]
                    src = PP[:].rearrange("p (n t) -> p n t", n=8)
                    ins = eng.copy(dst, src) if semn == "sS" else eng.tensor_copy(dst, src)
                    nm = "S" if parity == 0 else "V"
                    k.bump(ins, semn, mark=f"m1pte{nm}_{h}_{qt}_{b}")

            def pv(t, h, g):
                for qq in range(4 * g, 4 * g + 4):
                    for b in range(2):
                        nm = "S" if b % 2 == 0 else "V"
                        k.wt(t, f"m1pte{nm}_{h}_{qq}_{b}")
                if not (h == 0 and g == 0):
                    k.wt(t, f"m1oev_{h}_{g-1}" if g > 0 else f"m1oev_{h-1}_3")
                for kt in range(16):
                    ins = t.matmul(OT[:], V1[:, kt * 128:(kt + 1) * 128],
                                   PT[:, kt * 512:(kt + 1) * 512],
                                   start=(kt == 0), stop=(kt == 15))
                k.bump(ins, "sT", mark=f"m1pv_{h}_{g}")

            def syncf(sy):
                sy.dma_start(out=k.Bp[:].rearrange("p (h t) -> p h t", h=NH),
                             in_=k.xn_h.rearrange("h p t -> p h t")).then_inc(k.sem("ld"), 16)
                k.cnt["ld"] = k.cnt.get("ld", 0) + 16
                sy.dma_start(out=V1[:].rearrange("p (n t) -> p n t", n=16),
                             in_=k.v_hb.rearrange("n p t -> p n t")[:, :, 0:128]
                             ).then_inc(k.sem("ld"), 16)
                k.cnt["ld"] = k.cnt.get("ld", 0) + 16
                k.wtv(sy, "ld", k.cnt.get("ld", 0))
                k.bump(sy.nop(), "sY", mark="m1_pre")
                un = [0]
                acc = [0.0]

                def ffn_units(nmax):
                    while un[0] < nmax:
                        n = un[0]
                        ft, tc = divmod(n, 4)
                        if tc == 3:
                            k.wt(sy, f"f1h_{ft}_{tc}")
                            sy.dma_start(out=k.hf_h[ft], in_=HF[:]).then_inc(k.sem("st"), 16)
                            k.bump(None, "st", 16)
                            k.wtv(sy, "st", k.cnt.get("st", 0))
                            k.bump(sy.nop(), "sY", mark=f"f1hfree_{ft}")
                        un[0] += 1

                for h in range(HEADS):
                    if h >= 1:
                        k.wt(sy, f"m1hdone_{h-1}")
                    sy.dma_start(out=QH[:], in_=k.q_hb[h]).then_inc(k.sem("dq0"), 16)
                    k.bump(None, "dq0", 16, mark=f"m1qv_{h}")
                    for g in range(4):
                        acc[0] += NU / (HEADS * 4)
                        ffn_units(min(NU, int(acc[0])))
                        k.wt(sy, f"m1oev_{h}_{g}")
                        sy.dma_start(out=k.o_hb[h][:, g * 512:(g + 1) * 512], in_=OS[:]
                                     ).then_inc(k.sem("st"), 16)
                        k.bump(None, "st", 16)
                        k.wtv(sy, "st", k.cnt.get("st", 0))
                        k.bump(sy.nop(), "sY", mark=f"m1ofree_{h}_{g}")
                ffn_units(NU)
                k.bump(sy.nop(), "sY", mark="af1_done")

            def tenf(t):
                k.wt(t, "m1_pre")
                ppq = []
                un = [0]
                acc = [0.0]

                def ffn_units(nmax):
                    while un[0] < nmax:
                        n = un[0]
                        ft, tc = divmod(n, 4)
                        k.wt(t, f"f1w_{ft}")
                        if n >= 1:
                            pf, pc = divmod(n - 1, 4)
                            k.wt(t, f"f1gt_{pf}_{pc}")
                            k.wt(t, f"f1ue_{pf}_{pc}")
                        for ht in range(NH):
                            t.matmul(GP[:], WGU[ft % 4][:, ht * 128:(ht + 1) * 128],
                                     k.Bp[:, ht * 2048 + tc * 512: ht * 2048 + tc * 512 + 512],
                                     start=(ht == 0), stop=(ht == NH - 1))
                        for ht in range(NH):
                            ins = t.matmul(UP[:], WGU[ft % 4][:, 2048 + ht * 128: 2048 + ht * 128 + 128],
                                           k.Bp[:, ht * 2048 + tc * 512: ht * 2048 + tc * 512 + 512],
                                           start=(ht == 0), stop=(ht == NH - 1))
                        k.bump(ins, "sT", mark=f"f1mm_{ft}_{tc}")
                        if tc == 3:
                            k.marks[f"f1ftdone_{ft}"] = k.marks[f"f1mm_{ft}_{tc}"]
                        un[0] += 1

                for h in range(HEADS):
                    k.wt(t, f"m1qv_{h}")
                    for qt in range(NT):
                        if qt > 0:
                            k.wt(t, f"m1exp_{h}_{qt-1}")
                        elif h > 0:
                            k.wt(t, f"m1exp_{h-1}_{NT-1}")
                        for c in range(4):
                            ins = t.matmul(SC[:, c * 512:(c + 1) * 512],
                                           QH[:, qt * 128:(qt + 1) * 128],
                                           K1[:, c * 512:(c + 1) * 512],
                                           start=True, stop=True)
                        k.bump(ins, "sT", mark=f"m1sc_{h}_{qt}")
                        if qt > 0:
                            transp(t, h, qt - 1, ppq)
                            if (qt - 1) % 4 == 3:
                                pv(t, h, (qt - 1) // 4)
                        acc[0] += NU / (HEADS * NT)
                        ffn_units(min(NU, int(acc[0])))
                    transp(t, h, NT - 1, ppq)
                    pv(t, h, 3)
                ffn_units(NU)

            def vecf(v):
                un = [0]
                acc = [0.0]

                def ffn_units(nmax):
                    while un[0] < nmax:
                        n = un[0]
                        ft, tc = divmod(n, 4)
                        k.wt(v, f"f1mm_{ft}_{tc}")
                        ins = v.tensor_scalar_mul(UU[:], UP[:], k.us_s[:, ft:ft + 1])
                        k.bump(ins, "sV", mark=f"f1ue_{ft}_{tc}")
                        k.wt(v, f"f1gt_{ft}_{tc}")
                        ins = v.scalar_tensor_tensor(GG[:], TT[:], 1.0, ZZ[:], OP.add, OP.mult)
                        k.bump(ins, "sV", mark=f"f1gg_{ft}_{tc}")
                        if tc == 0 and ft >= 1:
                            k.wt(v, f"f1hfree_{ft-1}")
                        ins = v.tensor_tensor(HF[:, tc * 512:(tc + 1) * 512], GG[:], UU[:], OP.mult)
                        k.bump(ins, "sV", mark=f"f1h_{ft}_{tc}")
                        un[0] += 1

                for h in range(HEADS):
                    for qt in range(NT):
                        k.wt(v, f"m1sc_{h}_{qt}")
                        ins = v.tensor_reduce(negm_b[qt % 2], SC[:], AX, OP.max, negate=True)
                        k.bump(ins, "sV", mark=f"m1neg_{h}_{qt}")
                        k.wt(v, f"m1exp_{h}_{qt}")
                        ins = v.reciprocal(rz, za)
                        k.bump(ins, "sV", mark=f"m1rz_{h}_{qt}")
                        k.wt(v, f"m1rz_{h}_{qt}")
                        PSB = PSBs[qt % 2]
                        ins = v.tensor_scalar_mul(PSB[:], PSB[:], rz)
                        k.bump(ins, "sV", mark=f"m1pn_{h}_{qt}")
                        if qt > 0:
                            ptevac(v, "sV", h, qt - 1, 1)
                        if qt % 4 == 1 and qt >= 5:
                            g = qt // 4 - 1
                            k.wt(v, f"m1pv_{h}_{g}")
                            if g > 0 or h > 0:
                                k.wt(v, f"m1ofree_{h}_{g-1}" if g > 0 else f"m1ofree_{h-1}_3")
                            ins = v.tensor_copy(OS[:], OT[:])
                            k.bump(ins, "sV", mark=f"m1oev_{h}_{g}")
                        acc[0] += NU / (HEADS * NT)
                        ffn_units(min(NU, int(acc[0])))
                    ptevac(v, "sV", h, NT - 1, 1)
                    k.wt(v, f"m1pv_{h}_3")
                    k.wt(v, f"m1ofree_{h}_2")
                    ins = v.tensor_copy(OS[:], OT[:])
                    k.bump(ins, "sV", mark=f"m1oev_{h}_3")
                    k.marks[f"m1hdone_{h}"] = k.marks[f"m1oev_{h}_3"]
                ffn_units(NU)

            def scaf(sc):
                un = [0]
                acc = [0.0]
                k.wt(sc, "gsh_ready")

                def ffn_units(nmax):
                    while un[0] < nmax:
                        n = un[0]
                        ft, tc = divmod(n, 4)
                        k.wt(sc, f"f1mm_{ft}_{tc}")
                        if n >= 1:
                            pf, pc = divmod(n - 1, 4)
                            k.wt(sc, f"f1gg_{pf}_{pc}")
                        ins = sc.activation(TT[:], GP[:], AF.Tanh, scale=k.gsh_s[:, ft:ft + 1])
                        k.bump(ins, "sS")
                        ins = sc.activation(ZZ[:], GP[:], AF.Copy, scale=k.gsh_s[:, ft:ft + 1])
                        k.bump(ins, "sS", mark=f"f1gt_{ft}_{tc}")
                        un[0] += 1

                for h in range(HEADS):
                    for qt in range(NT):
                        if qt > 0:
                            ptevac(sc, "sS", h, qt - 1, 0)
                        k.wt(sc, f"m1neg_{h}_{qt}")
                        if qt > 1:
                            k.wt(sc, f"m1tp_{h}_{qt-2}")
                        elif h > 0:
                            k.wt(sc, f"m1tp_{h-1}_{NT-2+qt}")
                        ins = sc.activation(PSBs[qt % 2][:], SC[:], AF.Exp,
                                            bias=negm_b[qt % 2], scale=1.0, accum_out=za)
                        k.bump(ins, "sS", mark=f"m1exp_{h}_{qt}")
                        acc[0] += NU / (HEADS * NT)
                        ffn_units(min(NU, int(acc[0])))
                    ptevac(sc, "sS", h, NT - 1, 0)
                ffn_units(NU)

            def gpsf(g):
                for ft in range(NFT):
                    if ft >= 4:
                        k.wt(g, f"f1ftdone_{ft-4}")
                    g.dma_start(out=WGU[ft % 4][:], in_=k.wgu1[ft]).then_inc(k.sem(f"dw{ft%4}"), 16)
                    k.bump(None, f"dw{ft%4}", 16, mark=f"f1w_{ft}")

            self.phase("af1", {"tensor": tenf, "vecf": None} if False else
                       {"tensor": tenf, "vector": vecf, "scalar": scaf,
                        "sync": syncf, "gpsimd": gpsf})

    def phase_wo(self, L):
        k = self
        wo = k.wo0 if L == 0 else k.wo1
        SST = k.Cp[:, 0:2048]
        with self.nc.psum_tensor(f"WOP{L}", [128, 2048], F32) as PB:
            def gpsf(g):
                for hd in range(NH):
                    g.dma_start(out=k.A[:, hd * 2048:(hd + 1) * 2048], in_=wo[hd]
                                ).then_inc(k.sem("cw"), 16)
                    k.cnt["cw"] = k.cnt.get("cw", 0) + 16
                k.wtv(g, "cw", k.cnt.get("cw", 0))
                k.bump(g.nop(), "sGc", mark=f"wo{L}_w")

            def syncf(sy):
                for hd in range(HEADS):
                    sy.dma_start(out=k.Bp[:, hd * 2048:(hd + 1) * 2048], in_=k.o_hb[hd]
                                 ).then_inc(k.sem("ld"), 16)
                    k.cnt["ld"] = k.cnt.get("ld", 0) + 16
                k.wtv(sy, "ld", k.cnt.get("ld", 0))
                k.bump(sy.nop(), "sY", mark=f"wo{L}_o")
                for ts in range(NT):
                    k.wt(sy, f"wo{L}ev_{ts*4+3}")
                    sy.dma_start(out=k.s_hb[ts], in_=SST[:]).then_inc(k.sem("st"), 16)
                    k.bump(None, "st", 16)
                    k.wtv(sy, "st", k.cnt.get("st", 0))
                    k.bump(sy.nop(), "sY", mark=f"wo{L}sfree_{ts}")
                k.bump(sy.nop(), "sY", mark=f"wo{L}_done")

            def tenf(t):
                k.wt(t, f"wo{L}_w")
                k.wt(t, f"wo{L}_o")
                for ts in range(NT):
                    for ec in range(4):
                        idx = ts * 4 + ec
                        if idx >= 4:
                            k.wt(t, f"wo{L}ev_{idx-4}")
                        pq = PB[:, ec * 512:(ec + 1) * 512]
                        for hd in range(NH):
                            ins = t.matmul(pq, k.Bp[:, hd * 2048 + ts * 128: hd * 2048 + ts * 128 + 128],
                                           k.A[:, hd * 2048 + ec * 512: hd * 2048 + ec * 512 + 512],
                                           start=(hd == 0), stop=(hd == NH - 1))
                        k.bump(ins, "sT", mark=f"wo{L}mm_{idx}")

            def vecf(v):
                for ts in range(NT):
                    for ec in range(4):
                        idx = ts * 4 + ec
                        k.wt(v, f"wo{L}mm_{idx}")
                        if ts > 0 and ec == 0:
                            k.wt(v, f"wo{L}sfree_{ts-1}")
                        ins = v.tensor_copy(SST[:, ec * 512:(ec + 1) * 512], PB[:, ec * 512:(ec + 1) * 512])
                        k.bump(ins, "sV", mark=f"wo{L}ev_{idx}")

            self.phase(f"wo{L}", {"tensor": tenf, "vector": vecf, "sync": syncf, "gpsimd": gpsf})

    def phase_ffn0A(self):
        k = self
        W1S = [k.A[:, 0:8192], k.A[:, 8192:16384]]
        W3S = [k.A[:, 16384:24576], k.A[:, 24576:32768]]
        NWB = k.Cp[:, 0:5632]
        GNS = [k.Cp[:, 5632:6144], k.Cp[:, 6144:6656]]
        DUMP = k.Cp[:, 6656:7168]
        RL = k.P32[:, 0:512]
        M1 = k.P32[:, 512:1024]
        GT = k.P32[:, 1024:1536]
        ssp = k.st1[:, 0:1]; mxp = k.st1[:, 1:2]

        with self.nc.psum_tensor("F0P", [128, 2048], F32) as PB:
            X1 = [PB[:, 0:512], PB[:, 512:1024]]
            X3 = [PB[:, 1024:1536], PB[:, 1536:2048]]

            def gpsf(g):
                g.dma_start(out=NWB[:], in_=k.nwbi).then_inc(k.sem("cw"), 16)
                k.cnt["cw"] = k.cnt.get("cw", 0) + 16
                k.wtv(g, "cw", k.cnt.get("cw", 0))
                k.bump(g.nop(), "sGc", mark="f0_nwb")
                for fc in range(NFC):
                    if fc >= 2:
                        k.wt(g, f"f0adone_{fc-2}")
                    g.dma_start(out=W1S[fc % 2][:].rearrange("p (h t) -> p h t", h=NH),
                                in_=k.w13[:, :, fc * 512:(fc + 1) * 512].rearrange("h p t -> p h t")
                                ).then_inc(k.sem(f"dw{fc%2}"), 16)
                    g.dma_start(out=W3S[fc % 2][:].rearrange("p (h t) -> p h t", h=NH),
                                in_=k.w13[:, :, F0 + fc * 512:F0 + fc * 512 + 512].rearrange("h p t -> p h t")
                                ).then_inc(k.sem(f"dw{fc%2}"), 16)
                    k.bump(None, f"dw{fc%2}", 32, mark=f"f0w_{fc}")

            def syncf(sy):
                sy.dma_start(out=k.Bp[:].rearrange("p (h t) -> p h t", h=NH),
                             in_=k.codes_h.rearrange("h p t -> p h t")).then_inc(k.sem("ld"), 16)
                k.cnt["ld"] = k.cnt.get("ld", 0) + 16
                k.wtv(sy, "ld", k.cnt.get("ld", 0))
                k.bump(sy.nop(), "sY", mark="f0_codes")
                for fc in range(NFC):
                    for tt in range(NT):
                        n = fc * NT + tt
                        k.wt(sy, f"f0gn_{fc}_{tt}")
                        sy.dma_start(out=k.gn_hb[tt * 128:(tt + 1) * 128, fc * 512:(fc + 1) * 512],
                                     in_=GNS[n % 2][:]).then_inc(k.sem("st"), 16)
                        k.bump(None, "st", 16)
                        k.wtv(sy, "st", k.cnt.get("st", 0))
                        k.bump(sy.nop(), "sY", mark=f"f0gnfree_{fc}_{tt}")
                k.bump(sy.nop(), "sY", mark="f0A_done")

            def tenf(t):
                k.wt(t, "f0_codes")
                for fc in range(NFC):
                    k.wt(t, f"f0w_{fc}")
                    for tt in range(NT):
                        pp = tt % 2
                        n = fc * NT + tt
                        if n >= 2:
                            pf, pt = divmod(n - 2, NT)
                            k.wt(t, f"f0gt_{pf}_{pt}")
                        for ht in range(NH):
                            t.matmul(X1[pp], k.Bp[:, ht * 2048 + tt * 128: ht * 2048 + tt * 128 + 128],
                                     W1S[fc % 2][:, ht * 512:(ht + 1) * 512],
                                     start=(ht == 0), stop=(ht == NH - 1))
                        for ht in range(NH):
                            ins = t.matmul(X3[pp], k.Bp[:, ht * 2048 + tt * 128: ht * 2048 + tt * 128 + 128],
                                           W3S[fc % 2][:, ht * 512:(ht + 1) * 512],
                                           start=(ht == 0), stop=(ht == NH - 1))
                        k.bump(ins, "sT", mark=f"f0mm_{fc}_{tt}")
                    k.marks[f"f0adone_{fc}"] = k.marks[f"f0mm_{fc}_{NT-1}"]

            def scaf(sc):
                for fc in range(NFC):
                    for tt in range(NT):
                        pp = tt % 2
                        n = fc * NT + tt
                        k.wt(sc, f"f0mm_{fc}_{tt}")
                        ins = sc.activation(RL[:], X1[pp], AF.Relu)
                        k.bump(ins, "sS", mark=f"f0rl_{fc}_{tt}")
                        k.wt(sc, f"f0gt_{fc}_{tt}")
                        if n >= 1:
                            pf, pt = divmod(n - 1, NT)
                            k.wt(sc, f"f0ssacc_{pf}_{pt}")
                        ins = sc.activation(DUMP[:], GT[:], AF.Square, accum_out=ssp)
                        k.bump(ins, "sS", mark=f"f0ss_{fc}_{tt}")

            def vecf(v):
                k.wt(v, "f0_nwb")
                for fc in range(NFC):
                    for tt in range(NT):
                        pp = tt % 2
                        n = fc * NT + tt
                        k.wt(v, f"f0rl_{fc}_{tt}")
                        v.tensor_tensor(M1[:], X1[pp], RL[:], OP.mult)
                        ins = v.scalar_tensor_tensor(GT[:], M1[:], k.sinv3[:, tt:tt + 1], X3[pp],
                                                     OP.mult, OP.mult)
                        k.bump(ins, "sV", mark=f"f0gt_{fc}_{tt}")
                        if n >= 2:
                            pf, pt = divmod(n - 2, NT)
                            k.wt(v, f"f0gnfree_{pf}_{pt}")
                        ins = v.tensor_tensor(GNS[n % 2][:], GT[:], NWB[:, fc * 512:(fc + 1) * 512], OP.mult)
                        k.bump(ins, "sV", mark=f"f0gn_{fc}_{tt}")
                        ins = v.tensor_reduce(mxp, GNS[n % 2][:], AX, OP.max, apply_absolute_value=True)
                        k.bump(ins, "sV", mark=f"f0mxr_{fc}_{tt}")
                        k.wt(v, f"f0ss_{fc}_{tt}")
                        k.wt(v, f"f0mxr_{fc}_{tt}")
                        if fc == 0:
                            v.tensor_copy(k.ssqB[:, tt:tt + 1], ssp)
                            ins = v.tensor_copy(k.mxgB[:, tt:tt + 1], mxp)
                        else:
                            v.tensor_tensor(k.ssqB[:, tt:tt + 1], k.ssqB[:, tt:tt + 1], ssp, OP.add)
                            ins = v.tensor_tensor(k.mxgB[:, tt:tt + 1], k.mxgB[:, tt:tt + 1], mxp, OP.max)
                        k.bump(ins, "sV", mark=f"f0ssacc_{fc}_{tt}")

            self.phase("ffn0A", {"tensor": tenf, "vector": vecf, "scalar": scaf,
                                 "sync": syncf, "gpsimd": gpsf})

    def phase_ffn0S(self):
        k = self
        SD16 = k.P32[:, 0:16]
        RSQ = k.P32[:, 16:32]
        CLP = k.P32[:, 32:48]
        REC = k.P32[:, 48:64]

        def scaf(sc):
            k.wt(sc, "f0A_vecdone")
            ins = sc.activation(SD16[:], k.ssqB[:], AF.Sqrt, bias=k.epsb[:], scale=1.0 / F0)
            k.bump(ins, "sS", mark="f0_sd")

        def vecf(v):
            k.bump(v.nop(), "sV", mark="f0A_vecdone")
            k.wt(v, "f0_sd")
            ins = v.reciprocal(RSQ[:], SD16[:])
            k.bump(ins, "sV", mark="f0_rsq")
            k.wt(v, "f0_rsq")
            v.tensor_tensor(CLP[:], RSQ[:], k.mxgB[:], OP.mult)
            v.tensor_single_scalar(CLP[:], CLP[:], 1e-5, OP.max)
            ins = v.reciprocal(REC[:], CLP[:])
            k.bump(ins, "sV", mark="f0_rec")
            k.wt(v, "f0_rec")
            v.tensor_single_scalar(k.i2v[:], CLP[:], 1.0 / 127.0, OP.mult)
            v.tensor_tensor(k.aa8[:], RSQ[:], REC[:], OP.mult)
            ins = v.tensor_single_scalar(k.aa8[:], k.aa8[:], 127.0, OP.mult)
            k.bump(ins, "sV", mark="f0_stats")

        self.phase("ffn0S", {"scalar": scaf, "vector": vecf})

    def phase_ffn0B(self):
        k = self
        iq2 = [k.Bp[:, ft * 1024:(ft + 1) * 1024] for ft in range(32)] + \
              [k.A[:, 20480 + (ft - 32) * 1024: 20480 + (ft - 31) * 1024] for ft in range(32, NFT)]
        GNL = [k.Cp[:, 0:512], k.Cp[:, 512:1024], k.Cp[:, 4096:4608], k.Cp[:, 4608:5120]]
        IQS = [k.Cp[:, 1024:1536], k.Cp[:, 3584:4096]]
        GF = k.Cp[:, 1536:2560]
        GS = k.Cp[:, 2560:3584]
        W2L = [k.A[:, 0:5632], k.A[:, 5632:11264]]
        YY = k.P32[:, 0:512]
        FF = k.P32[:, 512:1024]
        CMP = k.P32[:, 1024:1536]

        with self.nc.psum_tensor("F0D", [128, 2048], F32) as DN, \
             self.nc.psum_tensor("F0T", [128, 2048], BF16) as TP, \
             self.nc.psum_tensor("F0T2", [128, 1024], BF16) as TP2:
            tph = [TP[:, 0:1024], TP[:, 1024:2048]]

            def syncf(sy):
                for tb in range(2):
                    for tt8 in range(8):
                        for fc in range(NFC):
                            n = (tb * 8 + tt8) * NFC + fc
                            if n >= 4:
                                k.wt(sy, f"f0y_{n-4}")
                            tt = tb * 8 + tt8
                            sy.dma_start(out=GNL[n % 4][:],
                                         in_=k.gn_hb[tt * 128:(tt + 1) * 128, fc * 512:(fc + 1) * 512]
                                         ).then_inc(k.sem(f"dx{n%4}"), 16)
                            k.bump(None, f"dx{n%4}", 16, mark=f"f0gnl_{n}")
                    for dblk in range(NH):
                        k.wt(sy, f"f0gse_{tb}_{dblk}")
                        for b in range(8):
                            tsub = tb * 8 + b
                            sy.dma_start(out=k.g_hb[tsub][:, dblk * 128:(dblk + 1) * 128],
                                         in_=GS[:, b * 128:(b + 1) * 128]).then_inc(k.sem("st"), 16)
                            k.bump(None, "st", 16)
                        k.wtv(sy, "st", k.cnt.get("st", 0))
                        k.bump(sy.nop(), "sY", mark=f"f0gsfree_{tb}_{dblk}")
                k.wtv(sy, "st", k.cnt.get("st", 0))
                k.bump(sy.nop(), "sY", mark="f0B_done")

            def gpsf(g):
                for tb in range(2):
                    for dblk in range(NH):
                        n = tb * NH + dblk
                        if n >= 2:
                            pn = n - 2
                            k.wt(g, f"f0dn_{pn//NH}_{pn%NH}")
                        g.dma_start(out=W2L[n % 2][:], in_=k.w2p[dblk]).then_inc(k.sem(f"dw{n%2}"), 16)
                        k.bump(None, f"dw{n%2}", 16, mark=f"f0w2_{tb}_{dblk}")

            def vecf(v):
                for tb in range(2):
                    for tt8 in range(8):
                        tt = tb * 8 + tt8
                        for fc in range(NFC):
                            n = (tb * 8 + tt8) * NFC + fc
                            k.wt(v, f"f0gnl_{n}")
                            v.tensor_scalar(YY[:], GNL[n % 4][:], k.aa8[:, tt:tt + 1], 0.5,
                                            OP.mult, OP.add)
                            ins = v.tensor_copy(k.PI[:, 0:512], YY[:])
                            k.bump(ins, "sV", mark=f"f0y_{n}")
                            v.tensor_copy(FF[:], k.PI[:, 0:512])
                            v.tensor_tensor(CMP[:], FF[:], YY[:], OP.is_gt)
                            v.tensor_tensor(FF[:], FF[:], CMP[:], OP.subtract)
                            if n >= 2:
                                k.wt(v, f"f0tiq_{n-2}")
                            ins = v.tensor_scalar_mul(IQS[n % 2][:], FF[:], k.i2v[:, tt:tt + 1])
                            k.bump(ins, "sV", mark=f"f0iq_{n}")

            def tenf(t):
                for tb in range(2):
                    for tt8 in range(8):
                        for fc in range(NFC):
                            n = (tb * 8 + tt8) * NFC + fc
                            k.wt(t, f"f0iq_{n}")
                            if n >= 2:
                                k.wt(t, f"f0iqe_{n-2}")
                            for b in range(4):
                                ins = t.transpose(tph[n % 2][:, b * 128:(b + 1) * 128],
                                                  IQS[n % 2][:, b * 128:(b + 1) * 128], k.ident[:])
                            k.bump(ins, "sT", mark=f"f0tiq_{n}")
                    for dblk in range(NH):
                        nn = tb * NH + dblk
                        k.wt(t, f"f0w2_{tb}_{dblk}")
                        k.wt(t, f"f0iqe_{(tb*8+7)*NFC + NFC - 1}")
                        if nn >= 2:
                            pn = nn - 2
                            k.wt(t, f"f0sil_{pn//NH}_{pn%NH}")
                        half = nn % 2
                        for ft in range(NFT):
                            for tc in range(2):
                                ins = t.matmul(DN[:, half * 1024 + tc * 512: half * 1024 + tc * 512 + 512],
                                               W2L[nn % 2][:, ft * 128:(ft + 1) * 128],
                                               iq2[ft][:, tc * 512:(tc + 1) * 512],
                                               start=(ft == 0), stop=(ft == NFT - 1))
                        k.bump(ins, "sT", mark=f"f0dn_{tb}_{dblk}")
                        k.wt(t, f"f0sil_{tb}_{dblk}")
                        if nn >= 1:
                            pn = nn - 1
                            k.wt(t, f"f0gte_{pn//NH}_{pn%NH}")
                        for b in range(8):
                            ins = t.transpose(TP2[:, (b % 8) * 128:(b % 8) * 128 + 128],
                                              GF[:, b * 128:(b + 1) * 128], k.ident[:])
                        k.bump(ins, "sT", mark=f"f0tg_{tb}_{dblk}")

            def scaf(sc):
                for tb in range(2):
                    for tt8 in range(8):
                        tt = tb * 8 + tt8
                        for fc in range(NFC):
                            n = (tb * 8 + tt8) * NFC + fc
                            k.wt(sc, f"f0tiq_{n}")
                            if tb == 1:
                                k.wt(sc, "f0dn_0_15")
                            if fc < 8:
                                base = k.Bp[:, fc * 4096:(fc + 1) * 4096]
                            else:
                                base = k.A[:, 20480 + (fc - 8) * 4096:20480 + (fc - 7) * 4096]
                            dst = base.rearrange("p (b t) -> p b t", b=4)[:, :, tt8 * 128:(tt8 + 1) * 128]
                            src = tph[n % 2][:, 0:512].rearrange("p (b t) -> p b t", b=4)
                            ins = sc.copy(dst, src)
                            k.bump(ins, "sS", mark=f"f0iqe_{n}")
                    for dblk in range(NH):
                        nn = tb * NH + dblk
                        k.wt(sc, f"f0dn_{tb}_{dblk}")
                        half = nn % 2
                        ins = sc.activation(GF[:], DN[:, half * 1024:half * 1024 + 1024], AF.Silu)
                        k.bump(ins, "sS", mark=f"f0sil_{tb}_{dblk}")
                        k.wt(sc, f"f0tg_{tb}_{dblk}")
                        if nn >= 1:
                            pn = nn - 1
                            k.wt(sc, f"f0gsfree_{pn//NH}_{pn%NH}")
                        ins = sc.copy(GS[:], TP2[:])
                        k.bump(ins, "sS", mark=f"f0gte_{tb}_{dblk}")
                        k.marks[f"f0gse_{tb}_{dblk}"] = k.marks[f"f0gte_{tb}_{dblk}"]

            self.phase("ffn0B", {"tensor": tenf, "vector": vecf, "scalar": scaf,
                                 "sync": syncf, "gpsimd": gpsf})

    def phase_ffn1gu(self):
        k = self
        WGU = [k.Cp[:, i * 4096:(i + 1) * 4096] for i in range(4)]
        WG = [WGU[i][:, 0:2048] for i in range(4)]
        WU = [WGU[i][:, 2048:4096] for i in range(4)]
        GG = k.P32[:, 0:2048]
        UU = k.P32[:, 2048:4096]
        HF = k.Cp[:, 16384:18432]

        with self.nc.psum_tensor("F1G", [128, 2048], F32) as GP, \
             self.nc.psum_tensor("F1U", [128, 2048], F32) as UP:
            def syncf(sy):
                sy.dma_start(out=k.Bp[:].rearrange("p (h t) -> p h t", h=NH),
                             in_=k.xn_h.rearrange("h p t -> p h t")).then_inc(k.sem("ld"), 16)
                k.cnt["ld"] = k.cnt.get("ld", 0) + 16
                k.wtv(sy, "ld", k.cnt.get("ld", 0))
                k.bump(sy.nop(), "sY", mark="f1_xn")
                for ft in range(NFT):
                    k.wt(sy, f"f1h_{ft}")
                    sy.dma_start(out=k.hf_h[ft], in_=HF[:]).then_inc(k.sem("st"), 16)
                    k.bump(None, "st", 16)
                    k.wtv(sy, "st", k.cnt.get("st", 0))
                    k.bump(sy.nop(), "sY", mark=f"f1hfree_{ft}")
                k.bump(sy.nop(), "sY", mark="f1gu_done")

            def gpsf(g):
                for ft in range(NFT):
                    if ft >= 4:
                        k.wt(g, f"f1mm_{ft-4}")
                    g.dma_start(out=WGU[ft % 4][:], in_=k.wgu1[ft]).then_inc(k.sem(f"dw{ft%4}"), 16)
                    k.bump(None, f"dw{ft%4}", 16, mark=f"f1w_{ft}")

            def tenf(t):
                k.wt(t, "f1_xn")
                for ft in range(NFT):
                    k.wt(t, f"f1w_{ft}")
                    if ft >= 1:
                        k.wt(t, f"f1ue_{ft-1}")
                        k.wt(t, f"f1gg_{ft-1}")
                    for ht in range(NH):
                        for tc in range(4):
                            t.matmul(GP[:, tc * 512:(tc + 1) * 512],
                                     WG[ft % 4][:, ht * 128:(ht + 1) * 128],
                                     k.Bp[:, ht * 2048 + tc * 512: ht * 2048 + tc * 512 + 512],
                                     start=(ht == 0), stop=(ht == NH - 1))
                    for ht in range(NH):
                        for tc in range(4):
                            ins = t.matmul(UP[:, tc * 512:(tc + 1) * 512],
                                           WU[ft % 4][:, ht * 128:(ht + 1) * 128],
                                           k.Bp[:, ht * 2048 + tc * 512: ht * 2048 + tc * 512 + 512],
                                           start=(ht == 0), stop=(ht == NH - 1))
                    k.bump(ins, "sT", mark=f"f1mm_{ft}")

            def scaf(sc):
                for ft in range(NFT):
                    k.wt(sc, f"f1mm_{ft}")
                    if ft >= 1:
                        k.wt(sc, f"f1h_{ft-1}")
                    ins = sc.activation(GG[:], GP[:], AF.Silu, scale=k.gs_s[:, ft:ft + 1])
                    k.bump(ins, "sS", mark=f"f1gg_{ft}")

            def vecf(v):
                for ft in range(NFT):
                    k.wt(v, f"f1mm_{ft}")
                    ins = v.tensor_scalar_mul(UU[:], UP[:], k.us_s[:, ft:ft + 1])
                    k.bump(ins, "sV", mark=f"f1ue_{ft}")
                    k.wt(v, f"f1gg_{ft}")
                    if ft >= 1:
                        k.wt(v, f"f1hfree_{ft-1}")
                    ins = v.tensor_tensor(HF[:], GG[:], UU[:], OP.mult)
                    k.bump(ins, "sV", mark=f"f1h_{ft}")

            self.phase("ffn1gu", {"tensor": tenf, "vector": vecf, "scalar": scaf,
                                  "sync": syncf, "gpsimd": gpsf})

    def phase_ffn1dn(self):
        k = self
        hfm = [k.Bp[:, ft * 1024:(ft + 1) * 1024] for ft in range(32)] + \
              [k.A[:, 20480 + (ft - 32) * 1024: 20480 + (ft - 31) * 1024] for ft in range(32, NFT)]
        WD = [k.A[:, 0:5632], k.A[:, 5632:11264]]
        GF = k.Cp[:, 0:1024]
        GS = k.Cp[:, 1024:2048]

        with self.nc.psum_tensor("F1D", [128, 2048], F32) as DN, \
             self.nc.psum_tensor("F1T", [128, 1024], BF16) as TP:
            def syncf(sy):
                for tb in range(2):
                    if tb == 1:
                        k.wt(sy, "f1dn_0_15")
                    for ft in range(NFT):
                        n = tb * NFT + ft
                        sy.dma_start(out=hfm[ft][:], in_=k.hf_h[ft][:, tb * 1024:(tb + 1) * 1024]
                                     ).then_inc(k.sem(f"dx{n%2}"), 16)
                        k.bump(None, f"dx{n%2}", 16)
                    k.wtv(sy, "dx0", k.cnt.get("dx0", 0))
                    k.wtv(sy, "dx1", k.cnt.get("dx1", 0))
                    k.bump(sy.nop(), "sY", mark=f"f1hl_all_{tb}")
                    for dblk in range(NH):
                        k.wt(sy, f"f1gte_{tb}_{dblk}")
                        for b in range(8):
                            tsub = tb * 8 + b
                            sy.dma_start(out=k.g_hb[tsub][:, dblk * 128:(dblk + 1) * 128],
                                         in_=GS[:, b * 128:(b + 1) * 128]).then_inc(k.sem("st"), 16)
                            k.bump(None, "st", 16)
                        k.wtv(sy, "st", k.cnt.get("st", 0))
                        k.bump(sy.nop(), "sY", mark=f"f1gsfree_{tb}_{dblk}")
                k.wtv(sy, "st", k.cnt.get("st", 0))
                k.bump(sy.nop(), "sY", mark="f1dn_done")

            def gpsf(g):
                for tb in range(2):
                    for dblk in range(NH):
                        n = tb * NH + dblk
                        if n >= 2:
                            pn = n - 2
                            k.wt(g, f"f1dn_{pn//NH}_{pn%NH}")
                        g.dma_start(out=WD[n % 2][:], in_=k.wd1[dblk]).then_inc(k.sem(f"dw{n%2}"), 16)
                        k.bump(None, f"dw{n%2}", 16, mark=f"f1wd_{tb}_{dblk}")

            def tenf(t):
                for tb in range(2):
                    k.wt(t, f"f1hl_all_{tb}")
                    for dblk in range(NH):
                        nn = tb * NH + dblk
                        k.wt(t, f"f1wd_{tb}_{dblk}")
                        if nn >= 2:
                            pn = nn - 2
                            k.wt(t, f"f1sil_{pn//NH}_{pn%NH}")
                        half = nn % 2
                        for ft in range(NFT):
                            for tc in range(2):
                                ins = t.matmul(DN[:, half * 1024 + tc * 512: half * 1024 + tc * 512 + 512],
                                               WD[nn % 2][:, ft * 128:(ft + 1) * 128],
                                               hfm[ft][:, tc * 512:(tc + 1) * 512],
                                               start=(ft == 0), stop=(ft == NFT - 1))
                        k.bump(ins, "sT", mark=f"f1dn_{tb}_{dblk}")
                        k.wt(t, f"f1sil_{tb}_{dblk}")
                        if nn >= 1:
                            pn = nn - 1
                            k.wt(t, f"f1gte_{pn//NH}_{pn%NH}")
                        for b in range(8):
                            ins = t.transpose(TP[:, b * 128:(b + 1) * 128],
                                              GF[:, b * 128:(b + 1) * 128], k.ident[:])
                        k.bump(ins, "sT", mark=f"f1tg_{tb}_{dblk}")

            def scaf(sc):
                for tb in range(2):
                    for dblk in range(NH):
                        nn = tb * NH + dblk
                        k.wt(sc, f"f1dn_{tb}_{dblk}")
                        half = nn % 2
                        ins = sc.activation(GF[:], DN[:, half * 1024:half * 1024 + 1024],
                                            AF.Silu, scale=k.ds_s[:, dblk:dblk + 1])
                        k.bump(ins, "sS", mark=f"f1sil_{tb}_{dblk}")
                        k.wt(sc, f"f1tg_{tb}_{dblk}")
                        if nn >= 1:
                            pn = nn - 1
                            k.wt(sc, f"f1gsfree_{pn//NH}_{pn%NH}")
                        ins = sc.copy(GS[:], TP[:])
                        k.bump(ins, "sS", mark=f"f1gte_{tb}_{dblk}")

            self.phase("ffn1dn", {"tensor": tenf, "scalar": scaf, "sync": syncf, "gpsimd": gpsf})

    def phase_final(self):
        k = self
        XF = [k.P32[:, 0:2048], k.P32[:, 2048:4096]]
        SGF = k.P32[:, 4096:6144]
        SGS = [k.Cp[:, 0:2048], k.Cp[:, 2048:4096]]
        GGS = [k.Cp[:, 4096:6144], k.Cp[:, 6144:8192]]

        def syncf(sy):
            for i in range(NT):
                sy.dma_start(out=XF[i % 2][:], in_=k.x1_h[i]).then_inc(k.sem(f"dx{i%2}"), 16)
                sy.dma_start(out=SGS[i % 2][:], in_=k.s_hb[i]).then_inc(k.sem(f"dx{i%2}"), 16)
                sy.dma_start(out=GGS[i % 2][:], in_=k.g_hb[i]).then_inc(k.sem(f"dx{i%2}"), 16)
                k.bump(None, f"dx{i%2}", 48, mark=f"fin_ld_{i}")
                k.wt(sy, f"fin_x2_{i}")
                sy.dma_start(out=k.out[i * 128:(i + 1) * 128, :], in_=XF[i % 2][:]
                             ).then_inc(k.sem("out"), 16)
                k.cnt["out"] = k.cnt.get("out", 0) + 16
                k.wtv(sy, "out", k.cnt.get("out", 0))
            k.wtv(sy, "out", k.cnt.get("out", 0))

        def vecf(v):
            for i in range(NT):
                k.wt(v, f"fin_ld_{i}")
                v.tensor_tensor(SGF[:], SGS[i % 2][:], GGS[i % 2][:], OP.mult)
                ins = v.tensor_tensor(XF[i % 2][:], XF[i % 2][:], SGF[:], OP.add)
                k.bump(ins, "sV", mark=f"fin_x2_{i}")

        self.phase("final", {"vector": vecf, "sync": syncf})

    def phase_flush(self):
        k = self

        def syncf(sy):
            if k.cnt.get("st", 0):
                k.wtv(sy, "st", k.cnt.get("st", 0))

        self.phase("flush", {"sync": syncf})


def _prep_inputs(ins):
    f32 = np.float32
    wqkvT = np.ascontiguousarray(ins["L0_attn_w_qkv"].astype(f32).T)
    woT0 = np.ascontiguousarray(ins["L0_attn_w_o"].astype(f32).T)
    w13T = np.ascontiguousarray(ins["L0_ffn_w13"].astype(f32).T)
    w2T = np.ascontiguousarray(ins["L0_ffn_w2"].astype(f32).T)
    qwT = np.ascontiguousarray(ins["L1_attn_q_w"].astype(f32).T)
    kwT = np.ascontiguousarray(ins["L1_attn_k_w"].astype(f32).T)
    vwT = np.ascontiguousarray(ins["L1_attn_v_w"].astype(f32).T)
    owT = np.ascontiguousarray(ins["L1_attn_o_w"].astype(f32).T)
    gwT = np.ascontiguousarray(ins["L1_ffn_gate_w"].astype(f32).T)
    uwT = np.ascontiguousarray(ins["L1_ffn_up_w"].astype(f32).T)
    dwT = np.ascontiguousarray(ins["L1_ffn_down_w"].astype(f32).T)

    d = {}
    d["wqk0"] = pack_lhsT(wqkvT[:, 0:4096])
    d["wv0"] = pack_lhsT(wqkvT[:, 4096:6144])
    d["wo0"] = pack_rhs(woT0)
    d["w13"] = pack_rhs(w13T)
    d["w2p"] = pack_lhsT(w2T)
    d["nwbi"] = np.ascontiguousarray(
        np.broadcast_to(ins["L0_ffn_norm_w"].astype(f32)[None, :], (128, F0)))
    d["wq1"] = pack_lhsT(qwT)
    d["wk1"] = pack_lhsT(kwT)
    d["wv1"] = pack_lhsT(vwT)
    d["wo1"] = pack_rhs(owT)
    d["wgu1"] = np.ascontiguousarray(
        np.concatenate([pack_lhsT(gwT), pack_lhsT(uwT)], axis=2))
    d["wd1"] = pack_lhsT(dwT)
    d["gsp"] = np.ascontiguousarray(ins["L1_ffn_gate_s"].astype(f32).reshape(NFT, 128).T)
    d["usp"] = np.ascontiguousarray(ins["L1_ffn_up_s"].astype(f32).reshape(NFT, 128).T)
    d["dsp"] = np.ascontiguousarray(ins["L1_ffn_down_s"].astype(f32).reshape(NH, 128).T)
    d["iden"] = np.eye(128, dtype=BF)
    d["mskin"] = make_masks()
    cst = np.zeros((128, 8), np.float32)
    cst[:, 0] = 1e-5
    d["cst"] = cst
    return d


def kernel(**inputs):
    shared = _prep_inputs(inputs)
    x_all = np.ascontiguousarray(inputs["L0_attn_hidden_states"].astype(np.float32))
    kb = KB()
    nc = kb.build()
    in_maps = []
    for b in range(8):
        m = dict(shared)
        m["x0"] = x_all[b]
        in_maps.append(m)
    res = run_bass_kernel_spmd(nc, in_maps, core_ids=list(range(8)))
    return np.stack([r["outp"] for r in res.results], axis=0).astype(np.float32)
